# version 14
# baseline (speedup 1.0000x reference)
import numpy as np
import ml_dtypes
import concourse.bass as bass
import concourse.mybir as mybir
from concourse.bass_utils import run_bass_kernel_spmd

F32 = mybir.dt.float32
BF16 = mybir.dt.bfloat16
I32 = mybir.dt.int32
AF = mybir.ActivationFunctionType
ALU = mybir.AluOpType
AX = mybir.AxisListType

B, S, D = 2, 4096, 4096
NKV, NQ, HD = 8, 32, 128
NCORE = 8
TPC = 1024
DC = D // 128
QSCALE = float(HD) ** -0.5
EPS = 1e-5
LORA_SCALE = 2.0

_STREAMS = ("tensor", "scalar", "vector", "gpsimd", "sync")


class _Op:
    __slots__ = ("idx", "stream", "pos", "fn", "sem", "val", "waits", "inc",
                 "snap", "tag")


class Prog:

    def __init__(self):
        self.ops = []
        self.pos = {s: 0 for s in _STREAMS}
        self.res = {}
        self.clock = {s: {} for s in _STREAMS}
        self.dmacnt = {}

    def emit(self, stream, fn, reads=(), writes=(), sem=None, tag=""):
        op = _Op()
        op.idx = len(self.ops)
        op.stream = stream
        op.fn = fn
        op.sem = sem
        op.tag = tag
        op.inc = False
        op.pos = self.pos[stream]
        self.pos[stream] += 1
        deps = {}
        for k in reads:
            r = self.res.get(k)
            if r and r[0] is not None:
                deps[r[0].idx] = r[0]
        for k in writes:
            r = self.res.get(k)
            if r:
                if r[0] is not None:
                    deps[r[0].idx] = r[0]
                for rd in r[1]:
                    deps[rd.idx] = rd
        clk = self.clock[stream]
        waits = {}
        for d in deps.values():
            if d.stream == stream and d.sem is None and stream == "tensor":
                continue
            if d.sem is None:
                ck, onum = "s:" + d.stream, d.pos
            else:
                ck, onum = "d:" + d.sem, d.val
            if clk.get(ck, -1) >= onum:
                continue
            if d.sem is not None:
                onum = self.dmacnt[d.sem]
            d.inc = True
            prev = waits.get(ck)
            if prev is None or prev[1] < onum:
                waits[ck] = (d, onum)
            for k2, v2 in d.snap.items():
                if clk.get(k2, -1) < v2:
                    clk[k2] = v2
            clk[ck] = max(clk.get(ck, -1), onum)
        op.waits = list(waits.items())
        if sem is not None:
            self.dmacnt[sem] = self.dmacnt.get(sem, 0) + 16
            op.val = self.dmacnt[sem]
        snap = dict(clk)
        ck0 = ("s:" + stream) if sem is None else ("d:" + sem)
        o0 = op.pos if sem is None else op.val
        snap[ck0] = max(snap.get(ck0, -1), o0)
        op.snap = snap
        for k in reads:
            self.res.setdefault(k, [None, []])[1].append(op)
        for k in writes:
            e = self.res.setdefault(k, [None, []])
            e[0] = op
            e[1] = []
        self.ops.append(op)
        return op

    def recarve(self, old_keys, new_keys):
        pend = []
        for k in old_keys:
            r = self.res.pop(k, None)
            if r:
                if r[0] is not None:
                    pend.append(r[0])
                pend.extend(r[1])
        for k in new_keys:
            self.res[k] = [None, list(pend)]

    def lower(self, block, semobjs):
        cnt = {s: 0 for s in _STREAMS}
        val = {}
        for op in self.ops:
            if op.sem is None and op.inc:
                cnt[op.stream] += 1
                val[op.idx] = cnt[op.stream]
        streams = {s: [] for s in _STREAMS}
        for op in self.ops:
            streams[op.stream].append(op)

        def mk(ops):
            def body(e):
                for op in ops:
                    for ck, (d, onum) in op.waits:
                        if d.sem is None:
                            e.wait_ge(semobjs["s:" + d.stream], val[d.idx])
                        else:
                            e.wait_ge(semobjs["d:" + d.sem], onum)
                    last = op.fn(e)
                    if op.sem is not None:
                        last.then_inc(semobjs["d:" + op.sem], 16)
                    elif op.inc:
                        last.then_inc(semobjs["s:" + op.stream], 1)
            return body

        for s in _STREAMS:
            if streams[s]:
                getattr(block, s)(mk(streams[s]))


def build_nc(dbg_scratch=False, upto=4):
    nc = bass.Bass("TRN2", target_bir_lowering=False)
    p = Prog()

    def din(name, shape, dt=F32):
        return nc.dram_tensor(name, shape, dt, kind="ExternalInput").ap()

    xoT = din("xoT", [D, TPC])
    xo = din("xo", [TPC, D])
    wq_sl = din("wq_sl", [32, 128, 32, 128])
    wk_sl = din("wk_sl", [8, 128, 32, 128])
    wv_sl = din("wv_sl", [4, 128, 32, 256])
    wo_sl = din("wo_sl", [16, 128, 32, 256])
    bw_sl = din("bw_sl", [16, 128, 32, 256])
    bv = din("bv", [1, 1024])
    bb = din("bb", [1, D])
    rw_sl = din("rw_sl", [128, 32, 16])
    rb = din("rb", [1, 16])
    acat_sl = din("acat_sl", [128, 32, 128])
    bcat = din("bcat", [128, D])
    emap = din("emap", [15, 1], I32)
    iota8 = din("iota8", [15, 8])
    e16 = din("e16", [8, 128])
    idbf = din("idbf", [128, 128], BF16)

    y = nc.dram_tensor("y", [TPC, D], F32, kind="ExternalOutput").ap()

    skind = {"kind": "ExternalOutput"} if dbg_scratch else {}
    if dbg_scratch:
        ctx_dbg = nc.dram_tensor("ctx_dbg", [D, TPC], BF16,
                                 kind="ExternalOutput").ap()
    k_own = nc.dram_tensor("k_own", [1024, TPC], BF16).ap()
    v_own = nc.dram_tensor("v_own", [4, TPC, 256], BF16).ap()
    k_scr = nc.dram_tensor("k_scr", [4 * 1024, TPC], BF16, **skind).ap()
    v_scr = nc.dram_tensor("v_scr", [4, 4, TPC, 256], BF16, **skind).ap()
    q_scr = nc.dram_tensor("q_scr", [D, TPC], BF16, **skind).ap()
    wo_bf = nc.dram_tensor("wo_bf", [16, 128, 8192], BF16).ap()
    bw_bf = nc.dram_tensor("bw_bf", [16, 128, 8192], BF16).ap()

    from contextlib import ExitStack
    with ExitStack() as st:
        def sb(name, shape, dt):
            return st.enter_context(nc.sbuf_tensor(name, shape, dt))

        def psalloc(name, shape, dt=F32):
            return st.enter_context(nc.psum_tensor(name, shape, dt))

        wbig = sb("wbig", [128, 32768], BF16)
        tbuf = sb("tbuf", [128, 4096], BF16)
        cx = sb("cx", [128, 16384], BF16)
        scoped = {}
        scoped_seq = [0]

        def salloc(name, shape, dt):
            g = nc.sbuf_tensor("%s_s%d" % (name, scoped_seq[0]), shape, dt)
            scoped[name] = (scoped_seq[0], g)
            scoped_seq[0] += 1
            return g.__enter__()

        def sfree(*names):
            for n in sorted(names, key=lambda n2: -scoped[n2][0]):
                scoped.pop(n)[1].__exit__(None, None, None)

        def palloc(name, shape, dt=F32):
            g = nc.psum_tensor("%s_s%d" % (name, scoped_seq[0]), shape, dt)
            scoped[name] = (scoped_seq[0], g)
            scoped_seq[0] += 1
            return g.__enter__()
        rwb = sb("rwb", [128, 512], BF16)
        idb = sb("idb", [128, 128], BF16)
        hwb = sb("hwb", [128, 512], BF16)
        wbcb = sb("wbcb", [128, 512], BF16)
        bvb = sb("bvb", [1, 1024], BF16)
        rbb = sb("rbb", [1, 16], BF16)
        e16b = sb("e16b", [8, 128], BF16)
        mmapb = sb("mmapb", [15, 8], BF16)
        mmapf = sb("mmapf", [15, 8], F32)
        iot = sb("iot", [15, 8], F32)
        emi = sb("emi", [15, 1], I32)
        emf = sb("emf", [15, 1], F32)
        onec = sb("onec", [128, 1], BF16)
        oner = sb("oner", [1, 128], BF16)
        onerf = sb("onerf", [1, 128], F32)
        epsb = sb("epsb", [128, 1], F32)
        lg = sb("lg", [128, 16], F32)
        pr = sb("pr", [128, 16], F32)
        j15 = sb("j15", [128, 16], F32)
        cnts = sb("cnts", [128, 16], F32)
        mskb = sb("mskb", [128, 16], F32)
        pmb = sb("pmb", [128, 16], BF16)
        pmT = sb("pmT", [15, 512], BF16)
        wtb = sb("wtb", [8, 512], BF16)
        stats = sb("stats", [128, 4 * 48], F32)
        mv = sb("mv", [128, 4 * 2], F32)
        s1 = sb("s1", [128, 4], F32)
        s2 = sb("s2", [128, 4], F32)
        s3 = sb("s3", [128, 4], F32)

        psA = psalloc("psA", [128, 1024])
        psB = psalloc("psB", [128, 1024])

        sem_names = (["s:" + s for s in _STREAMS] +
                     ["d:" + n for n in
                      ("g_w", "g_x0", "g_x1", "g_wq0", "g_wq1", "g_c",
                       "s_c", "s_k", "s_v", "s_q", "s_ctx", "s_y",
                       "s_kg0", "s_kg1", "s_kg2", "s_vg0", "s_vg1", "s_vg2", "s_qh0", "s_qh1",
                       "s_cx", "s_res", "g_w0", "g_w1", "g_pc", "g_ps", "g_wv0", "g_wv1")])
        semobjs = {n: st.enter_context(nc.semaphore(n.replace(":", "_")))
                   for n in sem_names}
        block = st.enter_context(nc.Block())

        em = p.emit

        em("sync", lambda e: e.dma_start(idb[:], idbf), writes=["idb"], sem="s_c")
        em("sync", lambda e: e.dma_start(iot[:], iota8), writes=["iot"], sem="s_c")
        em("sync", lambda e: e.dma_start(emi[:], emap), writes=["emi"], sem="s_c")
        em("gpsimd", lambda e: e.dma_start(bvb[:], bv), writes=["bvb"], sem="g_c")
        em("gpsimd", lambda e: e.dma_start(rbb[:], rb), writes=["rbb"], sem="g_c")
        em("gpsimd", lambda e: e.dma_start(e16b[:], e16), writes=["e16b"], sem="g_c")
        em("gpsimd", lambda e: e.dma_start(
            rwb[:].rearrange("p (dc r) -> p dc r", dc=32), rw_sl),
           writes=["rwb"], sem="g_c")
        em("vector", lambda e: e.memset(onec[:], 1.0), writes=["onec"])
        em("vector", lambda e: e.memset(oner[:], 1.0), writes=["oner"])
        em("vector", lambda e: e.memset(onerf[:], 1.0), writes=["onerf"])
        em("vector", lambda e: e.memset(epsb[:], EPS), writes=["epsb"])
        em("vector", lambda e: e.tensor_copy(emf[:], emi[:]),
           reads=["emi"], writes=["emf"])
        em("vector", lambda e: e.tensor_scalar(
            mmapf[:], iot[:], emf[:], None, ALU.is_equal),
           reads=["iot", "emf"], writes=["mmapf"])
        em("vector", lambda e: e.tensor_copy(mmapb[:], mmapf[:]),
           reads=["mmapf"], writes=["mmapb"])

        wview = wbig[:].rearrange("p (dc f) -> p dc f", f=1024)

        psum_rot = [("psA0", psA[:, 0:512]), ("psA1", psA[:, 512:1024]),
                    ("psB0", psB[:, 0:512]), ("psB1", psB[:, 512:1024])]

        def rot(i):
            return psum_rot[i % 4]

        tq = [("tb0", tbuf[:, 0:1024]), ("tb1", tbuf[:, 1024:2048]),
              ("tb2", tbuf[:, 2048:3072]), ("tb3", tbuf[:, 3072:4096])]

        ei = [0]

        def mm_group(ps_key, ps_ap, lhs_list, rhs_list, reads):
            n = len(lhs_list)
            for i, (lh, rh) in enumerate(zip(lhs_list, rhs_list)):
                em("tensor",
                   lambda e, a=lh, b=rh, st0=(i == 0), sp=(i == n - 1), o=ps_ap:
                   e.matmul(o, a, b, start=st0, stop=sp),
                   reads=list(reads), writes=[ps_key])

        p.recarve(["wbig"], ["wbx0", "wbx1"])
        kg_t = (salloc("kg0", [128, 4096], BF16), salloc("kg1", [128, 4096], BF16))
        vg_t = (salloc("vg0", [128, 4096], BF16), salloc("vg1", [128, 4096], BF16))
        qh_t = (salloc("qh0", [128, 1024], BF16), salloc("qh1", [128, 1024], BF16))
        wq0 = salloc("wq0", [128, 4096], BF16)
        wq1 = salloc("wq1", [128, 4096], BF16)
        wvs0 = salloc("wvs0", [128, 8192], BF16)
        wvs1 = salloc("wvs1", [128, 8192], BF16)
        wqv = (wq0[:].rearrange("p (dc f) -> p dc f", dc=32),
               wq1[:].rearrange("p (dc f) -> p dc f", dc=32))
        xqv = wbig[:].rearrange("p (dc t) -> p dc t", dc=32)
        wvv2 = (wvs0[:].rearrange("p (dc f) -> p dc f", dc=32),
                wvs1[:].rearrange("p (dc f) -> p dc f", dc=32))
        xsrc = xoT.rearrange("(dc p) t -> p dc t", p=128)
        em("gpsimd", lambda e: e.dma_start(xqv[:, 0:16, :], xsrc[:, 0:16, :]),
           writes=["wbx0"], sem="g_w", tag="xqa")
        em("gpsimd", lambda e: e.dma_start(xqv[:, 16:32, :], xsrc[:, 16:32, :]),
           writes=["wbx1"], sem="g_w", tag="xqb")

        def load_wslot(wsl_in, fc, q="gpsimd"):
            par = fc % 2
            em(q, lambda e, s=wsl_in[fc], d=wqv[par]: e.dma_start(d, s),
               writes=["wq%d" % par], sem="g_wq%d" % par)

        load_wslot(wk_sl, 0)
        for kfc in range(8):
            if kfc + 1 < 8:
                load_wslot(wk_sl, kfc + 1)
            par = kfc % 2
            for osc in range(2):
                pk, pa = rot(ei[0]); ei[0] += 1
                lhs = [wqv[par][:, dc, :] for dc in range(DC)]
                rhs = [xqv[:, dc, osc * 512:(osc + 1) * 512] for dc in range(DC)]
                n_ = len(lhs)
                for i_, (lh_, rh_) in enumerate(zip(lhs, rhs)):
                    em("tensor",
                       lambda e, a=lh_, b=rh_, st0=(i_ == 0),
                       sp=(i_ == n_ - 1), o=pa:
                       e.matmul(o, a, b, start=st0, stop=sp),
                       reads=["wq%d" % par, "wbx%d" % (i_ // 16)], writes=[pk])
                tk, ta = tq[ei[0] % 4]
                em("vector", lambda e, o=ta[:, 0:512], i_=pa: e.tensor_copy(o, i_),
                   reads=[pk], writes=[tk])
                dst = k_own[kfc * 128:(kfc + 1) * 128, osc * 512:(osc + 1) * 512]
                em("sync", lambda e, d=dst, s=ta[:, 0:512]: e.dma_start(d, s),
                   reads=[tk], writes=["k_own"], sem="s_k")

        for vfc in range(4):
            par = vfc % 2
            em("gpsimd", lambda e, s=wv_sl[vfc], d=wvv2[par]: e.dma_start(d, s),
               writes=["wbv%d" % par], sem="g_wv%d" % par, tag="wv%d" % vfc)
            wvv = wvv2[par]
            for tb8 in range(8):
                pk, pa0 = rot(ei[0]); ei[0] += 1
                pa = pa0[:, 0:256]
                for dc in range(DC):
                    em("tensor", lambda e, o=pa,
                       a=xqv[:, dc, tb8 * 128:(tb8 + 1) * 128],
                       b=wvv[:, dc, :], st0=(dc == 0):
                       e.matmul(o, a, b, start=st0, stop=False),
                       reads=["wbv%d" % par, "wbx%d" % (dc // 16)], writes=[pk])
                em("tensor", lambda e, o=pa,
                   b=bvb[:, vfc * 256:(vfc + 1) * 256]:
                   e.matmul(o, oner[:], b, start=False, stop=True),
                   reads=["oner", "bvb"], writes=[pk])
                tk, ta = tq[ei[0] % 4]
                em("vector", lambda e, o=ta[:, 0:256], i_=pa: e.tensor_copy(o, i_),
                   reads=[pk], writes=[tk])
                dst = v_own[vfc, tb8 * 128:(tb8 + 1) * 128, :]
                em("sync", lambda e, d=dst, s=ta[:, 0:256]:
                   e.dma_start(d, s), reads=[tk], writes=["v_own"], sem="s_v")

        groups = [[0, 1, 2, 3], [4, 5, 6, 7]]
        em("gpsimd", lambda e: e.collective_compute(
            "AllGather", ALU.bypass, replica_groups=groups,
            ins=[k_own.opt()], outs=[k_scr.opt()]),
           reads=["k_own"], writes=["k_scr"], tag="ag_k")
        em("gpsimd", lambda e: e.collective_compute(
            "AllGather", ALU.bypass, replica_groups=groups,
            ins=[v_own.opt()], outs=[v_scr.opt()]),
           reads=["v_own"], writes=["v_scr"], tag="ag_v")

        sfree("wvs0", "wvs1")
        kg = tuple(t[:] for t in kg_t)
        vg = tuple(t[:].rearrange("p (kc h) -> p kc h", kc=32) for t in vg_t)
        vgflat = tuple(t[:] for t in vg_t)
        qh = (qh_t[0][:], qh_t[1][:])

        def load_kv(g, q="sync"):
            par = g % 2
            sk_ = k_scr.rearrange("(r kf) t -> r kf t", r=4)[
                :, g * 128:(g + 1) * 128, :].rearrange("r p t -> p r t")
            em(q, lambda e, d=kg[par].rearrange("p (r t) -> p r t", r=4),
               s=sk_: e.dma_start(d, s), reads=["k_scr"],
               writes=["kg%d" % par], sem="s_kg%d" % par)

            vsrc = v_scr.rearrange("r v (kcl p) h -> r v p kcl h", p=128)
            for rr in range(4):
                em(q, lambda e,
                   d=vgflat[par][:, rr * 1024:(rr + 1) * 1024].rearrange(
                       "p (kcl h) -> p kcl h", kcl=8),
                   s=vsrc[rr, g // 2, :, :, (g % 2) * 128:(g % 2 + 1) * 128]:
                   e.dma_start(d, s),
                   reads=["v_scr"], writes=["vg%d" % par], sem="s_vg%d" % par)

        def load_qh(h):
            par = h % 2
            em("scalar", lambda e, d=qh[par], s=q_scr[h * 128:(h + 1) * 128, :]:
               e.dma_start(d, s), reads=["q_scr"],
               writes=["qh%d" % par], sem="s_qh%d" % par)

        load_kv(0, q="scalar")
        load_kv(1, q="scalar")

        load_wslot(wq_sl, 0)
        for qfc in range(32):
            if qfc + 1 < 32:
                load_wslot(wq_sl, qfc + 1)
            par = qfc % 2
            for osc in range(2):
                pk, pa = rot(ei[0]); ei[0] += 1
                lhs = [wqv[par][:, dc, :] for dc in range(DC)]
                rhs = [xqv[:, dc, osc * 512:(osc + 1) * 512] for dc in range(DC)]
                n_ = len(lhs)
                for i_, (lh_, rh_) in enumerate(zip(lhs, rhs)):
                    em("tensor",
                       lambda e, a=lh_, b=rh_, st0=(i_ == 0),
                       sp=(i_ == n_ - 1), o=pa:
                       e.matmul(o, a, b, start=st0, stop=sp),
                       reads=["wq%d" % par, "wbx%d" % (i_ // 16)], writes=[pk])
                tk, ta = tq[ei[0] % 4]
                em("vector", lambda e, o=ta[:, 0:512], i_=pa: e.tensor_copy(o, i_),
                   reads=[pk], writes=[tk])
                dst = q_scr[qfc * 128:(qfc + 1) * 128, osc * 512:(osc + 1) * 512]
                em("sync", lambda e, d=dst, s=ta[:, 0:512]: e.dma_start(d, s),
                   reads=[tk], writes=["q_scr"], sem="s_q")

        def fin():
            em("sync", lambda e: e.nop(),
               reads=["y_hbm", "k_scr", "v_scr", "q_scr"])

        if upto < 2:
            fin()
            p.lower(block, semobjs)
            sfree(*list(scoped.keys()))
            return nc

        p.recarve(["wq0", "wq1", "wbv0", "wbv1"],
                  ["pb0", "pb1", "pb2", "ilbc", "l_sb"])
        wsl = (cx[:, 0:8192].rearrange("p (dc f) -> p dc f", dc=32),
               cx[:, 8192:16384].rearrange("p (dc f) -> p dc f", dc=32))

        def load_w(obc, wT):
            par = obc % 2
            em("gpsimd", lambda e, s=wT[obc], d=cx[:, par * 8192:(par + 1) * 8192]:
               e.dma_start(d, s),
               reads=["wo_bf"], writes=["ws%d" % par], sem="g_w%d" % par)
        p.recarve(["wbx0", "wbx1"], ["w%d_%d" % (i_, j_) for i_ in range(32)
                             for j_ in range(2)])
        p.recarve(["psA0", "psA1"], ["sc0", "sc1"])
        p.recarve(["psB0", "psB1"], ["invl", "lps"])
        sfree("wq0", "wq1")
        pb_t = (salloc("pb0", [128, 1024], BF16), salloc("pb1", [128, 1024], BF16),
                salloc("pb2", [128, 1024], BF16))
        ilbc = salloc("ilbc", [128, 1024], F32)[:]
        l_sb = salloc("l_sb", [1, 1024], F32)[:]
        il_sb = l_sb
        psC0 = palloc("psC0", [128, 1024])
        psC1 = palloc("psC1", [128, 1024])
        pb = (pb_t[0][:], pb_t[1][:], pb_t[2][:])
        scv = (psA[:, 0:512], psA[:, 512:1024])
        invlps = psB[:, 0:512]
        lps = psB[:, 512:1024]
        ctxps = (psC0, psC1)
        ctxv = wview

        def emit_pv_l(gp, hp, t):
            kc, half = t // 2, t % 2
            pk = "pb%d" % (kc % 3)
            pbh = pb[kc % 3][:, half * 512:(half + 1) * 512]
            cp = ctxps[hp]
            em("tensor", lambda e, o=cp[:, half * 512:(half + 1) * 512],
               a=vg[gp][:, kc, :], b=pbh,
               st0=(t < 2), sp=(t >= 62):
               e.matmul(o, a, b, start=st0, stop=sp),
               reads=["vg%d" % gp, pk], writes=["psC%d" % hp])
            lrow = lps[0:1, :] if half == 0 else lps[32:33, :]
            em("tensor", lambda e, o=lrow, b=pbh,
               st0=(t < 2), sp=(t >= 62):
               e.matmul(o, onec[:], b, start=st0, stop=sp,
                        skip_group_check=True),
               reads=[pk, "onec"], writes=["lps"])

        def make_finalizer(h):
            hp2 = h % 2

            def fin_head_a():
                em("vector", lambda e: e.tensor_copy(l_sb[:, 0:512],
                                                     lps[0:1, :]),
                   reads=["lps"], writes=["l_sb"])
                em("vector", lambda e: e.tensor_copy(l_sb[:, 512:1024],
                                                     lps[32:33, :]),
                   reads=["lps"], writes=["l_sb"])
                em("scalar", lambda e: e.activation(l_sb, l_sb, AF.Ln),
                   reads=["l_sb"], writes=["l_sb"])
                em("scalar", lambda e: e.activation(l_sb, l_sb, AF.Exp,
                                                    scale=-1.0),
                   reads=["l_sb"], writes=["l_sb"])

            def fin_head_b():
                for half in range(2):
                    em("tensor", lambda e,
                       b=il_sb[:, half * 512:(half + 1) * 512]:
                       e.matmul(invlps, onerf[:], b, start=True, stop=True),
                       reads=["l_sb", "onerf"], writes=["invl"])
                    em("vector", lambda e,
                       o=ilbc[:, half * 512:(half + 1) * 512]:
                       e.tensor_copy(o, invlps), reads=["invl"],
                       writes=["ilbc"])
                em("vector", lambda e, o=ctxv[:, h, :], a=ctxps[hp2]:
                   e.tensor_tensor(o, a[:, :], ilbc, ALU.mult),
                   reads=["psC%d" % hp2, "ilbc"],
                   writes=["w%d_0" % h, "w%d_1" % h])
            return fin_head_a, fin_head_b

        pc_jobs = []
        tbv = tbuf[:].rearrange("p (hf dc f) -> p hf dc f", hf=2, dc=8)
        for wsrc, wdst in ((wo_sl, wo_bf), (bw_sl, bw_bf)):
            for j in range(16):
                for q_ in range(4):
                    s_ = wsrc[j, :, q_ * 8:(q_ + 1) * 8, :]
                    d_ = wdst[j][:, q_ * 2048:(q_ + 1) * 2048]
                    t_ = tbv[:, q_ % 2]
                    tk_ = "tbh%d" % (q_ % 2)
                    pc_jobs.append((s_, d_, t_, tk_))
        p.recarve(["tb0", "tb1", "tb2", "tb3"], ["tbh0", "tbh1"])
        pc_i = [0]

        def pump_precast(njobs):
            for _ in range(njobs):
                if pc_i[0] >= len(pc_jobs):
                    return
                s_, d_, t_, tk_ = pc_jobs[pc_i[0]]
                pc_i[0] += 1
                em("gpsimd", lambda e, a=t_, b=s_: e.dma_start(a, b),
                   writes=[tk_], sem="g_pc")
                em("gpsimd", lambda e, a=d_, b=t_: e.dma_start(a, b),
                   reads=[tk_], writes=["wo_bf"], sem="g_ps")

        load_qh(0)
        pend_b = None
        for g in range(NKV):
            gp = g % 2
            if g + 1 < NKV and g >= 1:
                load_kv(g + 1)
            for q4 in range(4):
                h = 4 * g + q4
                hp = h % 2
                if h + 1 < NQ:
                    load_qh(h + 1)
                pump_precast(4)
                kkey, qkey = "kg%d" % gp, "qh%d" % hp
                for t in range(64):
                    kc, half = t // 2, t % 2
                    em("tensor", lambda e, o=scv[t % 2],
                       a=kg[gp][:, kc * 128:(kc + 1) * 128],
                       b=qh[hp][:, half * 512:(half + 1) * 512]:
                       e.matmul(o, a, b, start=True, stop=True),
                       reads=[kkey, qkey], writes=["sc%d" % (t % 2)])
                    if half == 1:
                        em("scalar", lambda e, o=pb[kc % 3], i_=psA[:, :]:
                           e.activation(o, i_, AF.Exp, scale=QSCALE),
                           reads=["sc0", "sc1"], writes=["pb%d" % (kc % 3)])
                    if t == 6 and pend_b is not None:
                        pend_b()
                        pend_b = None
                    if t >= 2:
                        emit_pv_l(gp, hp, t - 2)
                emit_pv_l(gp, hp, 62)
                emit_pv_l(gp, hp, 63)
                fa, fb = make_finalizer(h)
                fa()
                pend_b = fb
        pend_b()
        pump_precast(len(pc_jobs))
        if dbg_scratch:
            em("sync", lambda e: e.dma_start(
                ctx_dbg.rearrange("(h p) t -> p h t", p=128), ctxv),
               reads=["w%d_%d" % (i_, j_) for i_ in range(32) for j_ in range(2)],
               writes=["ctx_dbg"], sem="s_c")
        load_w(0, wo_bf)

        if upto < 3:
            fin()
            p.lower(block, semobjs)
            sfree(*list(scoped.keys()))
            return nc

        p.recarve(["kg0", "kg1", "vg0", "vg1", "qh0", "qh1",
                   "pb0", "pb1", "pb2", "ilbc", "l_sb"],
                  ["yb0", "yb1", "yb2", "yb3"])
        p.recarve(["tbh0", "tbh1"], ["tb0", "tb1", "tb2", "tb3"])
        p.recarve(["sc0", "sc1"], ["psA0", "psA1"])
        p.recarve(["invl", "lps"], ["psB0", "psB1"])
        sfree("kg0", "kg1", "vg0", "vg1", "qh0", "qh1",
              "pb0", "pb1", "pb2", "ilbc", "l_sb", "psC0", "psC1")
        ybig = salloc("ybig", [128, 16384], F32)
        acat = salloc("acat", [128, 4096], BF16)
        bcat_s = salloc("bcat_s", [128, 4096], BF16)
        bbb = salloc("bbb", [1, 4096], BF16)
        em("gpsimd", lambda e: e.dma_start(
            acat[:].rearrange("p (dc r) -> p dc r", dc=32), acat_sl),
           writes=["acat"], sem="g_c")
        em("gpsimd", lambda e: e.dma_start(bcat_s[:], bcat),
           writes=["bcat_s"], sem="g_c")
        em("gpsimd", lambda e: e.dma_start(bbb[:], bb), writes=["bbb"],
           sem="g_c")
        psDt = palloc("psDt", [64, 1024], BF16)
        psE = palloc("psE", [128, 512])
        ybv = ybig[:].rearrange("p (tb f) -> p tb f", tb=4)
        yblk = [ybv[:, i, :] for i in range(4)]
        wres = wview
        rwv = rwb[:].rearrange("p (dc r) -> p dc r", dc=32)
        acv = acat[:].rearrange("p (dc r) -> p dc r", dc=32)

        def wkey(i, tc):
            return "w%d_%d" % (i, tc)

        def ln_pass(tb):
            ybk = "yb%d" % tb
            yb = yblk[tb]
            stv = stats[:, 48 * tb:48 * (tb + 1)]
            mvv = mv[:, 2 * tb:2 * (tb + 1)]
            s1v, s2v, s3v = (s1[:, tb:tb + 1], s2[:, tb:tb + 1],
                             s3[:, tb:tb + 1])
            for i in range(8):
                em("vector", lambda e, o=stv[:, 6 * i:6 * i + 6],
                   a=yb[:, 512 * i:512 * (i + 1)]:
                   e.bn_stats(o, a), reads=[ybk], writes=["st%d_%d" % (tb, i)])
            em("vector", lambda e, o=mvv, a=stv: e.bn_aggr(o, a),
               reads=["st%d_%d" % (tb, i2) for i2 in range(8)],
               writes=["mv%d" % tb])
            em("scalar", lambda e, o=s1v, a=mvv[:, 1:2]:
               e.activation(o, a, AF.Ln, bias=epsb[:]),
               reads=["mv%d" % tb, "epsb"], writes=["s1%d" % tb])
            em("scalar", lambda e, o=s2v, a=s1v:
               e.activation(o, a, AF.Exp, scale=-0.5),
               reads=["s1%d" % tb], writes=["s2%d" % tb])
            em("vector", lambda e, o=s3v, a=mvv[:, 0:1], b=s2v:
               e.tensor_tensor(o, a, b, ALU.mult),
               reads=["mv%d" % tb, "s2%d" % tb], writes=["s3%d" % tb])
            em("vector", lambda e, o=yb, a=s2v, b=s3v: e.tensor_scalar(
                o, o, a, b, ALU.mult, op1=ALU.subtract),
               reads=[ybk, "s2%d" % tb, "s3%d" % tb], writes=[ybk])

        def mm_pass(tc, wbf, lora):
            for obc in range(16):
                if obc + 1 < 16:
                    load_w(obc + 1, wbf)
                par = obc % 2
                for tb in range(4):
                    pk, pa0 = rot(ei[0]); ei[0] += 1
                    pa = pa0[:, 0:256]
                    for dc in range(DC):
                        em("tensor", lambda e, o=pa,
                           a=wres[:, dc, tc * 512 + tb * 128:
                                  tc * 512 + (tb + 1) * 128],
                           b=wsl[par][:, dc, :], st0=(dc == 0),
                           sp=(dc == 31 and not lora):
                           e.matmul(o, a, b, start=st0, stop=sp),
                           reads=[wkey(dc, tc), "ws%d" % par], writes=[pk])
                    if lora:
                        em("tensor", lambda e, o=pa,
                           b=bbb[:, obc * 256:(obc + 1) * 256]:
                           e.matmul(o, oner[:], b, start=False, stop=False),
                           reads=["oner", "bbb"], writes=[pk])
                        em("tensor", lambda e, o=pa,
                           a=hwb[:, tb * 128:(tb + 1) * 128],
                           b=bcat_s[:, obc * 256:(obc + 1) * 256]:
                           e.matmul(o, a, b, start=False, stop=True),
                           reads=["hwb", "bcat_s"], writes=[pk])
                    em("vector", lambda e,
                       o=yblk[tb][:, obc * 256:(obc + 1) * 256],
                       i_=pa: e.tensor_tensor(o, o, i_, ALU.add),
                       reads=[pk, "yb%d" % tb], writes=["yb%d" % tb])

        for tc in range(2):
            for tb in range(4):
                trow = tc * 512 + tb * 128
                em("sync", lambda e, d=yblk[tb], s=xo[trow:trow + 128, :]:
                   e.dma_start(d, s), writes=["yb%d" % tb], sem="s_res")
            mm_pass(tc, wo_bf, lora=False)
            load_w(0, bw_bf)

            psCt = palloc("psCt", [128, 2048], BF16)
            rps = []
            for tb in range(4):
                ln_pass(tb)
                em("vector", lambda e, i_=yblk[tb]: e.tensor_copy(tbuf[:], i_),
                   reads=["yb%d" % tb], writes=["tb0", "tb1", "tb2", "tb3"])
                for dg in range(8):
                    for j in range(4):
                        dc = dg * 4 + j
                        em("tensor", lambda e,
                           o=psCt[:, j * 128:(j + 1) * 128],
                           i_=tbuf[:, dc * 128:(dc + 1) * 128]:
                           e.transpose(o, i_, idb[:]),
                           reads=["tb0", "tb1", "tb2", "tb3", "idb"],
                           writes=["psC"])
                    em("vector", lambda e,
                       o=wres[:, dg * 4:dg * 4 + 4,
                              tc * 512 + tb * 128:tc * 512 + (tb + 1) * 128],
                       i_=psCt[:, 0:512].rearrange("p (j t) -> p j t", j=4):
                       e.tensor_copy(o, i_), reads=["psC"],
                       writes=[wkey(dg * 4 + j2, tc) for j2 in range(4)])
                pk, pa = rot(ei[0]); ei[0] += 1
                rps.append((pk, pa))
                em("tensor", lambda e, o=pa[:, 0:16]:
                   e.matmul(o, oner[:], rbb[:], start=True, stop=False),
                   reads=["oner", "rbb"], writes=[pk])
                for dc in range(DC):
                    em("tensor", lambda e, o=pa[:, 0:16],
                       a=wres[:, dc, tc * 512 + tb * 128:
                              tc * 512 + (tb + 1) * 128],
                       b=rwv[:, dc, :], sp=(dc == 31):
                       e.matmul(o, a, b, start=False, stop=sp),
                       reads=[wkey(dc, tc), "rwb"], writes=[pk])
            sfree("psCt")

            for tb in range(4):
                pk, pa = rps[tb]
                lgv = lg[:, 0:15]
                prv = pr[:, 0:15]
                s1v, s2v, s3v = (s1[:, tb:tb + 1], s2[:, tb:tb + 1],
                                 s3[:, tb:tb + 1])
                em("vector", lambda e, a=pa[:, 0:16], o=lg[:]:
                   e.tensor_copy(o, a), reads=[pk], writes=["lg"])
                em("vector", lambda e, o=s1v: e.tensor_reduce(
                    o, lgv, AX.X, ALU.max, negate=True),
                   reads=["lg"], writes=["s1%d" % tb])
                em("scalar", lambda e, o=prv, b=s1v, ac=s2v: e.activation(
                    o, lgv, AF.Exp, bias=b, accum_out=ac),
                   reads=["lg", "s1%d" % tb], writes=["pr", "s2%d" % tb])
                em("vector", lambda e, o=s3v, a=s2v: e.reciprocal(o, a),
                   reads=["s2%d" % tb], writes=["s3%d" % tb])
                em("vector", lambda e, o=prv, a=s3v: e.tensor_scalar(
                    o, o, a, None, ALU.mult),
                   reads=["pr", "s3%d" % tb], writes=["pr"])
                for n in range(15):
                    em("vector", lambda e, nn_=n: e.tensor_scalar(
                        j15[:, 0:15], pr[:, 0:15], pr[:, nn_:nn_ + 1], None,
                        ALU.is_gt, op1=ALU.add,
                        accum_out=cnts[:, nn_:nn_ + 1]),
                       reads=["pr"], writes=["j15", "c%d" % n])
                em("vector", lambda e: e.tensor_scalar(
                    mskb[:, 0:15], cnts[:, 0:15], 8.0, None, ALU.is_lt),
                   reads=["c%d" % n2 for n2 in range(15)], writes=["mskb"])
                em("vector", lambda e: e.tensor_tensor(
                    pmb[:, 0:15], pr[:, 0:15], mskb[:, 0:15], ALU.mult),
                   reads=["pr", "mskb"], writes=["pmb"])
                em("tensor", lambda e: e.transpose(
                    psDt[0:15, 0:128], pmb[:, 0:15], idb[:]),
                   reads=["pmb", "idb"], writes=["psD"])
                em("vector", lambda e, o=pmT[:, tb * 128:(tb + 1) * 128]:
                   e.tensor_copy(o, psDt[0:15, 0:128]),
                   reads=["psD"], writes=["pmT"])
            em("tensor", lambda e: e.matmul(psE[0:8, 0:512], mmapb[:], pmT[:],
                                            start=True, stop=True),
               reads=["mmapb", "pmT"], writes=["psE"])
            em("vector", lambda e: e.tensor_copy(wtb[:], psE[0:8, 0:512]),
               reads=["psE"], writes=["wtb"])
            em("tensor", lambda e: e.matmul(psE[:], e16b[:], wtb[:],
                                            start=True, stop=True),
               reads=["e16b", "wtb"], writes=["psE"])
            em("vector", lambda e: e.tensor_copy(wbcb[:], psE[:]),
               reads=["psE"], writes=["wbcb"])
            psCh = palloc("psCh", [128, 512])
            for dc in range(DC):
                em("tensor", lambda e, a=acv[:, dc, :],
                   b=wres[:, dc, tc * 512:(tc + 1) * 512],
                   st0=(dc == 0), sp=(dc == 31):
                   e.matmul(psCh[:], a, b, start=st0, stop=sp),
                   reads=["acat", wkey(dc, tc)], writes=["psC"])
            em("vector", lambda e: e.tensor_tensor(hwb[:], psCh[:],
                                                   wbcb[:], ALU.mult),
               reads=["psC", "wbcb"], writes=["hwb"])
            sfree("psCh")
            mm_pass(tc, bw_bf, lora=True)
            if tc == 0:
                load_w(0, wo_bf)
            for tb in range(4):
                ln_pass(tb)
                trow = tc * 512 + tb * 128
                em("sync", lambda e, d=y[trow:trow + 128, :], s=yblk[tb]:
                   e.dma_start(d, s), reads=["yb%d" % tb], writes=["y_hbm"],
                   sem="s_y")

        fin()
        p.lower(block, semobjs)
        sfree(*list(scoped.keys()))
    return nc


def _host_inputs(inp):
    x = np.ascontiguousarray(np.asarray(inp["L0_attn_hidden_states"],
                                        dtype=np.float32))
    c = np.ascontiguousarray

    def slot_img(w, nf):
        O = w.shape[0]
        return c(w.reshape(O // nf, nf, 32, 128).transpose(0, 3, 2, 1))

    wq_sl = slot_img(np.asarray(inp["L0_attn_Wq"], np.float32), 128)
    wk_sl = slot_img(np.asarray(inp["L0_attn_Wk"], np.float32), 128)
    wv_sl = slot_img(np.asarray(inp["L0_attn_Wv"], np.float32), 256)
    wo_sl = slot_img(np.asarray(inp["L0_attn_Wo"], np.float32), 256)
    bw_sl = slot_img(np.asarray(inp["L0_ffn_base_W"], np.float32), 256)
    bv = np.asarray(inp["L0_attn_bv"], np.float32).reshape(1, 1024)
    bb = np.asarray(inp["L0_ffn_base_b"], np.float32).reshape(1, D)
    rw = np.asarray(inp["L0_ffn_router_W"], np.float32)
    rwp = np.zeros((16, D), np.float32)
    rwp[0:15] = rw
    rw_sl = slot_img(rwp, 16)[0]
    rb = np.zeros((1, 16), np.float32)
    rb[0, 0:15] = np.asarray(inp["L0_ffn_router_b"], np.float32)
    A = np.asarray(inp["L0_ffn_A"], np.float32)
    acat_sl = slot_img(A.reshape(128, D), 128)[0]
    Bm = np.asarray(inp["L0_ffn_B"], np.float32)
    bcat = c(Bm.transpose(0, 2, 1).reshape(128, D))
    emap = np.asarray(inp["L0_ffn_expert_map"], np.int32).reshape(15, 1)
    iota8 = np.broadcast_to(np.arange(8, dtype=np.float32), (15, 8)).copy()
    e16 = np.zeros((8, 128), np.float32)
    for e_ in range(8):
        e16[e_, e_ * 16:(e_ + 1) * 16] = LORA_SCALE
    idbf = np.eye(128).astype(ml_dtypes.bfloat16)

    shared = dict(wq_sl=wq_sl, wk_sl=wk_sl, wv_sl=wv_sl, wo_sl=wo_sl,
                  bw_sl=bw_sl, bv=bv, bb=bb, rw_sl=rw_sl, rb=rb,
                  acat_sl=acat_sl, bcat=bcat, emap=emap,
                  iota8=iota8, e16=e16, idbf=idbf)
    xT = [c(x[b_].T) for b_ in range(B)]
    in_maps = []
    for core in range(NCORE):
        b_ = core // 4
        q0 = (core % 4) * TPC
        m = dict(shared)
        m["xoT"] = c(xT[b_][:, q0:q0 + TPC])
        m["xo"] = c(x[b_, q0:q0 + TPC, :])
        in_maps.append(m)
    return in_maps


def kernel(**inputs):
    in_maps = _host_inputs(inputs)
    nc = build_nc()
    res = run_bass_kernel_spmd(nc, in_maps, core_ids=list(range(NCORE)))
    out = np.empty((B, S, D), np.float32)
    for core in range(NCORE):
        b_ = core // 4
        q0 = (core % 4) * TPC
        out[b_, q0:q0 + TPC, :] = np.asarray(res.results[core]["y"],
                                             dtype=np.float32)
    return out


# version 15
# speedup vs baseline: 1.0834x; 1.0834x over previous
import numpy as np
import ml_dtypes
import concourse.bass as bass
import concourse.mybir as mybir
from concourse.bass_utils import run_bass_kernel_spmd

F32 = mybir.dt.float32
BF16 = mybir.dt.bfloat16
I32 = mybir.dt.int32
AF = mybir.ActivationFunctionType
ALU = mybir.AluOpType
AX = mybir.AxisListType

B, S, D = 2, 4096, 4096
NKV, NQ, HD = 8, 32, 128
NCORE = 8
TPC = 1024
DC = D // 128
QSCALE = float(HD) ** -0.5
EPS = 1e-5
LORA_SCALE = 2.0

_STREAMS = ("tensor", "scalar", "vector", "gpsimd", "sync")


class _Op:
    __slots__ = ("idx", "stream", "pos", "fn", "sem", "val", "waits", "inc",
                 "snap", "tag")


class Prog:

    def __init__(self):
        self.ops = []
        self.pos = {s: 0 for s in _STREAMS}
        self.res = {}
        self.clock = {s: {} for s in _STREAMS}
        self.dmacnt = {}

    def emit(self, stream, fn, reads=(), writes=(), sem=None, tag=""):
        op = _Op()
        op.idx = len(self.ops)
        op.stream = stream
        op.fn = fn
        op.sem = sem
        op.tag = tag
        op.inc = False
        op.pos = self.pos[stream]
        self.pos[stream] += 1
        deps = {}
        for k in reads:
            r = self.res.get(k)
            if r and r[0] is not None:
                deps[r[0].idx] = r[0]
        for k in writes:
            r = self.res.get(k)
            if r:
                if r[0] is not None:
                    deps[r[0].idx] = r[0]
                for rd in r[1]:
                    deps[rd.idx] = rd
        clk = self.clock[stream]
        waits = {}
        for d in deps.values():
            if d.stream == stream and d.sem is None and stream == "tensor":
                continue
            if d.sem is None:
                ck, onum = "s:" + d.stream, d.pos
            else:
                ck, onum = "d:" + d.sem, d.val
            if clk.get(ck, -1) >= onum:
                continue
            if d.sem is not None:
                onum = self.dmacnt[d.sem]
            d.inc = True
            prev = waits.get(ck)
            if prev is None or prev[1] < onum:
                waits[ck] = (d, onum)
            for k2, v2 in d.snap.items():
                if clk.get(k2, -1) < v2:
                    clk[k2] = v2
            clk[ck] = max(clk.get(ck, -1), onum)
        op.waits = list(waits.items())
        if sem is not None:
            self.dmacnt[sem] = self.dmacnt.get(sem, 0) + 16
            op.val = self.dmacnt[sem]
        snap = dict(clk)
        ck0 = ("s:" + stream) if sem is None else ("d:" + sem)
        o0 = op.pos if sem is None else op.val
        snap[ck0] = max(snap.get(ck0, -1), o0)
        op.snap = snap
        for k in reads:
            self.res.setdefault(k, [None, []])[1].append(op)
        for k in writes:
            e = self.res.setdefault(k, [None, []])
            e[0] = op
            e[1] = []
        self.ops.append(op)
        return op

    def recarve(self, old_keys, new_keys):
        pend = []
        for k in old_keys:
            r = self.res.pop(k, None)
            if r:
                if r[0] is not None:
                    pend.append(r[0])
                pend.extend(r[1])
        for k in new_keys:
            self.res[k] = [None, list(pend)]

    def lower(self, block, semobjs):
        cnt = {s: 0 for s in _STREAMS}
        val = {}
        for op in self.ops:
            if op.sem is None and op.inc:
                cnt[op.stream] += 1
                val[op.idx] = cnt[op.stream]
        streams = {s: [] for s in _STREAMS}
        for op in self.ops:
            streams[op.stream].append(op)

        def mk(ops):
            def body(e):
                for op in ops:
                    for ck, (d, onum) in op.waits:
                        if d.sem is None:
                            e.wait_ge(semobjs["s:" + d.stream], val[d.idx])
                        else:
                            e.wait_ge(semobjs["d:" + d.sem], onum)
                    last = op.fn(e)
                    if op.sem is not None:
                        last.then_inc(semobjs["d:" + op.sem], 16)
                    elif op.inc:
                        last.then_inc(semobjs["s:" + op.stream], 1)
            return body

        for s in _STREAMS:
            if streams[s]:
                getattr(block, s)(mk(streams[s]))


def build_nc(dbg_scratch=False, upto=4):
    nc = bass.Bass("TRN2", target_bir_lowering=False)
    p = Prog()

    def din(name, shape, dt=F32):
        return nc.dram_tensor(name, shape, dt, kind="ExternalInput").ap()

    xoT = din("xoT", [D, TPC])
    xo = din("xo", [TPC, D])
    wq_sl = din("wq_sl", [32, 128, 32, 128])
    wk_sl = din("wk_sl", [8, 128, 32, 128])
    wv_sl = din("wv_sl", [4, 128, 32, 256])
    wo_sl = din("wo_sl", [16, 128, 32, 256])
    bw_sl = din("bw_sl", [16, 128, 32, 256])
    bv = din("bv", [1, 1024])
    bb = din("bb", [1, D])
    rw_sl = din("rw_sl", [128, 32, 16])
    rb = din("rb", [1, 16])
    acat_sl = din("acat_sl", [128, 32, 128])
    bcat = din("bcat", [128, D])
    emap = din("emap", [15, 1], I32)
    iota8 = din("iota8", [15, 8])
    e16 = din("e16", [8, 128])
    idbf = din("idbf", [128, 128], BF16)

    y = nc.dram_tensor("y", [TPC, D], F32, kind="ExternalOutput").ap()

    skind = {"kind": "ExternalOutput"} if dbg_scratch else {}
    if dbg_scratch:
        ctx_dbg = nc.dram_tensor("ctx_dbg", [D, TPC], BF16,
                                 kind="ExternalOutput").ap()
    k_own = nc.dram_tensor("k_own", [1024, TPC], BF16).ap()
    v_own = nc.dram_tensor("v_own", [4, TPC, 256], BF16).ap()
    k_scr = nc.dram_tensor("k_scr", [4 * 1024, TPC], BF16, **skind).ap()
    v_scr = nc.dram_tensor("v_scr", [4, 4, TPC, 256], BF16, **skind).ap()
    q_scr = nc.dram_tensor("q_scr", [D, TPC], BF16, **skind).ap()
    wo_bf = nc.dram_tensor("wo_bf", [16, 128, 8192], BF16).ap()
    bw_bf = nc.dram_tensor("bw_bf", [16, 128, 8192], BF16).ap()

    from contextlib import ExitStack
    with ExitStack() as st:
        def sb(name, shape, dt):
            return st.enter_context(nc.sbuf_tensor(name, shape, dt))

        def psalloc(name, shape, dt=F32):
            return st.enter_context(nc.psum_tensor(name, shape, dt))

        wbig = sb("wbig", [128, 32768], BF16)
        tbuf = sb("tbuf", [128, 4096], BF16)
        cx = sb("cx", [128, 16384], BF16)
        scoped = {}
        scoped_seq = [0]

        def salloc(name, shape, dt):
            g = nc.sbuf_tensor("%s_s%d" % (name, scoped_seq[0]), shape, dt)
            scoped[name] = (scoped_seq[0], g)
            scoped_seq[0] += 1
            return g.__enter__()

        def sfree(*names):
            for n in sorted(names, key=lambda n2: -scoped[n2][0]):
                scoped.pop(n)[1].__exit__(None, None, None)

        def palloc(name, shape, dt=F32):
            g = nc.psum_tensor("%s_s%d" % (name, scoped_seq[0]), shape, dt)
            scoped[name] = (scoped_seq[0], g)
            scoped_seq[0] += 1
            return g.__enter__()
        rwb = sb("rwb", [128, 512], BF16)
        idb = sb("idb", [128, 128], BF16)
        hwb = sb("hwb", [128, 512], BF16)
        wbcb = sb("wbcb", [128, 512], BF16)
        bvb = sb("bvb", [1, 1024], BF16)
        rbb = sb("rbb", [1, 16], BF16)
        e16b = sb("e16b", [8, 128], BF16)
        mmapb = sb("mmapb", [15, 8], BF16)
        mmapf = sb("mmapf", [15, 8], F32)
        iot = sb("iot", [15, 8], F32)
        emi = sb("emi", [15, 1], I32)
        emf = sb("emf", [15, 1], F32)
        onec = sb("onec", [128, 1], BF16)
        oner = sb("oner", [1, 128], BF16)
        onerf = sb("onerf", [1, 128], F32)
        epsb = sb("epsb", [128, 1], F32)
        lg = sb("lg", [128, 16], F32)
        pr = sb("pr", [128, 16], F32)
        j15 = sb("j15", [128, 16], F32)
        cnts = sb("cnts", [128, 16], F32)
        mskb = sb("mskb", [128, 16], F32)
        pmb = sb("pmb", [128, 16], BF16)
        pmT = sb("pmT", [15, 512], BF16)
        wtb = sb("wtb", [8, 512], BF16)
        stats = sb("stats", [128, 4 * 48], F32)
        mv = sb("mv", [128, 4 * 2], F32)
        s1 = sb("s1", [128, 4], F32)
        s2 = sb("s2", [128, 4], F32)
        s3 = sb("s3", [128, 4], F32)

        psA = psalloc("psA", [128, 1024])
        psB = psalloc("psB", [128, 1024])

        sem_names = (["s:" + s for s in _STREAMS] +
                     ["d:" + n for n in
                      ("g_w", "g_x0", "g_x1", "g_wq0", "g_wq1", "g_c",
                       "s_c", "s_k", "s_v", "s_q", "s_ctx", "s_y",
                       "s_kg0", "s_kg1", "s_kg2", "s_vg0", "s_vg1", "s_vg2", "s_qh0", "s_qh1",
                       "s_cx", "s_res", "g_w0", "g_w1", "g_pc", "g_ps", "g_wv0", "g_wv1")])
        semobjs = {n: st.enter_context(nc.semaphore(n.replace(":", "_")))
                   for n in sem_names}
        block = st.enter_context(nc.Block())

        em = p.emit

        em("sync", lambda e: e.dma_start(idb[:], idbf), writes=["idb"], sem="s_c")
        em("sync", lambda e: e.dma_start(iot[:], iota8), writes=["iot"], sem="s_c")
        em("sync", lambda e: e.dma_start(emi[:], emap), writes=["emi"], sem="s_c")
        em("gpsimd", lambda e: e.dma_start(bvb[:], bv), writes=["bvb"], sem="g_c")
        em("gpsimd", lambda e: e.dma_start(rbb[:], rb), writes=["rbb"], sem="g_c")
        em("gpsimd", lambda e: e.dma_start(e16b[:], e16), writes=["e16b"], sem="g_c")
        em("gpsimd", lambda e: e.dma_start(
            rwb[:].rearrange("p (dc r) -> p dc r", dc=32), rw_sl),
           writes=["rwb"], sem="g_c")
        em("vector", lambda e: e.memset(onec[:], 1.0), writes=["onec"])
        em("vector", lambda e: e.memset(oner[:], 1.0), writes=["oner"])
        em("vector", lambda e: e.memset(onerf[:], 1.0), writes=["onerf"])
        em("vector", lambda e: e.memset(epsb[:], EPS), writes=["epsb"])
        em("vector", lambda e: e.tensor_copy(emf[:], emi[:]),
           reads=["emi"], writes=["emf"])
        em("vector", lambda e: e.tensor_scalar(
            mmapf[:], iot[:], emf[:], None, ALU.is_equal),
           reads=["iot", "emf"], writes=["mmapf"])
        em("vector", lambda e: e.tensor_copy(mmapb[:], mmapf[:]),
           reads=["mmapf"], writes=["mmapb"])

        wview = wbig[:].rearrange("p (dc f) -> p dc f", f=1024)

        psum_rot = [("psA0", psA[:, 0:512]), ("psA1", psA[:, 512:1024]),
                    ("psB0", psB[:, 0:512]), ("psB1", psB[:, 512:1024])]

        def rot(i):
            return psum_rot[i % 4]

        tq = [("tb0", tbuf[:, 0:1024]), ("tb1", tbuf[:, 1024:2048]),
              ("tb2", tbuf[:, 2048:3072]), ("tb3", tbuf[:, 3072:4096])]

        ei = [0]

        def mm_group(ps_key, ps_ap, lhs_list, rhs_list, reads):
            n = len(lhs_list)
            for i, (lh, rh) in enumerate(zip(lhs_list, rhs_list)):
                em("tensor",
                   lambda e, a=lh, b=rh, st0=(i == 0), sp=(i == n - 1), o=ps_ap:
                   e.matmul(o, a, b, start=st0, stop=sp),
                   reads=list(reads), writes=[ps_key])

        p.recarve(["wbig"], ["wbx0", "wbx1"])
        kg_t = (salloc("kg0", [128, 4096], BF16), salloc("kg1", [128, 4096], BF16))
        vg_t = (salloc("vg0", [128, 4096], BF16), salloc("vg1", [128, 4096], BF16))
        qh_t = (salloc("qh0", [128, 1024], BF16), salloc("qh1", [128, 1024], BF16))
        wq0 = salloc("wq0", [128, 4096], BF16)
        wq1 = salloc("wq1", [128, 4096], BF16)
        wvs0 = salloc("wvs0", [128, 8192], BF16)
        wvs1 = salloc("wvs1", [128, 8192], BF16)
        wqv = (wq0[:].rearrange("p (dc f) -> p dc f", dc=32),
               wq1[:].rearrange("p (dc f) -> p dc f", dc=32))
        xqv = wbig[:].rearrange("p (dc t) -> p dc t", dc=32)
        wvv2 = (wvs0[:].rearrange("p (dc f) -> p dc f", dc=32),
                wvs1[:].rearrange("p (dc f) -> p dc f", dc=32))
        xsrc = xoT.rearrange("(dc p) t -> p dc t", p=128)
        em("gpsimd", lambda e: e.dma_start(xqv[:, 0:16, :], xsrc[:, 0:16, :]),
           writes=["wbx0"], sem="g_w", tag="xqa")
        em("gpsimd", lambda e: e.dma_start(xqv[:, 16:32, :], xsrc[:, 16:32, :]),
           writes=["wbx1"], sem="g_w", tag="xqb")

        def load_wslot(wsl_in, fc, q="gpsimd"):
            par = fc % 2
            em(q, lambda e, s=wsl_in[fc], d=wqv[par]: e.dma_start(d, s),
               writes=["wq%d" % par], sem="g_wq%d" % par)

        load_wslot(wk_sl, 0)
        for kfc in range(8):
            if kfc + 1 < 8:
                load_wslot(wk_sl, kfc + 1)
            par = kfc % 2
            for osc in range(2):
                pk, pa = rot(ei[0]); ei[0] += 1
                lhs = [wqv[par][:, dc, :] for dc in range(DC)]
                rhs = [xqv[:, dc, osc * 512:(osc + 1) * 512] for dc in range(DC)]
                n_ = len(lhs)
                for i_, (lh_, rh_) in enumerate(zip(lhs, rhs)):
                    em("tensor",
                       lambda e, a=lh_, b=rh_, st0=(i_ == 0),
                       sp=(i_ == n_ - 1), o=pa:
                       e.matmul(o, a, b, start=st0, stop=sp),
                       reads=["wq%d" % par, "wbx%d" % (i_ // 16)], writes=[pk])
                tk, ta = tq[ei[0] % 4]
                em("vector", lambda e, o=ta[:, 0:512], i_=pa: e.tensor_copy(o, i_),
                   reads=[pk], writes=[tk])
                dst = k_own[kfc * 128:(kfc + 1) * 128, osc * 512:(osc + 1) * 512]
                em("sync", lambda e, d=dst, s=ta[:, 0:512]: e.dma_start(d, s),
                   reads=[tk], writes=["k_own"], sem="s_k")

        for vfc in range(4):
            par = vfc % 2
            em("gpsimd", lambda e, s=wv_sl[vfc], d=wvv2[par]: e.dma_start(d, s),
               writes=["wbv%d" % par], sem="g_wv%d" % par, tag="wv%d" % vfc)
            wvv = wvv2[par]
            for tb8 in range(8):
                pk, pa0 = rot(ei[0]); ei[0] += 1
                pa = pa0[:, 0:256]
                for dc in range(DC):
                    em("tensor", lambda e, o=pa,
                       a=xqv[:, dc, tb8 * 128:(tb8 + 1) * 128],
                       b=wvv[:, dc, :], st0=(dc == 0):
                       e.matmul(o, a, b, start=st0, stop=False),
                       reads=["wbv%d" % par, "wbx%d" % (dc // 16)], writes=[pk])
                em("tensor", lambda e, o=pa,
                   b=bvb[:, vfc * 256:(vfc + 1) * 256]:
                   e.matmul(o, oner[:], b, start=False, stop=True),
                   reads=["oner", "bvb"], writes=[pk])
                tk, ta = tq[ei[0] % 4]
                em("vector", lambda e, o=ta[:, 0:256], i_=pa: e.tensor_copy(o, i_),
                   reads=[pk], writes=[tk])
                dst = v_own[vfc, tb8 * 128:(tb8 + 1) * 128, :]
                em("sync", lambda e, d=dst, s=ta[:, 0:256]:
                   e.dma_start(d, s), reads=[tk], writes=["v_own"], sem="s_v")

        groups = [[0, 1, 2, 3], [4, 5, 6, 7]]
        em("gpsimd", lambda e: e.collective_compute(
            "AllGather", ALU.bypass, replica_groups=groups,
            ins=[k_own.opt()], outs=[k_scr.opt()]),
           reads=["k_own"], writes=["k_scr"], tag="ag_k")
        em("gpsimd", lambda e: e.collective_compute(
            "AllGather", ALU.bypass, replica_groups=groups,
            ins=[v_own.opt()], outs=[v_scr.opt()]),
           reads=["v_own"], writes=["v_scr"], tag="ag_v")

        sfree("wvs0", "wvs1")
        kg = tuple(t[:] for t in kg_t)
        vg = tuple(t[:].rearrange("p (kc h) -> p kc h", kc=32) for t in vg_t)
        vgflat = tuple(t[:] for t in vg_t)
        qh = (qh_t[0][:], qh_t[1][:])

        def load_kv(g, q="sync"):
            par = g % 2
            sk_ = k_scr.rearrange("(r kf) t -> r kf t", r=4)[
                :, g * 128:(g + 1) * 128, :].rearrange("r p t -> p r t")
            em(q, lambda e, d=kg[par].rearrange("p (r t) -> p r t", r=4),
               s=sk_: e.dma_start(d, s), reads=["k_scr"],
               writes=["kg%d" % par], sem="s_kg%d" % par)

            vsrc = v_scr.rearrange("r v (kcl p) h -> r v p kcl h", p=128)
            for rr in range(4):
                em(q, lambda e,
                   d=vgflat[par][:, rr * 1024:(rr + 1) * 1024].rearrange(
                       "p (kcl h) -> p kcl h", kcl=8),
                   s=vsrc[rr, g // 2, :, :, (g % 2) * 128:(g % 2 + 1) * 128]:
                   e.dma_start(d, s),
                   reads=["v_scr"], writes=["vg%d" % par], sem="s_vg%d" % par)

        def load_qh(h):
            par = h % 2
            em("scalar", lambda e, d=qh[par], s=q_scr[h * 128:(h + 1) * 128, :]:
               e.dma_start(d, s), reads=["q_scr"],
               writes=["qh%d" % par], sem="s_qh%d" % par)

        load_kv(0, q="scalar")
        load_kv(1, q="scalar")

        load_wslot(wq_sl, 0)
        for qfc in range(32):
            if qfc + 1 < 32:
                load_wslot(wq_sl, qfc + 1)
            par = qfc % 2
            for osc in range(2):
                pk, pa = rot(ei[0]); ei[0] += 1
                lhs = [wqv[par][:, dc, :] for dc in range(DC)]
                rhs = [xqv[:, dc, osc * 512:(osc + 1) * 512] for dc in range(DC)]
                n_ = len(lhs)
                for i_, (lh_, rh_) in enumerate(zip(lhs, rhs)):
                    em("tensor",
                       lambda e, a=lh_, b=rh_, st0=(i_ == 0),
                       sp=(i_ == n_ - 1), o=pa:
                       e.matmul(o, a, b, start=st0, stop=sp),
                       reads=["wq%d" % par, "wbx%d" % (i_ // 16)], writes=[pk])
                tk, ta = tq[ei[0] % 4]
                em("vector", lambda e, o=ta[:, 0:512], i_=pa: e.tensor_copy(o, i_),
                   reads=[pk], writes=[tk])
                dst = q_scr[qfc * 128:(qfc + 1) * 128, osc * 512:(osc + 1) * 512]
                em("sync", lambda e, d=dst, s=ta[:, 0:512]: e.dma_start(d, s),
                   reads=[tk], writes=["q_scr"], sem="s_q")

        def fin():
            em("sync", lambda e: e.nop(),
               reads=["y_hbm", "k_scr", "v_scr", "q_scr"])

        if upto < 2:
            fin()
            p.lower(block, semobjs)
            sfree(*list(scoped.keys()))
            return nc

        p.recarve(["wq0", "wq1", "wbv0", "wbv1"],
                  ["pb0", "pb1", "pb2", "ilbc", "l_sb"])
        wsl = (cx[:, 0:8192].rearrange("p (dc f) -> p dc f", dc=32),
               cx[:, 8192:16384].rearrange("p (dc f) -> p dc f", dc=32))

        def load_w(obc, wT):
            par = obc % 2
            em("gpsimd", lambda e, s=wT[obc], d=cx[:, par * 8192:(par + 1) * 8192]:
               e.dma_start(d, s),
               reads=["wo_bf"], writes=["ws%d" % par], sem="g_w%d" % par)
        p.recarve(["wbx0", "wbx1"], ["w%d_%d" % (i_, j_) for i_ in range(32)
                             for j_ in range(2)])
        p.recarve(["psA0", "psA1"], ["sc0", "sc1"])
        p.recarve(["psB0", "psB1"], ["invl", "lps"])
        sfree("wq0", "wq1")
        pb_t = (salloc("pb0", [128, 512], BF16), salloc("pb1", [128, 512], BF16),
                salloc("pb2", [128, 512], BF16))
        ilbc = salloc("ilbc", [128, 1024], F32)[:]
        l_sb = salloc("l_sb", [1, 1024], F32)[:]
        il_sb = l_sb
        psC0 = palloc("psC0", [128, 1024])
        psC1 = palloc("psC1", [128, 1024])
        pb = (pb_t[0][:], pb_t[1][:], pb_t[2][:])
        scv = (psA[:, 0:512], psA[:, 512:1024])
        invlps = psB[:, 0:512]
        lps = psB[:, 512:1024]
        ctxps = (psC0, psC1)
        ctxv = wview

        def emit_pv_l(gp, hp, t):
            kc, half = t // 2, t % 2
            pk = "pb%d" % (t % 3)
            pbh = pb[t % 3]
            cp = ctxps[hp]
            em("tensor", lambda e, o=cp[:, half * 512:(half + 1) * 512],
               a=vg[gp][:, kc, :], b=pbh,
               st0=(t < 2), sp=(t >= 62):
               e.matmul(o, a, b, start=st0, stop=sp),
               reads=["vg%d" % gp, pk], writes=["psC%d" % hp])
            lrow = lps[0:1, :] if half == 0 else lps[32:33, :]
            em("tensor", lambda e, o=lrow, b=pbh,
               st0=(t < 2), sp=(t >= 62):
               e.matmul(o, onec[:], b, start=st0, stop=sp,
                        skip_group_check=True),
               reads=[pk, "onec"], writes=["lps"])

        def make_finalizer(h):
            hp2 = h % 2

            def fin_head_a():
                em("vector", lambda e: e.tensor_copy(l_sb[:, 0:512],
                                                     lps[0:1, :]),
                   reads=["lps"], writes=["l_sb"])
                em("vector", lambda e: e.tensor_copy(l_sb[:, 512:1024],
                                                     lps[32:33, :]),
                   reads=["lps"], writes=["l_sb"])
                em("scalar", lambda e: e.activation(l_sb, l_sb, AF.Ln),
                   reads=["l_sb"], writes=["l_sb"])
                em("scalar", lambda e: e.activation(l_sb, l_sb, AF.Exp,
                                                    scale=-1.0),
                   reads=["l_sb"], writes=["l_sb"])

            def fin_head_b():
                for half in range(2):
                    em("tensor", lambda e,
                       b=il_sb[:, half * 512:(half + 1) * 512]:
                       e.matmul(invlps, onerf[:], b, start=True, stop=True),
                       reads=["l_sb", "onerf"], writes=["invl"])
                    em("vector", lambda e,
                       o=ilbc[:, half * 512:(half + 1) * 512]:
                       e.tensor_copy(o, invlps), reads=["invl"],
                       writes=["ilbc"])
                em("vector", lambda e, o=ctxv[:, h, :], a=ctxps[hp2]:
                   e.tensor_tensor(o, a[:, :], ilbc, ALU.mult),
                   reads=["psC%d" % hp2, "ilbc"],
                   writes=["w%d_0" % h, "w%d_1" % h])
            return fin_head_a, fin_head_b

        pc_jobs = []
        tbv = tbuf[:].rearrange("p (hf dc f) -> p hf dc f", hf=2, dc=8)
        for wsrc, wdst in ((wo_sl, wo_bf), (bw_sl, bw_bf)):
            for j in range(16):
                for q_ in range(4):
                    s_ = wsrc[j, :, q_ * 8:(q_ + 1) * 8, :]
                    d_ = wdst[j][:, q_ * 2048:(q_ + 1) * 2048]
                    t_ = tbv[:, q_ % 2]
                    tk_ = "tbh%d" % (q_ % 2)
                    pc_jobs.append((s_, d_, t_, tk_))
        p.recarve(["tb0", "tb1", "tb2", "tb3"], ["tbh0", "tbh1"])
        pc_i = [0]

        def pump_precast(njobs):
            for _ in range(njobs):
                if pc_i[0] >= len(pc_jobs):
                    return
                s_, d_, t_, tk_ = pc_jobs[pc_i[0]]
                pc_i[0] += 1
                em("gpsimd", lambda e, a=t_, b=s_: e.dma_start(a, b),
                   writes=[tk_], sem="g_pc")
                em("gpsimd", lambda e, a=d_, b=t_: e.dma_start(a, b),
                   reads=[tk_], writes=["wo_bf"], sem="g_ps")

        load_qh(0)
        pend_b = None
        for g in range(NKV):
            gp = g % 2
            if g + 1 < NKV and g >= 1:
                load_kv(g + 1)
            for q4 in range(4):
                h = 4 * g + q4
                hp = h % 2
                if h + 1 < NQ:
                    load_qh(h + 1)
                pump_precast(4)
                kkey, qkey = "kg%d" % gp, "qh%d" % hp
                for t in range(64):
                    kc, half = t // 2, t % 2
                    em("tensor", lambda e, o=scv[t % 2],
                       a=kg[gp][:, kc * 128:(kc + 1) * 128],
                       b=qh[hp][:, half * 512:(half + 1) * 512]:
                       e.matmul(o, a, b, start=True, stop=True),
                       reads=[kkey, qkey], writes=["sc%d" % (t % 2)])
                    em("scalar", lambda e, o=pb[t % 3], i_=scv[t % 2]:
                       e.activation(o, i_, AF.Exp, scale=QSCALE),
                       reads=["sc%d" % (t % 2)], writes=["pb%d" % (t % 3)])
                    if t == 6 and pend_b is not None:
                        pend_b()
                        pend_b = None
                    if t >= 2:
                        emit_pv_l(gp, hp, t - 2)
                emit_pv_l(gp, hp, 62)
                emit_pv_l(gp, hp, 63)
                fa, fb = make_finalizer(h)
                fa()
                pend_b = fb
        pend_b()
        pump_precast(len(pc_jobs))
        if dbg_scratch:
            em("sync", lambda e: e.dma_start(
                ctx_dbg.rearrange("(h p) t -> p h t", p=128), ctxv),
               reads=["w%d_%d" % (i_, j_) for i_ in range(32) for j_ in range(2)],
               writes=["ctx_dbg"], sem="s_c")
        load_w(0, wo_bf)

        if upto < 3:
            fin()
            p.lower(block, semobjs)
            sfree(*list(scoped.keys()))
            return nc

        p.recarve(["kg0", "kg1", "vg0", "vg1", "qh0", "qh1",
                   "pb0", "pb1", "pb2", "ilbc", "l_sb"],
                  ["yb0", "yb1", "yb2", "yb3"])
        p.recarve(["tbh0", "tbh1"], ["tb0", "tb1", "tb2", "tb3"])
        p.recarve(["sc0", "sc1"], ["psA0", "psA1"])
        p.recarve(["invl", "lps"], ["psB0", "psB1"])
        sfree("kg0", "kg1", "vg0", "vg1", "qh0", "qh1",
              "pb0", "pb1", "pb2", "ilbc", "l_sb", "psC0", "psC1")
        ybig = salloc("ybig", [128, 16384], F32)
        acat = salloc("acat", [128, 4096], BF16)
        bcat_s = salloc("bcat_s", [128, 4096], BF16)
        bbb = salloc("bbb", [1, 4096], BF16)
        em("gpsimd", lambda e: e.dma_start(
            acat[:].rearrange("p (dc r) -> p dc r", dc=32), acat_sl),
           writes=["acat"], sem="g_c")
        em("gpsimd", lambda e: e.dma_start(bcat_s[:], bcat),
           writes=["bcat_s"], sem="g_c")
        em("gpsimd", lambda e: e.dma_start(bbb[:], bb), writes=["bbb"],
           sem="g_c")
        psDt = palloc("psDt", [64, 1024], BF16)
        psE = palloc("psE", [128, 512])
        ybv = ybig[:].rearrange("p (tb f) -> p tb f", tb=4)
        yblk = [ybv[:, i, :] for i in range(4)]
        wres = wview
        rwv = rwb[:].rearrange("p (dc r) -> p dc r", dc=32)
        acv = acat[:].rearrange("p (dc r) -> p dc r", dc=32)

        def wkey(i, tc):
            return "w%d_%d" % (i, tc)

        def ln_pass(tb):
            ybk = "yb%d" % tb
            yb = yblk[tb]
            stv = stats[:, 48 * tb:48 * (tb + 1)]
            mvv = mv[:, 2 * tb:2 * (tb + 1)]
            s1v, s2v, s3v = (s1[:, tb:tb + 1], s2[:, tb:tb + 1],
                             s3[:, tb:tb + 1])
            for i in range(8):
                em("vector", lambda e, o=stv[:, 6 * i:6 * i + 6],
                   a=yb[:, 512 * i:512 * (i + 1)]:
                   e.bn_stats(o, a), reads=[ybk], writes=["st%d_%d" % (tb, i)])
            em("vector", lambda e, o=mvv, a=stv: e.bn_aggr(o, a),
               reads=["st%d_%d" % (tb, i2) for i2 in range(8)],
               writes=["mv%d" % tb])
            em("scalar", lambda e, o=s1v, a=mvv[:, 1:2]:
               e.activation(o, a, AF.Ln, bias=epsb[:]),
               reads=["mv%d" % tb, "epsb"], writes=["s1%d" % tb])
            em("scalar", lambda e, o=s2v, a=s1v:
               e.activation(o, a, AF.Exp, scale=-0.5),
               reads=["s1%d" % tb], writes=["s2%d" % tb])
            em("vector", lambda e, o=s3v, a=mvv[:, 0:1], b=s2v:
               e.tensor_tensor(o, a, b, ALU.mult),
               reads=["mv%d" % tb, "s2%d" % tb], writes=["s3%d" % tb])
            em("vector", lambda e, o=yb, a=s2v, b=s3v: e.tensor_scalar(
                o, o, a, b, ALU.mult, op1=ALU.subtract),
               reads=[ybk, "s2%d" % tb, "s3%d" % tb], writes=[ybk])

        def mm_pass(tc, wbf, lora):
            for obc in range(16):
                if obc + 1 < 16:
                    load_w(obc + 1, wbf)
                par = obc % 2
                for tb in range(4):
                    pk, pa0 = rot(ei[0]); ei[0] += 1
                    pa = pa0[:, 0:256]
                    for dc in range(DC):
                        em("tensor", lambda e, o=pa,
                           a=wres[:, dc, tc * 512 + tb * 128:
                                  tc * 512 + (tb + 1) * 128],
                           b=wsl[par][:, dc, :], st0=(dc == 0),
                           sp=(dc == 31 and not lora):
                           e.matmul(o, a, b, start=st0, stop=sp),
                           reads=[wkey(dc, tc), "ws%d" % par], writes=[pk])
                    if lora:
                        em("tensor", lambda e, o=pa,
                           b=bbb[:, obc * 256:(obc + 1) * 256]:
                           e.matmul(o, oner[:], b, start=False, stop=False),
                           reads=["oner", "bbb"], writes=[pk])
                        em("tensor", lambda e, o=pa,
                           a=hwb[:, tb * 128:(tb + 1) * 128],
                           b=bcat_s[:, obc * 256:(obc + 1) * 256]:
                           e.matmul(o, a, b, start=False, stop=True),
                           reads=["hwb", "bcat_s"], writes=[pk])
                    em("vector", lambda e,
                       o=yblk[tb][:, obc * 256:(obc + 1) * 256],
                       i_=pa: e.tensor_tensor(o, o, i_, ALU.add),
                       reads=[pk, "yb%d" % tb], writes=["yb%d" % tb])

        for tc in range(2):
            for tb in range(4):
                trow = tc * 512 + tb * 128
                em("sync", lambda e, d=yblk[tb], s=xo[trow:trow + 128, :]:
                   e.dma_start(d, s), writes=["yb%d" % tb], sem="s_res")
            mm_pass(tc, wo_bf, lora=False)
            load_w(0, bw_bf)

            psCt = palloc("psCt", [128, 2048], BF16)
            rps = []
            for tb in range(4):
                ln_pass(tb)
                em("vector", lambda e, i_=yblk[tb]: e.tensor_copy(tbuf[:], i_),
                   reads=["yb%d" % tb], writes=["tb0", "tb1", "tb2", "tb3"])
                for dg in range(8):
                    for j in range(4):
                        dc = dg * 4 + j
                        em("tensor", lambda e,
                           o=psCt[:, j * 128:(j + 1) * 128],
                           i_=tbuf[:, dc * 128:(dc + 1) * 128]:
                           e.transpose(o, i_, idb[:]),
                           reads=["tb0", "tb1", "tb2", "tb3", "idb"],
                           writes=["psC"])
                    em("vector", lambda e,
                       o=wres[:, dg * 4:dg * 4 + 4,
                              tc * 512 + tb * 128:tc * 512 + (tb + 1) * 128],
                       i_=psCt[:, 0:512].rearrange("p (j t) -> p j t", j=4):
                       e.tensor_copy(o, i_), reads=["psC"],
                       writes=[wkey(dg * 4 + j2, tc) for j2 in range(4)])
                pk, pa = rot(ei[0]); ei[0] += 1
                rps.append((pk, pa))
                em("tensor", lambda e, o=pa[:, 0:16]:
                   e.matmul(o, oner[:], rbb[:], start=True, stop=False),
                   reads=["oner", "rbb"], writes=[pk])
                for dc in range(DC):
                    em("tensor", lambda e, o=pa[:, 0:16],
                       a=wres[:, dc, tc * 512 + tb * 128:
                              tc * 512 + (tb + 1) * 128],
                       b=rwv[:, dc, :], sp=(dc == 31):
                       e.matmul(o, a, b, start=False, stop=sp),
                       reads=[wkey(dc, tc), "rwb"], writes=[pk])
            sfree("psCt")

            for tb in range(4):
                pk, pa = rps[tb]
                lgv = lg[:, 0:15]
                prv = pr[:, 0:15]
                s1v, s2v, s3v = (s1[:, tb:tb + 1], s2[:, tb:tb + 1],
                                 s3[:, tb:tb + 1])
                em("vector", lambda e, a=pa[:, 0:16], o=lg[:]:
                   e.tensor_copy(o, a), reads=[pk], writes=["lg"])
                em("vector", lambda e, o=s1v: e.tensor_reduce(
                    o, lgv, AX.X, ALU.max, negate=True),
                   reads=["lg"], writes=["s1%d" % tb])
                em("scalar", lambda e, o=prv, b=s1v, ac=s2v: e.activation(
                    o, lgv, AF.Exp, bias=b, accum_out=ac),
                   reads=["lg", "s1%d" % tb], writes=["pr", "s2%d" % tb])
                em("vector", lambda e, o=s3v, a=s2v: e.reciprocal(o, a),
                   reads=["s2%d" % tb], writes=["s3%d" % tb])
                em("vector", lambda e, o=prv, a=s3v: e.tensor_scalar(
                    o, o, a, None, ALU.mult),
                   reads=["pr", "s3%d" % tb], writes=["pr"])
                for n in range(15):
                    em("vector", lambda e, nn_=n: e.tensor_scalar(
                        j15[:, 0:15], pr[:, 0:15], pr[:, nn_:nn_ + 1], None,
                        ALU.is_gt, op1=ALU.add,
                        accum_out=cnts[:, nn_:nn_ + 1]),
                       reads=["pr"], writes=["j15", "c%d" % n])
                em("vector", lambda e: e.tensor_scalar(
                    mskb[:, 0:15], cnts[:, 0:15], 8.0, None, ALU.is_lt),
                   reads=["c%d" % n2 for n2 in range(15)], writes=["mskb"])
                em("vector", lambda e: e.tensor_tensor(
                    pmb[:, 0:15], pr[:, 0:15], mskb[:, 0:15], ALU.mult),
                   reads=["pr", "mskb"], writes=["pmb"])
                em("tensor", lambda e: e.transpose(
                    psDt[0:15, 0:128], pmb[:, 0:15], idb[:]),
                   reads=["pmb", "idb"], writes=["psD"])
                em("vector", lambda e, o=pmT[:, tb * 128:(tb + 1) * 128]:
                   e.tensor_copy(o, psDt[0:15, 0:128]),
                   reads=["psD"], writes=["pmT"])
            em("tensor", lambda e: e.matmul(psE[0:8, 0:512], mmapb[:], pmT[:],
                                            start=True, stop=True),
               reads=["mmapb", "pmT"], writes=["psE"])
            em("vector", lambda e: e.tensor_copy(wtb[:], psE[0:8, 0:512]),
               reads=["psE"], writes=["wtb"])
            em("tensor", lambda e: e.matmul(psE[:], e16b[:], wtb[:],
                                            start=True, stop=True),
               reads=["e16b", "wtb"], writes=["psE"])
            em("vector", lambda e: e.tensor_copy(wbcb[:], psE[:]),
               reads=["psE"], writes=["wbcb"])
            psCh = palloc("psCh", [128, 512])
            for dc in range(DC):
                em("tensor", lambda e, a=acv[:, dc, :],
                   b=wres[:, dc, tc * 512:(tc + 1) * 512],
                   st0=(dc == 0), sp=(dc == 31):
                   e.matmul(psCh[:], a, b, start=st0, stop=sp),
                   reads=["acat", wkey(dc, tc)], writes=["psC"])
            em("vector", lambda e: e.tensor_tensor(hwb[:], psCh[:],
                                                   wbcb[:], ALU.mult),
               reads=["psC", "wbcb"], writes=["hwb"])
            sfree("psCh")
            mm_pass(tc, bw_bf, lora=True)
            if tc == 0:
                load_w(0, wo_bf)
            for tb in range(4):
                ln_pass(tb)
                trow = tc * 512 + tb * 128
                em("sync", lambda e, d=y[trow:trow + 128, :], s=yblk[tb]:
                   e.dma_start(d, s), reads=["yb%d" % tb], writes=["y_hbm"],
                   sem="s_y")

        fin()
        p.lower(block, semobjs)
        sfree(*list(scoped.keys()))
    return nc


def _host_inputs(inp):
    x = np.ascontiguousarray(np.asarray(inp["L0_attn_hidden_states"],
                                        dtype=np.float32))
    c = np.ascontiguousarray

    def slot_img(w, nf):
        O = w.shape[0]
        return c(w.reshape(O // nf, nf, 32, 128).transpose(0, 3, 2, 1))

    wq_sl = slot_img(np.asarray(inp["L0_attn_Wq"], np.float32), 128)
    wk_sl = slot_img(np.asarray(inp["L0_attn_Wk"], np.float32), 128)
    wv_sl = slot_img(np.asarray(inp["L0_attn_Wv"], np.float32), 256)
    wo_sl = slot_img(np.asarray(inp["L0_attn_Wo"], np.float32), 256)
    bw_sl = slot_img(np.asarray(inp["L0_ffn_base_W"], np.float32), 256)
    bv = np.asarray(inp["L0_attn_bv"], np.float32).reshape(1, 1024)
    bb = np.asarray(inp["L0_ffn_base_b"], np.float32).reshape(1, D)
    rw = np.asarray(inp["L0_ffn_router_W"], np.float32)
    rwp = np.zeros((16, D), np.float32)
    rwp[0:15] = rw
    rw_sl = slot_img(rwp, 16)[0]
    rb = np.zeros((1, 16), np.float32)
    rb[0, 0:15] = np.asarray(inp["L0_ffn_router_b"], np.float32)
    A = np.asarray(inp["L0_ffn_A"], np.float32)
    acat_sl = slot_img(A.reshape(128, D), 128)[0]
    Bm = np.asarray(inp["L0_ffn_B"], np.float32)
    bcat = c(Bm.transpose(0, 2, 1).reshape(128, D))
    emap = np.asarray(inp["L0_ffn_expert_map"], np.int32).reshape(15, 1)
    iota8 = np.broadcast_to(np.arange(8, dtype=np.float32), (15, 8)).copy()
    e16 = np.zeros((8, 128), np.float32)
    for e_ in range(8):
        e16[e_, e_ * 16:(e_ + 1) * 16] = LORA_SCALE
    idbf = np.eye(128).astype(ml_dtypes.bfloat16)

    shared = dict(wq_sl=wq_sl, wk_sl=wk_sl, wv_sl=wv_sl, wo_sl=wo_sl,
                  bw_sl=bw_sl, bv=bv, bb=bb, rw_sl=rw_sl, rb=rb,
                  acat_sl=acat_sl, bcat=bcat, emap=emap,
                  iota8=iota8, e16=e16, idbf=idbf)
    xT = [c(x[b_].T) for b_ in range(B)]
    in_maps = []
    for core in range(NCORE):
        b_ = core // 4
        q0 = (core % 4) * TPC
        m = dict(shared)
        m["xoT"] = c(xT[b_][:, q0:q0 + TPC])
        m["xo"] = c(x[b_, q0:q0 + TPC, :])
        in_maps.append(m)
    return in_maps


def kernel(**inputs):
    in_maps = _host_inputs(inputs)
    nc = build_nc()
    res = run_bass_kernel_spmd(nc, in_maps, core_ids=list(range(NCORE)))
    out = np.empty((B, S, D), np.float32)
    for core in range(NCORE):
        b_ = core // 4
        q0 = (core % 4) * TPC
        out[b_, q0:q0 + TPC, :] = np.asarray(res.results[core]["y"],
                                             dtype=np.float32)
    return out


# version 16
# speedup vs baseline: 1.0954x; 1.0110x over previous
import numpy as np
import ml_dtypes
import concourse.bass as bass
import concourse.mybir as mybir
from concourse.bass_utils import run_bass_kernel_spmd

F32 = mybir.dt.float32
BF16 = mybir.dt.bfloat16
I32 = mybir.dt.int32
AF = mybir.ActivationFunctionType
ALU = mybir.AluOpType
AX = mybir.AxisListType

B, S, D = 2, 4096, 4096
NKV, NQ, HD = 8, 32, 128
NCORE = 8
TPC = 1024
DC = D // 128
QSCALE = float(HD) ** -0.5
EPS = 1e-5
LORA_SCALE = 2.0

_STREAMS = ("tensor", "scalar", "vector", "gpsimd", "sync")


class _Op:
    __slots__ = ("idx", "stream", "pos", "fn", "sem", "val", "waits", "inc",
                 "snap", "tag")


class Prog:

    def __init__(self):
        self.ops = []
        self.pos = {s: 0 for s in _STREAMS}
        self.res = {}
        self.clock = {s: {} for s in _STREAMS}
        self.dmacnt = {}

    def emit(self, stream, fn, reads=(), writes=(), sem=None, tag=""):
        op = _Op()
        op.idx = len(self.ops)
        op.stream = stream
        op.fn = fn
        op.sem = sem
        op.tag = tag
        op.inc = False
        op.pos = self.pos[stream]
        self.pos[stream] += 1
        deps = {}
        for k in reads:
            r = self.res.get(k)
            if r and r[0] is not None:
                deps[r[0].idx] = r[0]
        for k in writes:
            r = self.res.get(k)
            if r:
                if r[0] is not None:
                    deps[r[0].idx] = r[0]
                for rd in r[1]:
                    deps[rd.idx] = rd
        clk = self.clock[stream]
        waits = {}
        for d in deps.values():
            if d.stream == stream and d.sem is None and stream == "tensor":
                continue
            if d.sem is None:
                ck, onum = "s:" + d.stream, d.pos
            else:
                ck, onum = "d:" + d.sem, d.val
            if clk.get(ck, -1) >= onum:
                continue
            if d.sem is not None:
                onum = self.dmacnt[d.sem]
            d.inc = True
            prev = waits.get(ck)
            if prev is None or prev[1] < onum:
                waits[ck] = (d, onum)
            for k2, v2 in d.snap.items():
                if clk.get(k2, -1) < v2:
                    clk[k2] = v2
            clk[ck] = max(clk.get(ck, -1), onum)
        op.waits = list(waits.items())
        if sem is not None:
            self.dmacnt[sem] = self.dmacnt.get(sem, 0) + 16
            op.val = self.dmacnt[sem]
        snap = dict(clk)
        ck0 = ("s:" + stream) if sem is None else ("d:" + sem)
        o0 = op.pos if sem is None else op.val
        snap[ck0] = max(snap.get(ck0, -1), o0)
        op.snap = snap
        for k in reads:
            self.res.setdefault(k, [None, []])[1].append(op)
        for k in writes:
            e = self.res.setdefault(k, [None, []])
            e[0] = op
            e[1] = []
        self.ops.append(op)
        return op

    def recarve(self, old_keys, new_keys):
        pend = []
        for k in old_keys:
            r = self.res.pop(k, None)
            if r:
                if r[0] is not None:
                    pend.append(r[0])
                pend.extend(r[1])
        for k in new_keys:
            self.res[k] = [None, list(pend)]

    def lower(self, block, semobjs):
        cnt = {s: 0 for s in _STREAMS}
        val = {}
        for op in self.ops:
            if op.sem is None and op.inc:
                cnt[op.stream] += 1
                val[op.idx] = cnt[op.stream]
        streams = {s: [] for s in _STREAMS}
        for op in self.ops:
            streams[op.stream].append(op)

        def mk(ops):
            def body(e):
                for op in ops:
                    for ck, (d, onum) in op.waits:
                        if d.sem is None:
                            e.wait_ge(semobjs["s:" + d.stream], val[d.idx])
                        else:
                            e.wait_ge(semobjs["d:" + d.sem], onum)
                    last = op.fn(e)
                    if op.sem is not None:
                        last.then_inc(semobjs["d:" + op.sem], 16)
                    elif op.inc:
                        last.then_inc(semobjs["s:" + op.stream], 1)
            return body

        for s in _STREAMS:
            if streams[s]:
                getattr(block, s)(mk(streams[s]))


def build_nc(dbg_scratch=False, upto=4):
    nc = bass.Bass("TRN2", target_bir_lowering=False)
    p = Prog()

    def din(name, shape, dt=F32):
        return nc.dram_tensor(name, shape, dt, kind="ExternalInput").ap()

    xoT = din("xoT", [D, TPC])
    xo = din("xo", [TPC, D])
    wq_sl = din("wq_sl", [32, 128, 32, 128])
    wk_sl = din("wk_sl", [8, 128, 32, 128])
    wv_sl = din("wv_sl", [4, 128, 32, 256])
    wo_sl = din("wo_sl", [16, 128, 32, 256])
    bw_sl = din("bw_sl", [16, 128, 32, 256])
    bv = din("bv", [1, 1024])
    bb = din("bb", [1, D])
    rw_sl = din("rw_sl", [128, 32, 16])
    rb = din("rb", [1, 16])
    acat_sl = din("acat_sl", [128, 32, 128])
    bcat = din("bcat", [128, D])
    emap = din("emap", [15, 1], I32)
    iota8 = din("iota8", [15, 8])
    e16 = din("e16", [8, 128])
    idbf = din("idbf", [128, 128], BF16)

    y = nc.dram_tensor("y", [TPC, D], F32, kind="ExternalOutput").ap()

    skind = {"kind": "ExternalOutput"} if dbg_scratch else {}
    if dbg_scratch:
        ctx_dbg = nc.dram_tensor("ctx_dbg", [D, TPC], BF16,
                                 kind="ExternalOutput").ap()
    k_own = nc.dram_tensor("k_own", [1024, TPC], BF16).ap()
    v_own = nc.dram_tensor("v_own", [4, TPC, 256], BF16).ap()
    k_scr = nc.dram_tensor("k_scr", [4 * 1024, TPC], BF16, **skind).ap()
    v_scr = nc.dram_tensor("v_scr", [4, 4, TPC, 256], BF16, **skind).ap()
    q_scr = nc.dram_tensor("q_scr", [D, TPC], BF16, **skind).ap()
    wo_bf = nc.dram_tensor("wo_bf", [16, 128, 8192], BF16).ap()
    bw_bf = nc.dram_tensor("bw_bf", [16, 128, 8192], BF16).ap()

    from contextlib import ExitStack
    with ExitStack() as st:
        def sb(name, shape, dt):
            return st.enter_context(nc.sbuf_tensor(name, shape, dt))

        def psalloc(name, shape, dt=F32):
            return st.enter_context(nc.psum_tensor(name, shape, dt))

        wbig = sb("wbig", [128, 32768], BF16)
        tbuf = sb("tbuf", [128, 4096], BF16)
        cx = sb("cx", [128, 16384], BF16)
        scoped = {}
        scoped_seq = [0]

        def salloc(name, shape, dt):
            g = nc.sbuf_tensor("%s_s%d" % (name, scoped_seq[0]), shape, dt)
            scoped[name] = (scoped_seq[0], g)
            scoped_seq[0] += 1
            return g.__enter__()

        def sfree(*names):
            for n in sorted(names, key=lambda n2: -scoped[n2][0]):
                scoped.pop(n)[1].__exit__(None, None, None)

        def palloc(name, shape, dt=F32):
            g = nc.psum_tensor("%s_s%d" % (name, scoped_seq[0]), shape, dt)
            scoped[name] = (scoped_seq[0], g)
            scoped_seq[0] += 1
            return g.__enter__()
        rwb = sb("rwb", [128, 512], BF16)
        idb = sb("idb", [128, 128], BF16)
        hwb = sb("hwb", [128, 512], BF16)
        wbcb = sb("wbcb", [128, 512], BF16)
        bvb = sb("bvb", [1, 1024], BF16)
        rbb = sb("rbb", [1, 16], BF16)
        e16b = sb("e16b", [8, 128], BF16)
        mmapb = sb("mmapb", [15, 8], BF16)
        mmapf = sb("mmapf", [15, 8], F32)
        iot = sb("iot", [15, 8], F32)
        emi = sb("emi", [15, 1], I32)
        emf = sb("emf", [15, 1], F32)
        onec = sb("onec", [128, 1], BF16)
        oner = sb("oner", [1, 128], BF16)
        onerf = sb("onerf", [1, 128], F32)
        epsb = sb("epsb", [128, 1], F32)
        lg = sb("lg", [128, 16], F32)
        pr = sb("pr", [128, 16], F32)
        j15 = sb("j15", [128, 16], F32)
        cnts = sb("cnts", [128, 16], F32)
        mskb = sb("mskb", [128, 16], F32)
        pmb = sb("pmb", [128, 16], BF16)
        pmT = sb("pmT", [15, 512], BF16)
        wtb = sb("wtb", [8, 512], BF16)
        stats = sb("stats", [128, 4 * 48], F32)
        mv = sb("mv", [128, 4 * 2], F32)
        s1 = sb("s1", [128, 4], F32)
        s2 = sb("s2", [128, 4], F32)
        s3 = sb("s3", [128, 4], F32)

        psA = psalloc("psA", [128, 1024])
        psB = psalloc("psB", [128, 1024])

        sem_names = (["s:" + s for s in _STREAMS] +
                     ["d:" + n for n in
                      ("g_w", "g_x0", "g_x1", "g_wq0", "g_wq1", "g_c",
                       "s_c", "s_k", "s_v", "s_q", "s_ctx", "s_y",
                       "s_kg0", "s_kg1", "s_kg2", "s_vg0", "s_vg1", "s_vg2", "s_qh0", "s_qh1",
                       "s_cx", "s_res", "g_w0", "g_w1", "g_pc", "g_ps", "g_wv0", "g_wv1")])
        semobjs = {n: st.enter_context(nc.semaphore(n.replace(":", "_")))
                   for n in sem_names}
        block = st.enter_context(nc.Block())

        em = p.emit

        em("sync", lambda e: e.dma_start(idb[:], idbf), writes=["idb"], sem="s_c")
        em("sync", lambda e: e.dma_start(iot[:], iota8), writes=["iot"], sem="s_c")
        em("sync", lambda e: e.dma_start(emi[:], emap), writes=["emi"], sem="s_c")
        em("gpsimd", lambda e: e.dma_start(bvb[:], bv), writes=["bvb"], sem="g_c")
        em("gpsimd", lambda e: e.dma_start(rbb[:], rb), writes=["rbb"], sem="g_c")
        em("gpsimd", lambda e: e.dma_start(e16b[:], e16), writes=["e16b"], sem="g_c")
        em("gpsimd", lambda e: e.dma_start(
            rwb[:].rearrange("p (dc r) -> p dc r", dc=32), rw_sl),
           writes=["rwb"], sem="g_c")
        em("vector", lambda e: e.memset(onec[:], 1.0), writes=["onec"])
        em("vector", lambda e: e.memset(oner[:], 1.0), writes=["oner"])
        em("vector", lambda e: e.memset(onerf[:], 1.0), writes=["onerf"])
        em("vector", lambda e: e.memset(epsb[:], EPS), writes=["epsb"])
        em("vector", lambda e: e.tensor_copy(emf[:], emi[:]),
           reads=["emi"], writes=["emf"])
        em("vector", lambda e: e.tensor_scalar(
            mmapf[:], iot[:], emf[:], None, ALU.is_equal),
           reads=["iot", "emf"], writes=["mmapf"])
        em("vector", lambda e: e.tensor_copy(mmapb[:], mmapf[:]),
           reads=["mmapf"], writes=["mmapb"])

        wview = wbig[:].rearrange("p (dc f) -> p dc f", f=1024)

        psum_rot = [("psA0", psA[:, 0:512]), ("psA1", psA[:, 512:1024]),
                    ("psB0", psB[:, 0:512]), ("psB1", psB[:, 512:1024])]

        def rot(i):
            return psum_rot[i % 4]

        tq = [("tb0", tbuf[:, 0:1024]), ("tb1", tbuf[:, 1024:2048]),
              ("tb2", tbuf[:, 2048:3072]), ("tb3", tbuf[:, 3072:4096])]

        ei = [0]

        def mm_group(ps_key, ps_ap, lhs_list, rhs_list, reads):
            n = len(lhs_list)
            for i, (lh, rh) in enumerate(zip(lhs_list, rhs_list)):
                em("tensor",
                   lambda e, a=lh, b=rh, st0=(i == 0), sp=(i == n - 1), o=ps_ap:
                   e.matmul(o, a, b, start=st0, stop=sp),
                   reads=list(reads), writes=[ps_key])

        p.recarve(["wbig"], ["wbx0", "wbx1"])
        kg_t = (salloc("kg0", [128, 4096], BF16), salloc("kg1", [128, 4096], BF16))
        vg_t = (salloc("vg0", [128, 4096], BF16), salloc("vg1", [128, 4096], BF16))
        qh_t = (salloc("qh0", [128, 1024], BF16), salloc("qh1", [128, 1024], BF16))
        wq0 = salloc("wq0", [128, 4096], BF16)
        wq1 = salloc("wq1", [128, 4096], BF16)
        wvs0 = salloc("wvs0", [128, 8192], BF16)
        wvs1 = salloc("wvs1", [128, 8192], BF16)
        wqv = (wq0[:].rearrange("p (dc f) -> p dc f", dc=32),
               wq1[:].rearrange("p (dc f) -> p dc f", dc=32))
        xqv = wbig[:].rearrange("p (dc t) -> p dc t", dc=32)
        wvv2 = (wvs0[:].rearrange("p (dc f) -> p dc f", dc=32),
                wvs1[:].rearrange("p (dc f) -> p dc f", dc=32))
        xsrc = xoT.rearrange("(dc p) t -> p dc t", p=128)
        em("gpsimd", lambda e: e.dma_start(xqv[:, 0:16, :], xsrc[:, 0:16, :]),
           writes=["wbx0"], sem="g_w", tag="xqa")
        em("gpsimd", lambda e: e.dma_start(xqv[:, 16:32, :], xsrc[:, 16:32, :]),
           writes=["wbx1"], sem="g_w", tag="xqb")

        def load_wslot(wsl_in, fc, q="gpsimd"):
            par = fc % 2
            em(q, lambda e, s=wsl_in[fc], d=wqv[par]: e.dma_start(d, s),
               writes=["wq%d" % par], sem="g_wq%d" % par)

        load_wslot(wk_sl, 0)
        for kfc in range(8):
            if kfc + 1 < 8:
                load_wslot(wk_sl, kfc + 1)
            par = kfc % 2
            for osc in range(2):
                pk, pa = rot(ei[0]); ei[0] += 1
                lhs = [wqv[par][:, dc, :] for dc in range(DC)]
                rhs = [xqv[:, dc, osc * 512:(osc + 1) * 512] for dc in range(DC)]
                n_ = len(lhs)
                for i_, (lh_, rh_) in enumerate(zip(lhs, rhs)):
                    em("tensor",
                       lambda e, a=lh_, b=rh_, st0=(i_ == 0),
                       sp=(i_ == n_ - 1), o=pa:
                       e.matmul(o, a, b, start=st0, stop=sp),
                       reads=["wq%d" % par, "wbx%d" % (i_ // 16)], writes=[pk])
                tk, ta = tq[ei[0] % 4]
                em("vector", lambda e, o=ta[:, 0:512], i_=pa: e.tensor_copy(o, i_),
                   reads=[pk], writes=[tk])
                dst = k_own[kfc * 128:(kfc + 1) * 128, osc * 512:(osc + 1) * 512]
                em("sync", lambda e, d=dst, s=ta[:, 0:512]: e.dma_start(d, s),
                   reads=[tk], writes=["k_own"], sem="s_k")

        for vfc in range(4):
            par = vfc % 2
            em("gpsimd", lambda e, s=wv_sl[vfc], d=wvv2[par]: e.dma_start(d, s),
               writes=["wbv%d" % par], sem="g_wv%d" % par, tag="wv%d" % vfc)
            wvv = wvv2[par]
            for tb8 in range(8):
                pk, pa0 = rot(ei[0]); ei[0] += 1
                pa = pa0[:, 0:256]
                for dc in range(DC):
                    em("tensor", lambda e, o=pa,
                       a=xqv[:, dc, tb8 * 128:(tb8 + 1) * 128],
                       b=wvv[:, dc, :], st0=(dc == 0):
                       e.matmul(o, a, b, start=st0, stop=False),
                       reads=["wbv%d" % par, "wbx%d" % (dc // 16)], writes=[pk])
                em("tensor", lambda e, o=pa,
                   b=bvb[:, vfc * 256:(vfc + 1) * 256]:
                   e.matmul(o, oner[:], b, start=False, stop=True),
                   reads=["oner", "bvb"], writes=[pk])
                tk, ta = tq[ei[0] % 4]
                em("vector", lambda e, o=ta[:, 0:256], i_=pa: e.tensor_copy(o, i_),
                   reads=[pk], writes=[tk])
                dst = v_own[vfc, tb8 * 128:(tb8 + 1) * 128, :]
                em("sync", lambda e, d=dst, s=ta[:, 0:256]:
                   e.dma_start(d, s), reads=[tk], writes=["v_own"], sem="s_v")

        groups = [[0, 1, 2, 3], [4, 5, 6, 7]]
        em("gpsimd", lambda e: e.collective_compute(
            "AllGather", ALU.bypass, replica_groups=groups,
            ins=[k_own.opt()], outs=[k_scr.opt()]),
           reads=["k_own"], writes=["k_scr"], tag="ag_k")
        em("gpsimd", lambda e: e.collective_compute(
            "AllGather", ALU.bypass, replica_groups=groups,
            ins=[v_own.opt()], outs=[v_scr.opt()]),
           reads=["v_own"], writes=["v_scr"], tag="ag_v")

        sfree("wvs0", "wvs1")
        kg = tuple(t[:] for t in kg_t)
        vg = tuple(t[:].rearrange("p (kc h) -> p kc h", kc=32) for t in vg_t)
        vgflat = tuple(t[:] for t in vg_t)
        qh = (qh_t[0][:], qh_t[1][:])

        def load_kv(g, q="sync"):
            par = g % 2
            sk_ = k_scr.rearrange("(r kf) t -> r kf t", r=4)[
                :, g * 128:(g + 1) * 128, :].rearrange("r p t -> p r t")
            em(q, lambda e, d=kg[par].rearrange("p (r t) -> p r t", r=4),
               s=sk_: e.dma_start(d, s), reads=["k_scr"],
               writes=["kg%d" % par], sem="s_kg%d" % par)

            vsrc = v_scr.rearrange("r v (kcl p) h -> r v p kcl h", p=128)
            for rr in range(4):
                em(q, lambda e,
                   d=vgflat[par][:, rr * 1024:(rr + 1) * 1024].rearrange(
                       "p (kcl h) -> p kcl h", kcl=8),
                   s=vsrc[rr, g // 2, :, :, (g % 2) * 128:(g % 2 + 1) * 128]:
                   e.dma_start(d, s),
                   reads=["v_scr"], writes=["vg%d" % par], sem="s_vg%d" % par)

        def load_qh(h):
            par = h % 2
            em("scalar", lambda e, d=qh[par], s=q_scr[h * 128:(h + 1) * 128, :]:
               e.dma_start(d, s), reads=["q_scr"],
               writes=["qh%d" % par], sem="s_qh%d" % par)

        load_kv(0, q="scalar")
        load_kv(1, q="scalar")

        load_wslot(wq_sl, 0)
        for qfc in range(32):
            if qfc + 1 < 32:
                load_wslot(wq_sl, qfc + 1)
            par = qfc % 2
            for osc in range(2):
                pk, pa = rot(ei[0]); ei[0] += 1
                lhs = [wqv[par][:, dc, :] for dc in range(DC)]
                rhs = [xqv[:, dc, osc * 512:(osc + 1) * 512] for dc in range(DC)]
                n_ = len(lhs)
                for i_, (lh_, rh_) in enumerate(zip(lhs, rhs)):
                    em("tensor",
                       lambda e, a=lh_, b=rh_, st0=(i_ == 0),
                       sp=(i_ == n_ - 1), o=pa:
                       e.matmul(o, a, b, start=st0, stop=sp),
                       reads=["wq%d" % par, "wbx%d" % (i_ // 16)], writes=[pk])
                tk, ta = tq[ei[0] % 4]
                em("vector", lambda e, o=ta[:, 0:512], i_=pa: e.tensor_copy(o, i_),
                   reads=[pk], writes=[tk])
                dst = q_scr[qfc * 128:(qfc + 1) * 128, osc * 512:(osc + 1) * 512]
                em("sync", lambda e, d=dst, s=ta[:, 0:512]: e.dma_start(d, s),
                   reads=[tk], writes=["q_scr"], sem="s_q")

        def fin():
            em("sync", lambda e: e.nop(),
               reads=["y_hbm", "k_scr", "v_scr", "q_scr"])

        if upto < 2:
            fin()
            p.lower(block, semobjs)
            sfree(*list(scoped.keys()))
            return nc

        p.recarve(["wq0", "wq1", "wbv0", "wbv1"],
                  ["pb0", "pb1", "pb2", "ilbc", "l_sb"])
        wsl = (cx[:, 0:8192].rearrange("p (dc f) -> p dc f", dc=32),
               cx[:, 8192:16384].rearrange("p (dc f) -> p dc f", dc=32))

        def load_w(obc, wT):
            par = obc % 2
            em("gpsimd", lambda e, s=wT[obc], d=cx[:, par * 8192:(par + 1) * 8192]:
               e.dma_start(d, s),
               reads=["wo_bf"], writes=["ws%d" % par], sem="g_w%d" % par)
        p.recarve(["wbx0", "wbx1"], ["w%d_%d" % (i_, j_) for i_ in range(32)
                             for j_ in range(2)])
        p.recarve(["psA0", "psA1"], ["sc0", "sc1"])
        p.recarve(["psB0", "psB1"], ["invl", "lps"])
        sfree("wq0", "wq1")
        pb_t = (salloc("pb0", [128, 512], BF16), salloc("pb1", [128, 512], BF16),
                salloc("pb2", [128, 512], BF16))
        ilbc = salloc("ilbc", [128, 1024], F32)[:]
        l_sb = salloc("l_sb", [1, 1024], F32)[:]
        il_sb = l_sb
        psC0 = palloc("psC0", [128, 1024])
        psC1 = palloc("psC1", [128, 1024])
        pb = (pb_t[0][:], pb_t[1][:], pb_t[2][:])
        scv = (psA[:, 0:512], psA[:, 512:1024])
        invlps = psB[:, 0:512]
        lps = psB[:, 512:1024]
        ctxps = (psC0, psC1)
        ctxv = wview

        def emit_pv_l(gp, hp, t):
            kc, half = t // 2, t % 2
            pk = "pb%d" % (t % 3)
            pbh = pb[t % 3]
            cp = ctxps[hp]
            em("tensor", lambda e, o=cp[:, half * 512:(half + 1) * 512],
               a=vg[gp][:, kc, :], b=pbh,
               st0=(t < 2), sp=(t >= 62):
               e.matmul(o, a, b, start=st0, stop=sp),
               reads=["vg%d" % gp, pk], writes=["psC%d" % hp])
            lrow = lps[0:1, :] if half == 0 else lps[32:33, :]
            em("tensor", lambda e, o=lrow, b=pbh,
               st0=(t < 2), sp=(t >= 62):
               e.matmul(o, onec[:], b, start=st0, stop=sp,
                        skip_group_check=True),
               reads=[pk, "onec"], writes=["lps"])

        def make_finalizer(h):
            hp2 = h % 2

            def fin_head_a():
                em("vector", lambda e: e.tensor_copy(l_sb[:, 0:512],
                                                     lps[0:1, :]),
                   reads=["lps"], writes=["l_sb"])
                em("vector", lambda e: e.tensor_copy(l_sb[:, 512:1024],
                                                     lps[32:33, :]),
                   reads=["lps"], writes=["l_sb"])
                em("scalar", lambda e: e.activation(l_sb, l_sb, AF.Ln),
                   reads=["l_sb"], writes=["l_sb"])
                em("scalar", lambda e: e.activation(l_sb, l_sb, AF.Exp,
                                                    scale=-1.0),
                   reads=["l_sb"], writes=["l_sb"])

            def fin_head_b():
                for half in range(2):
                    em("tensor", lambda e,
                       b=il_sb[:, half * 512:(half + 1) * 512]:
                       e.matmul(invlps, onerf[:], b, start=True, stop=True),
                       reads=["l_sb", "onerf"], writes=["invl"])
                    em("vector", lambda e,
                       o=ilbc[:, half * 512:(half + 1) * 512]:
                       e.tensor_copy(o, invlps), reads=["invl"],
                       writes=["ilbc"])
                em("vector", lambda e, o=ctxv[:, h, :], a=ctxps[hp2]:
                   e.tensor_tensor(o, a[:, :], ilbc, ALU.mult),
                   reads=["psC%d" % hp2, "ilbc"],
                   writes=["w%d_0" % h, "w%d_1" % h])
            return fin_head_a, fin_head_b

        pc_jobs = []
        tbv = tbuf[:].rearrange("p (hf dc f) -> p hf dc f", hf=2, dc=8)
        for wsrc, wdst in ((wo_sl, wo_bf), (bw_sl, bw_bf)):
            for j in range(16):
                for q_ in range(4):
                    s_ = wsrc[j, :, q_ * 8:(q_ + 1) * 8, :]
                    d_ = wdst[j][:, q_ * 2048:(q_ + 1) * 2048]
                    t_ = tbv[:, q_ % 2]
                    tk_ = "tbh%d" % (q_ % 2)
                    pc_jobs.append((s_, d_, t_, tk_))
        p.recarve(["tb0", "tb1", "tb2", "tb3"], ["tbh0", "tbh1"])
        pc_i = [0]

        def pump_precast(njobs):
            for _ in range(njobs):
                if pc_i[0] >= len(pc_jobs):
                    return
                s_, d_, t_, tk_ = pc_jobs[pc_i[0]]
                pc_i[0] += 1
                em("gpsimd", lambda e, a=t_, b=s_: e.dma_start(a, b),
                   writes=[tk_], sem="g_pc")
                em("gpsimd", lambda e, a=d_, b=t_: e.dma_start(a, b),
                   reads=[tk_], writes=["wo_bf"], sem="g_ps")

        load_qh(0)
        pend_b = None
        for g in range(NKV):
            gp = g % 2
            if g + 1 < NKV and g >= 1:
                load_kv(g + 1)
            for q4 in range(4):
                h = 4 * g + q4
                hp = h % 2
                if h + 1 < NQ:
                    load_qh(h + 1)
                pump_precast(4)
                kkey, qkey = "kg%d" % gp, "qh%d" % hp
                for t in range(64):
                    kc, half = t // 2, t % 2
                    em("tensor", lambda e, o=scv[t % 2],
                       a=kg[gp][:, kc * 128:(kc + 1) * 128],
                       b=qh[hp][:, half * 512:(half + 1) * 512]:
                       e.matmul(o, a, b, start=True, stop=True),
                       reads=[kkey, qkey], writes=["sc%d" % (t % 2)])
                    em("scalar", lambda e, o=pb[t % 3], i_=scv[t % 2]:
                       e.activation(o, i_, AF.Exp, scale=QSCALE),
                       reads=["sc%d" % (t % 2)], writes=["pb%d" % (t % 3)])
                    if t == 6 and pend_b is not None:
                        pend_b()
                        pend_b = None
                    if t >= 2:
                        emit_pv_l(gp, hp, t - 2)
                emit_pv_l(gp, hp, 62)
                emit_pv_l(gp, hp, 63)
                fa, fb = make_finalizer(h)
                fa()
                pend_b = fb
        pend_b()
        pump_precast(len(pc_jobs))
        if dbg_scratch:
            em("sync", lambda e: e.dma_start(
                ctx_dbg.rearrange("(h p) t -> p h t", p=128), ctxv),
               reads=["w%d_%d" % (i_, j_) for i_ in range(32) for j_ in range(2)],
               writes=["ctx_dbg"], sem="s_c")
        load_w(0, wo_bf)

        if upto < 3:
            fin()
            p.lower(block, semobjs)
            sfree(*list(scoped.keys()))
            return nc

        p.recarve(["kg0", "kg1", "vg0", "vg1", "qh0", "qh1",
                   "pb0", "pb1", "pb2", "ilbc", "l_sb"],
                  ["yb0", "yb1", "yb2", "yb3"])
        p.recarve(["tbh0", "tbh1"], ["tb0", "tb1", "tb2", "tb3"])
        p.recarve(["sc0", "sc1"], ["psA0", "psA1"])
        p.recarve(["invl", "lps"], ["psB0", "psB1"])
        sfree("kg0", "kg1", "vg0", "vg1", "qh0", "qh1",
              "pb0", "pb1", "pb2", "ilbc", "l_sb", "psC0", "psC1")
        ybig = salloc("ybig", [128, 16384], F32)
        acat = salloc("acat", [128, 4096], BF16)
        bcat_s = salloc("bcat_s", [128, 4096], BF16)
        bbb = salloc("bbb", [1, 4096], BF16)
        em("gpsimd", lambda e: e.dma_start(
            acat[:].rearrange("p (dc r) -> p dc r", dc=32), acat_sl),
           writes=["acat"], sem="g_c")
        em("gpsimd", lambda e: e.dma_start(bcat_s[:], bcat),
           writes=["bcat_s"], sem="g_c")
        em("gpsimd", lambda e: e.dma_start(bbb[:], bb), writes=["bbb"],
           sem="g_c")
        psDt = palloc("psDt", [64, 1024], BF16)
        psE = palloc("psE", [128, 512])
        ybv = ybig[:].rearrange("p (tb f) -> p tb f", tb=4)
        yblk = [ybv[:, i, :] for i in range(4)]
        wres = wview
        rwv = rwb[:].rearrange("p (dc r) -> p dc r", dc=32)
        acv = acat[:].rearrange("p (dc r) -> p dc r", dc=32)

        def wkey(i, tc):
            return "w%d_%d" % (i, tc)

        def emit_bn(tb, i):
            yb = yblk[tb]
            stv = stats[:, 48 * tb:48 * (tb + 1)]
            em("vector", lambda e, o=stv[:, 6 * i:6 * i + 6],
               a=yb[:, 512 * i:512 * (i + 1)]:
               e.bn_stats(o, a), reads=["yb%d" % tb],
               writes=["st%d_%d" % (tb, i)])

        def ln_pass(tb, stats_done=False):
            ybk = "yb%d" % tb
            yb = yblk[tb]
            stv = stats[:, 48 * tb:48 * (tb + 1)]
            mvv = mv[:, 2 * tb:2 * (tb + 1)]
            s1v, s2v, s3v = (s1[:, tb:tb + 1], s2[:, tb:tb + 1],
                             s3[:, tb:tb + 1])
            if not stats_done:
                for i in range(8):
                    emit_bn(tb, i)
            em("vector", lambda e, o=mvv, a=stv: e.bn_aggr(o, a),
               reads=["st%d_%d" % (tb, i2) for i2 in range(8)],
               writes=["mv%d" % tb])
            em("scalar", lambda e, o=s1v, a=mvv[:, 1:2]:
               e.activation(o, a, AF.Ln, bias=epsb[:]),
               reads=["mv%d" % tb, "epsb"], writes=["s1%d" % tb])
            em("scalar", lambda e, o=s2v, a=s1v:
               e.activation(o, a, AF.Exp, scale=-0.5),
               reads=["s1%d" % tb], writes=["s2%d" % tb])
            em("vector", lambda e, o=s3v, a=mvv[:, 0:1], b=s2v:
               e.tensor_tensor(o, a, b, ALU.mult),
               reads=["mv%d" % tb, "s2%d" % tb], writes=["s3%d" % tb])
            em("vector", lambda e, o=yb, a=s2v, b=s3v: e.tensor_scalar(
                o, o, a, b, ALU.mult, op1=ALU.subtract),
               reads=[ybk, "s2%d" % tb, "s3%d" % tb], writes=[ybk])

        def mm_pass(tc, wbf, lora):
            for obc in range(16):
                if obc + 1 < 16:
                    load_w(obc + 1, wbf)
                par = obc % 2
                for tb in range(4):
                    pk, pa0 = rot(ei[0]); ei[0] += 1
                    pa = pa0[:, 0:256]
                    for dc in range(DC):
                        em("tensor", lambda e, o=pa,
                           a=wres[:, dc, tc * 512 + tb * 128:
                                  tc * 512 + (tb + 1) * 128],
                           b=wsl[par][:, dc, :], st0=(dc == 0),
                           sp=(dc == 31 and not lora):
                           e.matmul(o, a, b, start=st0, stop=sp),
                           reads=[wkey(dc, tc), "ws%d" % par], writes=[pk])
                    if lora:
                        em("tensor", lambda e, o=pa,
                           b=bbb[:, obc * 256:(obc + 1) * 256]:
                           e.matmul(o, oner[:], b, start=False, stop=False),
                           reads=["oner", "bbb"], writes=[pk])
                        em("tensor", lambda e, o=pa,
                           a=hwb[:, tb * 128:(tb + 1) * 128],
                           b=bcat_s[:, obc * 256:(obc + 1) * 256]:
                           e.matmul(o, a, b, start=False, stop=True),
                           reads=["hwb", "bcat_s"], writes=[pk])
                    em("vector", lambda e,
                       o=yblk[tb][:, obc * 256:(obc + 1) * 256],
                       i_=pa: e.tensor_tensor(o, o, i_, ALU.add),
                       reads=[pk, "yb%d" % tb], writes=["yb%d" % tb])
                    if obc % 2 == 1:
                        emit_bn(tb, obc // 2)

        for tc in range(2):
            for tb in range(4):
                trow = tc * 512 + tb * 128
                em("sync", lambda e, d=yblk[tb], s=xo[trow:trow + 128, :]:
                   e.dma_start(d, s), writes=["yb%d" % tb], sem="s_res")
            mm_pass(tc, wo_bf, lora=False)
            load_w(0, bw_bf)

            psCt = palloc("psCt", [128, 2048], BF16)
            rps = []
            for tb in range(4):
                ln_pass(tb, stats_done=True)
                em("vector", lambda e, i_=yblk[tb]: e.tensor_copy(tbuf[:], i_),
                   reads=["yb%d" % tb], writes=["tb0", "tb1", "tb2", "tb3"])
                for dg in range(8):
                    for j in range(4):
                        dc = dg * 4 + j
                        em("tensor", lambda e,
                           o=psCt[:, j * 128:(j + 1) * 128],
                           i_=tbuf[:, dc * 128:(dc + 1) * 128]:
                           e.transpose(o, i_, idb[:]),
                           reads=["tb0", "tb1", "tb2", "tb3", "idb"],
                           writes=["psC"])
                    em("vector", lambda e,
                       o=wres[:, dg * 4:dg * 4 + 4,
                              tc * 512 + tb * 128:tc * 512 + (tb + 1) * 128],
                       i_=psCt[:, 0:512].rearrange("p (j t) -> p j t", j=4):
                       e.tensor_copy(o, i_), reads=["psC"],
                       writes=[wkey(dg * 4 + j2, tc) for j2 in range(4)])
                pk, pa = rot(ei[0]); ei[0] += 1
                rps.append((pk, pa))
                em("tensor", lambda e, o=pa[:, 0:16]:
                   e.matmul(o, oner[:], rbb[:], start=True, stop=False),
                   reads=["oner", "rbb"], writes=[pk])
                for dc in range(DC):
                    em("tensor", lambda e, o=pa[:, 0:16],
                       a=wres[:, dc, tc * 512 + tb * 128:
                              tc * 512 + (tb + 1) * 128],
                       b=rwv[:, dc, :], sp=(dc == 31):
                       e.matmul(o, a, b, start=False, stop=sp),
                       reads=[wkey(dc, tc), "rwb"], writes=[pk])
            sfree("psCt")

            for tb in range(4):
                pk, pa = rps[tb]
                lgv = lg[:, 0:15]
                prv = pr[:, 0:15]
                s1v, s2v, s3v = (s1[:, tb:tb + 1], s2[:, tb:tb + 1],
                                 s3[:, tb:tb + 1])
                em("vector", lambda e, a=pa[:, 0:16], o=lg[:]:
                   e.tensor_copy(o, a), reads=[pk], writes=["lg"])
                em("vector", lambda e, o=s1v: e.tensor_reduce(
                    o, lgv, AX.X, ALU.max, negate=True),
                   reads=["lg"], writes=["s1%d" % tb])
                em("scalar", lambda e, o=prv, b=s1v, ac=s2v: e.activation(
                    o, lgv, AF.Exp, bias=b, accum_out=ac),
                   reads=["lg", "s1%d" % tb], writes=["pr", "s2%d" % tb])
                em("vector", lambda e, o=s3v, a=s2v: e.reciprocal(o, a),
                   reads=["s2%d" % tb], writes=["s3%d" % tb])
                em("vector", lambda e, o=prv, a=s3v: e.tensor_scalar(
                    o, o, a, None, ALU.mult),
                   reads=["pr", "s3%d" % tb], writes=["pr"])
                for n in range(15):
                    em("vector", lambda e, nn_=n: e.tensor_scalar(
                        j15[:, 0:15], pr[:, 0:15], pr[:, nn_:nn_ + 1], None,
                        ALU.is_gt, op1=ALU.add,
                        accum_out=cnts[:, nn_:nn_ + 1]),
                       reads=["pr"], writes=["j15", "c%d" % n])
                em("vector", lambda e: e.tensor_scalar(
                    mskb[:, 0:15], cnts[:, 0:15], 8.0, None, ALU.is_lt),
                   reads=["c%d" % n2 for n2 in range(15)], writes=["mskb"])
                em("vector", lambda e: e.tensor_tensor(
                    pmb[:, 0:15], pr[:, 0:15], mskb[:, 0:15], ALU.mult),
                   reads=["pr", "mskb"], writes=["pmb"])
                em("tensor", lambda e: e.transpose(
                    psDt[0:15, 0:128], pmb[:, 0:15], idb[:]),
                   reads=["pmb", "idb"], writes=["psD"])
                em("vector", lambda e, o=pmT[:, tb * 128:(tb + 1) * 128]:
                   e.tensor_copy(o, psDt[0:15, 0:128]),
                   reads=["psD"], writes=["pmT"])
            em("tensor", lambda e: e.matmul(psE[0:8, 0:512], mmapb[:], pmT[:],
                                            start=True, stop=True),
               reads=["mmapb", "pmT"], writes=["psE"])
            em("vector", lambda e: e.tensor_copy(wtb[:], psE[0:8, 0:512]),
               reads=["psE"], writes=["wtb"])
            em("tensor", lambda e: e.matmul(psE[:], e16b[:], wtb[:],
                                            start=True, stop=True),
               reads=["e16b", "wtb"], writes=["psE"])
            em("vector", lambda e: e.tensor_copy(wbcb[:], psE[:]),
               reads=["psE"], writes=["wbcb"])
            psCh = palloc("psCh", [128, 512])
            for dc in range(DC):
                em("tensor", lambda e, a=acv[:, dc, :],
                   b=wres[:, dc, tc * 512:(tc + 1) * 512],
                   st0=(dc == 0), sp=(dc == 31):
                   e.matmul(psCh[:], a, b, start=st0, stop=sp),
                   reads=["acat", wkey(dc, tc)], writes=["psC"])
            em("vector", lambda e: e.tensor_tensor(hwb[:], psCh[:],
                                                   wbcb[:], ALU.mult),
               reads=["psC", "wbcb"], writes=["hwb"])
            sfree("psCh")
            mm_pass(tc, bw_bf, lora=True)
            if tc == 0:
                load_w(0, wo_bf)
            for tb in range(4):
                ln_pass(tb, stats_done=True)
                trow = tc * 512 + tb * 128
                em("sync", lambda e, d=y[trow:trow + 128, :], s=yblk[tb]:
                   e.dma_start(d, s), reads=["yb%d" % tb], writes=["y_hbm"],
                   sem="s_y")

        fin()
        p.lower(block, semobjs)
        sfree(*list(scoped.keys()))
    return nc


def _host_inputs(inp):
    x = np.ascontiguousarray(np.asarray(inp["L0_attn_hidden_states"],
                                        dtype=np.float32))
    c = np.ascontiguousarray

    def slot_img(w, nf):
        O = w.shape[0]
        return c(w.reshape(O // nf, nf, 32, 128).transpose(0, 3, 2, 1))

    wq_sl = slot_img(np.asarray(inp["L0_attn_Wq"], np.float32), 128)
    wk_sl = slot_img(np.asarray(inp["L0_attn_Wk"], np.float32), 128)
    wv_sl = slot_img(np.asarray(inp["L0_attn_Wv"], np.float32), 256)
    wo_sl = slot_img(np.asarray(inp["L0_attn_Wo"], np.float32), 256)
    bw_sl = slot_img(np.asarray(inp["L0_ffn_base_W"], np.float32), 256)
    bv = np.asarray(inp["L0_attn_bv"], np.float32).reshape(1, 1024)
    bb = np.asarray(inp["L0_ffn_base_b"], np.float32).reshape(1, D)
    rw = np.asarray(inp["L0_ffn_router_W"], np.float32)
    rwp = np.zeros((16, D), np.float32)
    rwp[0:15] = rw
    rw_sl = slot_img(rwp, 16)[0]
    rb = np.zeros((1, 16), np.float32)
    rb[0, 0:15] = np.asarray(inp["L0_ffn_router_b"], np.float32)
    A = np.asarray(inp["L0_ffn_A"], np.float32)
    acat_sl = slot_img(A.reshape(128, D), 128)[0]
    Bm = np.asarray(inp["L0_ffn_B"], np.float32)
    bcat = c(Bm.transpose(0, 2, 1).reshape(128, D))
    emap = np.asarray(inp["L0_ffn_expert_map"], np.int32).reshape(15, 1)
    iota8 = np.broadcast_to(np.arange(8, dtype=np.float32), (15, 8)).copy()
    e16 = np.zeros((8, 128), np.float32)
    for e_ in range(8):
        e16[e_, e_ * 16:(e_ + 1) * 16] = LORA_SCALE
    idbf = np.eye(128).astype(ml_dtypes.bfloat16)

    shared = dict(wq_sl=wq_sl, wk_sl=wk_sl, wv_sl=wv_sl, wo_sl=wo_sl,
                  bw_sl=bw_sl, bv=bv, bb=bb, rw_sl=rw_sl, rb=rb,
                  acat_sl=acat_sl, bcat=bcat, emap=emap,
                  iota8=iota8, e16=e16, idbf=idbf)
    xT = [c(x[b_].T) for b_ in range(B)]
    in_maps = []
    for core in range(NCORE):
        b_ = core // 4
        q0 = (core % 4) * TPC
        m = dict(shared)
        m["xoT"] = c(xT[b_][:, q0:q0 + TPC])
        m["xo"] = c(x[b_, q0:q0 + TPC, :])
        in_maps.append(m)
    return in_maps


def kernel(**inputs):
    in_maps = _host_inputs(inputs)
    nc = build_nc()
    res = run_bass_kernel_spmd(nc, in_maps, core_ids=list(range(NCORE)))
    out = np.empty((B, S, D), np.float32)
    for core in range(NCORE):
        b_ = core // 4
        q0 = (core % 4) * TPC
        out[b_, q0:q0 + TPC, :] = np.asarray(res.results[core]["y"],
                                             dtype=np.float32)
    return out


# version 17 (final)
# speedup vs baseline: 1.1052x; 1.0090x over previous
import numpy as np
import ml_dtypes
import concourse.bass as bass
import concourse.mybir as mybir
from concourse.bass_utils import run_bass_kernel_spmd

F32 = mybir.dt.float32
BF16 = mybir.dt.bfloat16
I32 = mybir.dt.int32
AF = mybir.ActivationFunctionType
ALU = mybir.AluOpType
AX = mybir.AxisListType

B, S, D = 2, 4096, 4096
NKV, NQ, HD = 8, 32, 128
NCORE = 8
TPC = 1024
DC = D // 128
QSCALE = float(HD) ** -0.5
EPS = 1e-5
LORA_SCALE = 2.0

_STREAMS = ("tensor", "scalar", "vector", "gpsimd", "sync")


class _Op:
    __slots__ = ("idx", "stream", "pos", "fn", "sem", "val", "waits", "inc",
                 "snap", "tag")


class Prog:

    def __init__(self):
        self.ops = []
        self.pos = {s: 0 for s in _STREAMS}
        self.res = {}
        self.clock = {s: {} for s in _STREAMS}
        self.dmacnt = {}

    def emit(self, stream, fn, reads=(), writes=(), sem=None, tag=""):
        op = _Op()
        op.idx = len(self.ops)
        op.stream = stream
        op.fn = fn
        op.sem = sem
        op.tag = tag
        op.inc = False
        op.pos = self.pos[stream]
        self.pos[stream] += 1
        deps = {}
        for k in reads:
            r = self.res.get(k)
            if r and r[0] is not None:
                deps[r[0].idx] = r[0]
        for k in writes:
            r = self.res.get(k)
            if r:
                if r[0] is not None:
                    deps[r[0].idx] = r[0]
                for rd in r[1]:
                    deps[rd.idx] = rd
        clk = self.clock[stream]
        waits = {}
        for d in deps.values():
            if d.stream == stream and d.sem is None and stream == "tensor":
                continue
            if sem is not None and d.sem == sem:
                continue
            if d.sem is None:
                ck, onum = "s:" + d.stream, d.pos
            else:
                ck, onum = "d:" + d.sem, d.val
            if clk.get(ck, -1) >= onum:
                continue
            if d.sem is not None:
                onum = self.dmacnt[d.sem]
            d.inc = True
            prev = waits.get(ck)
            if prev is None or prev[1] < onum:
                waits[ck] = (d, onum)
            for k2, v2 in d.snap.items():
                if clk.get(k2, -1) < v2:
                    clk[k2] = v2
            clk[ck] = max(clk.get(ck, -1), onum)
        op.waits = list(waits.items())
        if sem is not None:
            self.dmacnt[sem] = self.dmacnt.get(sem, 0) + 16
            op.val = self.dmacnt[sem]
        snap = dict(clk)
        ck0 = ("s:" + stream) if sem is None else ("d:" + sem)
        o0 = op.pos if sem is None else op.val
        snap[ck0] = max(snap.get(ck0, -1), o0)
        op.snap = snap
        for k in reads:
            self.res.setdefault(k, [None, []])[1].append(op)
        for k in writes:
            e = self.res.setdefault(k, [None, []])
            e[0] = op
            e[1] = []
        self.ops.append(op)
        return op

    def recarve(self, old_keys, new_keys):
        pend = []
        for k in old_keys:
            r = self.res.pop(k, None)
            if r:
                if r[0] is not None:
                    pend.append(r[0])
                pend.extend(r[1])
        for k in new_keys:
            self.res[k] = [None, list(pend)]

    def lower(self, block, semobjs):
        cnt = {s: 0 for s in _STREAMS}
        val = {}
        for op in self.ops:
            if op.sem is None and op.inc:
                cnt[op.stream] += 1
                val[op.idx] = cnt[op.stream]
        streams = {s: [] for s in _STREAMS}
        for op in self.ops:
            streams[op.stream].append(op)

        def mk(ops):
            def body(e):
                for op in ops:
                    for ck, (d, onum) in op.waits:
                        if d.sem is None:
                            e.wait_ge(semobjs["s:" + d.stream], val[d.idx])
                        else:
                            e.wait_ge(semobjs["d:" + d.sem], onum)
                    last = op.fn(e)
                    if op.sem is not None:
                        last.then_inc(semobjs["d:" + op.sem], 16)
                    elif op.inc:
                        last.then_inc(semobjs["s:" + op.stream], 1)
            return body

        for s in _STREAMS:
            if streams[s]:
                getattr(block, s)(mk(streams[s]))


def build_nc(dbg_scratch=False, upto=4):
    nc = bass.Bass("TRN2", target_bir_lowering=False)
    p = Prog()

    def din(name, shape, dt=F32):
        return nc.dram_tensor(name, shape, dt, kind="ExternalInput").ap()

    xoT = din("xoT", [D, TPC])
    xo = din("xo", [TPC, D])
    wq_sl = din("wq_sl", [32, 128, 32, 128])
    wk_sl = din("wk_sl", [8, 128, 32, 128])
    wv_sl = din("wv_sl", [4, 128, 32, 256])
    wo_sl = din("wo_sl", [16, 128, 32, 256])
    bw_sl = din("bw_sl", [16, 128, 32, 256])
    bv = din("bv", [1, 1024])
    bb = din("bb", [1, D])
    rw_sl = din("rw_sl", [128, 32, 16])
    rb = din("rb", [1, 16])
    acat_sl = din("acat_sl", [128, 32, 128])
    bcat = din("bcat", [128, D])
    emap = din("emap", [15, 1], I32)
    iota8 = din("iota8", [15, 8])
    e16 = din("e16", [8, 128])
    idbf = din("idbf", [128, 128], BF16)

    y = nc.dram_tensor("y", [TPC, D], F32, kind="ExternalOutput").ap()

    skind = {"kind": "ExternalOutput"} if dbg_scratch else {}
    if dbg_scratch:
        ctx_dbg = nc.dram_tensor("ctx_dbg", [D, TPC], BF16,
                                 kind="ExternalOutput").ap()
    k_own = nc.dram_tensor("k_own", [1024, TPC], BF16).ap()
    v_own = nc.dram_tensor("v_own", [4, TPC, 256], BF16).ap()
    k_scr = nc.dram_tensor("k_scr", [4 * 1024, TPC], BF16, **skind).ap()
    v_scr = nc.dram_tensor("v_scr", [4, 4, TPC, 256], BF16, **skind).ap()
    q_scr = nc.dram_tensor("q_scr", [D, TPC], BF16, **skind).ap()
    wo_bf = nc.dram_tensor("wo_bf", [16, 128, 8192], BF16).ap()
    bw_bf = nc.dram_tensor("bw_bf", [16, 128, 8192], BF16).ap()

    from contextlib import ExitStack
    with ExitStack() as st:
        def sb(name, shape, dt):
            return st.enter_context(nc.sbuf_tensor(name, shape, dt))

        def psalloc(name, shape, dt=F32):
            return st.enter_context(nc.psum_tensor(name, shape, dt))

        wbig = sb("wbig", [128, 32768], BF16)
        tbuf = sb("tbuf", [128, 4096], BF16)
        cx = sb("cx", [128, 16384], BF16)
        scoped = {}
        scoped_seq = [0]

        def salloc(name, shape, dt):
            g = nc.sbuf_tensor("%s_s%d" % (name, scoped_seq[0]), shape, dt)
            scoped[name] = (scoped_seq[0], g)
            scoped_seq[0] += 1
            return g.__enter__()

        def sfree(*names):
            for n in sorted(names, key=lambda n2: -scoped[n2][0]):
                scoped.pop(n)[1].__exit__(None, None, None)

        def palloc(name, shape, dt=F32):
            g = nc.psum_tensor("%s_s%d" % (name, scoped_seq[0]), shape, dt)
            scoped[name] = (scoped_seq[0], g)
            scoped_seq[0] += 1
            return g.__enter__()
        rwb = sb("rwb", [128, 512], BF16)
        idb = sb("idb", [128, 128], BF16)
        hwb = sb("hwb", [128, 512], BF16)
        wbcb = sb("wbcb", [128, 512], BF16)
        bvb = sb("bvb", [1, 1024], BF16)
        rbb = sb("rbb", [1, 16], BF16)
        e16b = sb("e16b", [8, 128], BF16)
        mmapb = sb("mmapb", [15, 8], BF16)
        mmapf = sb("mmapf", [15, 8], F32)
        iot = sb("iot", [15, 8], F32)
        emi = sb("emi", [15, 1], I32)
        emf = sb("emf", [15, 1], F32)
        onec = sb("onec", [128, 1], BF16)
        oner = sb("oner", [1, 128], BF16)
        onerf = sb("onerf", [1, 128], F32)
        epsb = sb("epsb", [128, 1], F32)
        lg = sb("lg", [128, 16], F32)
        pr = sb("pr", [128, 16], F32)
        j15 = sb("j15", [128, 16], F32)
        cnts = sb("cnts", [128, 16], F32)
        mskb = sb("mskb", [128, 16], F32)
        pmb = sb("pmb", [128, 16], BF16)
        pmT = sb("pmT", [15, 512], BF16)
        wtb = sb("wtb", [8, 512], BF16)
        stats = sb("stats", [128, 4 * 48], F32)
        mv = sb("mv", [128, 4 * 2], F32)
        s1 = sb("s1", [128, 4], F32)
        s2 = sb("s2", [128, 4], F32)
        s3 = sb("s3", [128, 4], F32)

        psA = psalloc("psA", [128, 1024])
        psB = psalloc("psB", [128, 1024])

        sem_names = (["s:" + s for s in _STREAMS] +
                     ["d:" + n for n in
                      ("g_w", "g_x0", "g_x1", "g_wq0", "g_wq1", "g_c",
                       "s_c", "s_k", "s_v", "s_q", "s_ctx", "s_y",
                       "s_kg0", "s_kg1", "s_kg2", "s_vg0", "s_vg1", "s_vg2", "s_qh0", "s_qh1",
                       "s_cx", "s_res", "g_w0", "g_w1", "g_pc", "g_ps", "g_wv0", "g_wv1")])
        semobjs = {n: st.enter_context(nc.semaphore(n.replace(":", "_")))
                   for n in sem_names}
        block = st.enter_context(nc.Block())

        em = p.emit

        em("sync", lambda e: e.dma_start(idb[:], idbf), writes=["idb"], sem="s_c")
        em("sync", lambda e: e.dma_start(iot[:], iota8), writes=["iot"], sem="s_c")
        em("sync", lambda e: e.dma_start(emi[:], emap), writes=["emi"], sem="s_c")
        em("gpsimd", lambda e: e.dma_start(bvb[:], bv), writes=["bvb"], sem="g_c")
        em("gpsimd", lambda e: e.dma_start(rbb[:], rb), writes=["rbb"], sem="g_c")
        em("gpsimd", lambda e: e.dma_start(e16b[:], e16), writes=["e16b"], sem="g_c")
        em("gpsimd", lambda e: e.dma_start(
            rwb[:].rearrange("p (dc r) -> p dc r", dc=32), rw_sl),
           writes=["rwb"], sem="g_c")
        em("vector", lambda e: e.memset(onec[:], 1.0), writes=["onec"])
        em("vector", lambda e: e.memset(oner[:], 1.0), writes=["oner"])
        em("vector", lambda e: e.memset(onerf[:], 1.0), writes=["onerf"])
        em("vector", lambda e: e.memset(epsb[:], EPS), writes=["epsb"])
        em("vector", lambda e: e.tensor_copy(emf[:], emi[:]),
           reads=["emi"], writes=["emf"])
        em("vector", lambda e: e.tensor_scalar(
            mmapf[:], iot[:], emf[:], None, ALU.is_equal),
           reads=["iot", "emf"], writes=["mmapf"])
        em("vector", lambda e: e.tensor_copy(mmapb[:], mmapf[:]),
           reads=["mmapf"], writes=["mmapb"])

        wview = wbig[:].rearrange("p (dc f) -> p dc f", f=1024)

        psum_rot = [("psA0", psA[:, 0:512]), ("psA1", psA[:, 512:1024]),
                    ("psB0", psB[:, 0:512]), ("psB1", psB[:, 512:1024])]

        def rot(i):
            return psum_rot[i % 4]

        tq = [("tb0", tbuf[:, 0:1024]), ("tb1", tbuf[:, 1024:2048]),
              ("tb2", tbuf[:, 2048:3072]), ("tb3", tbuf[:, 3072:4096])]

        ei = [0]

        def mm_group(ps_key, ps_ap, lhs_list, rhs_list, reads):
            n = len(lhs_list)
            for i, (lh, rh) in enumerate(zip(lhs_list, rhs_list)):
                em("tensor",
                   lambda e, a=lh, b=rh, st0=(i == 0), sp=(i == n - 1), o=ps_ap:
                   e.matmul(o, a, b, start=st0, stop=sp),
                   reads=list(reads), writes=[ps_key])

        p.recarve(["wbig"], ["wbx0", "wbx1"])
        kg_t = (salloc("kg0", [128, 4096], BF16), salloc("kg1", [128, 4096], BF16))
        vg_t = (salloc("vg0", [128, 4096], BF16), salloc("vg1", [128, 4096], BF16))
        qh_t = (salloc("qh0", [128, 1024], BF16), salloc("qh1", [128, 1024], BF16))
        wq0 = salloc("wq0", [128, 4096], BF16)
        wq1 = salloc("wq1", [128, 4096], BF16)
        wvs0 = salloc("wvs0", [128, 8192], BF16)
        wvs1 = salloc("wvs1", [128, 8192], BF16)
        wqv = (wq0[:].rearrange("p (dc f) -> p dc f", dc=32),
               wq1[:].rearrange("p (dc f) -> p dc f", dc=32))
        xqv = wbig[:].rearrange("p (dc t) -> p dc t", dc=32)
        wvv2 = (wvs0[:].rearrange("p (dc f) -> p dc f", dc=32),
                wvs1[:].rearrange("p (dc f) -> p dc f", dc=32))
        xsrc = xoT.rearrange("(dc p) t -> p dc t", p=128)
        em("gpsimd", lambda e: e.dma_start(xqv[:, 0:16, :], xsrc[:, 0:16, :]),
           writes=["wbx0"], sem="g_w", tag="xqa")
        em("gpsimd", lambda e: e.dma_start(xqv[:, 16:32, :], xsrc[:, 16:32, :]),
           writes=["wbx1"], sem="g_w", tag="xqb")

        def load_wslot(wsl_in, fc, q="gpsimd"):
            par = fc % 2
            em(q, lambda e, s=wsl_in[fc], d=wqv[par]: e.dma_start(d, s),
               writes=["wq%d" % par], sem="g_wq%d" % par)

        load_wslot(wk_sl, 0)
        for kfc in range(8):
            if kfc + 1 < 8:
                load_wslot(wk_sl, kfc + 1)
            par = kfc % 2
            for osc in range(2):
                pk, pa = rot(ei[0]); ei[0] += 1
                lhs = [wqv[par][:, dc, :] for dc in range(DC)]
                rhs = [xqv[:, dc, osc * 512:(osc + 1) * 512] for dc in range(DC)]
                n_ = len(lhs)
                for i_, (lh_, rh_) in enumerate(zip(lhs, rhs)):
                    em("tensor",
                       lambda e, a=lh_, b=rh_, st0=(i_ == 0),
                       sp=(i_ == n_ - 1), o=pa:
                       e.matmul(o, a, b, start=st0, stop=sp),
                       reads=["wq%d" % par, "wbx%d" % (i_ // 16)], writes=[pk])
                tk, ta = tq[ei[0] % 4]
                em("vector", lambda e, o=ta[:, 0:512], i_=pa: e.tensor_copy(o, i_),
                   reads=[pk], writes=[tk])
                dst = k_own[kfc * 128:(kfc + 1) * 128, osc * 512:(osc + 1) * 512]
                em("sync", lambda e, d=dst, s=ta[:, 0:512]: e.dma_start(d, s),
                   reads=[tk], writes=["k_own"], sem="s_k")

        for vfc in range(4):
            par = vfc % 2
            em("gpsimd", lambda e, s=wv_sl[vfc], d=wvv2[par]: e.dma_start(d, s),
               writes=["wbv%d" % par], sem="g_wv%d" % par, tag="wv%d" % vfc)
            wvv = wvv2[par]
            for tb8 in range(8):
                pk, pa0 = rot(ei[0]); ei[0] += 1
                pa = pa0[:, 0:256]
                for dc in range(DC):
                    em("tensor", lambda e, o=pa,
                       a=xqv[:, dc, tb8 * 128:(tb8 + 1) * 128],
                       b=wvv[:, dc, :], st0=(dc == 0):
                       e.matmul(o, a, b, start=st0, stop=False),
                       reads=["wbv%d" % par, "wbx%d" % (dc // 16)], writes=[pk])
                em("tensor", lambda e, o=pa,
                   b=bvb[:, vfc * 256:(vfc + 1) * 256]:
                   e.matmul(o, oner[:], b, start=False, stop=True),
                   reads=["oner", "bvb"], writes=[pk])
                tk, ta = tq[ei[0] % 4]
                em("vector", lambda e, o=ta[:, 0:256], i_=pa: e.tensor_copy(o, i_),
                   reads=[pk], writes=[tk])
                dst = v_own[vfc, tb8 * 128:(tb8 + 1) * 128, :]
                em("sync", lambda e, d=dst, s=ta[:, 0:256]:
                   e.dma_start(d, s), reads=[tk], writes=["v_own"], sem="s_v")

        groups = [[0, 1, 2, 3], [4, 5, 6, 7]]
        em("gpsimd", lambda e: e.collective_compute(
            "AllGather", ALU.bypass, replica_groups=groups,
            ins=[k_own.opt()], outs=[k_scr.opt()]),
           reads=["k_own"], writes=["k_scr"], tag="ag_k")
        em("gpsimd", lambda e: e.collective_compute(
            "AllGather", ALU.bypass, replica_groups=groups,
            ins=[v_own.opt()], outs=[v_scr.opt()]),
           reads=["v_own"], writes=["v_scr"], tag="ag_v")

        sfree("wvs0", "wvs1")
        kg = tuple(t[:] for t in kg_t)
        vg = tuple(t[:].rearrange("p (kc h) -> p kc h", kc=32) for t in vg_t)
        vgflat = tuple(t[:] for t in vg_t)
        qh = (qh_t[0][:], qh_t[1][:])

        def load_kv(g, q="sync"):
            par = g % 2
            sk_ = k_scr.rearrange("(r kf) t -> r kf t", r=4)[
                :, g * 128:(g + 1) * 128, :].rearrange("r p t -> p r t")
            em(q, lambda e, d=kg[par].rearrange("p (r t) -> p r t", r=4),
               s=sk_: e.dma_start(d, s), reads=["k_scr"],
               writes=["kg%d" % par], sem="s_kg%d" % par)

            vsrc = v_scr.rearrange("r v (kcl p) h -> r v p kcl h", p=128)
            for rr in range(4):
                em(q, lambda e,
                   d=vgflat[par][:, rr * 1024:(rr + 1) * 1024].rearrange(
                       "p (kcl h) -> p kcl h", kcl=8),
                   s=vsrc[rr, g // 2, :, :, (g % 2) * 128:(g % 2 + 1) * 128]:
                   e.dma_start(d, s),
                   reads=["v_scr"], writes=["vg%d" % par], sem="s_vg%d" % par)

        def load_qh(h):
            par = h % 2
            em("scalar", lambda e, d=qh[par], s=q_scr[h * 128:(h + 1) * 128, :]:
               e.dma_start(d, s), reads=["q_scr"],
               writes=["qh%d" % par], sem="s_qh%d" % par)

        load_kv(0, q="scalar")
        load_kv(1, q="scalar")

        load_wslot(wq_sl, 0)
        for qfc in range(32):
            if qfc + 1 < 32:
                load_wslot(wq_sl, qfc + 1)
            par = qfc % 2
            for osc in range(2):
                pk, pa = rot(ei[0]); ei[0] += 1
                lhs = [wqv[par][:, dc, :] for dc in range(DC)]
                rhs = [xqv[:, dc, osc * 512:(osc + 1) * 512] for dc in range(DC)]
                n_ = len(lhs)
                for i_, (lh_, rh_) in enumerate(zip(lhs, rhs)):
                    em("tensor",
                       lambda e, a=lh_, b=rh_, st0=(i_ == 0),
                       sp=(i_ == n_ - 1), o=pa:
                       e.matmul(o, a, b, start=st0, stop=sp),
                       reads=["wq%d" % par, "wbx%d" % (i_ // 16)], writes=[pk])
                tk, ta = tq[ei[0] % 4]
                em("vector", lambda e, o=ta[:, 0:512], i_=pa: e.tensor_copy(o, i_),
                   reads=[pk], writes=[tk])
                dst = q_scr[qfc * 128:(qfc + 1) * 128, osc * 512:(osc + 1) * 512]
                em("sync", lambda e, d=dst, s=ta[:, 0:512]: e.dma_start(d, s),
                   reads=[tk], writes=["q_scr"], sem="s_q")

        def fin():
            em("sync", lambda e: e.nop(),
               reads=["y_hbm", "k_scr", "v_scr", "q_scr"])

        if upto < 2:
            fin()
            p.lower(block, semobjs)
            sfree(*list(scoped.keys()))
            return nc

        p.recarve(["wq0", "wq1", "wbv0", "wbv1"],
                  ["pb0", "pb1", "pb2", "ilbc", "l_sb"])
        wsl = (cx[:, 0:8192].rearrange("p (dc f) -> p dc f", dc=32),
               cx[:, 8192:16384].rearrange("p (dc f) -> p dc f", dc=32))

        def load_w(obc, wT):
            par = obc % 2
            em("gpsimd", lambda e, s=wT[obc], d=cx[:, par * 8192:(par + 1) * 8192]:
               e.dma_start(d, s),
               reads=["wo_bf"], writes=["ws%d" % par], sem="g_w%d" % par)
        p.recarve(["wbx0", "wbx1"], ["w%d_%d" % (i_, j_) for i_ in range(32)
                             for j_ in range(2)])
        p.recarve(["psA0", "psA1"], ["sc0", "sc1"])
        p.recarve(["psB0", "psB1"], ["invl", "lps"])
        sfree("wq0", "wq1")
        pb_t = (salloc("pb0", [128, 512], BF16), salloc("pb1", [128, 512], BF16),
                salloc("pb2", [128, 512], BF16))
        ilbc = salloc("ilbc", [128, 1024], F32)[:]
        l_sb = salloc("l_sb", [1, 1024], F32)[:]
        il_sb = l_sb
        psC0 = palloc("psC0", [128, 1024])
        psC1 = palloc("psC1", [128, 1024])
        pb = (pb_t[0][:], pb_t[1][:], pb_t[2][:])
        scv = (psA[:, 0:512], psA[:, 512:1024])
        invlps = psB[:, 0:512]
        lps = psB[:, 512:1024]
        ctxps = (psC0, psC1)
        ctxv = wview

        def emit_pv_l(gp, hp, t):
            kc, half = t // 2, t % 2
            pk = "pb%d" % (t % 3)
            pbh = pb[t % 3]
            cp = ctxps[hp]
            em("tensor", lambda e, o=cp[:, half * 512:(half + 1) * 512],
               a=vg[gp][:, kc, :], b=pbh,
               st0=(t < 2), sp=(t >= 62):
               e.matmul(o, a, b, start=st0, stop=sp),
               reads=["vg%d" % gp, pk], writes=["psC%d" % hp])
            lrow = lps[0:1, :] if half == 0 else lps[32:33, :]
            em("tensor", lambda e, o=lrow, b=pbh,
               st0=(t < 2), sp=(t >= 62):
               e.matmul(o, onec[:], b, start=st0, stop=sp,
                        skip_group_check=True),
               reads=[pk, "onec"], writes=["lps"])

        def make_finalizer(h):
            hp2 = h % 2

            def fin_head_a():
                em("vector", lambda e: e.tensor_copy(l_sb[:, 0:512],
                                                     lps[0:1, :]),
                   reads=["lps"], writes=["l_sb"])
                em("vector", lambda e: e.tensor_copy(l_sb[:, 512:1024],
                                                     lps[32:33, :]),
                   reads=["lps"], writes=["l_sb"])
                em("scalar", lambda e: e.activation(l_sb, l_sb, AF.Ln),
                   reads=["l_sb"], writes=["l_sb"])
                em("scalar", lambda e: e.activation(l_sb, l_sb, AF.Exp,
                                                    scale=-1.0),
                   reads=["l_sb"], writes=["l_sb"])

            def fin_head_b():
                for half in range(2):
                    em("tensor", lambda e,
                       b=il_sb[:, half * 512:(half + 1) * 512]:
                       e.matmul(invlps, onerf[:], b, start=True, stop=True),
                       reads=["l_sb", "onerf"], writes=["invl"])
                    em("vector", lambda e,
                       o=ilbc[:, half * 512:(half + 1) * 512]:
                       e.tensor_copy(o, invlps), reads=["invl"],
                       writes=["ilbc"])
                em("vector", lambda e, o=ctxv[:, h, :], a=ctxps[hp2]:
                   e.tensor_tensor(o, a[:, :], ilbc, ALU.mult),
                   reads=["psC%d" % hp2, "ilbc"],
                   writes=["w%d_0" % h, "w%d_1" % h])
            return fin_head_a, fin_head_b

        pc_jobs = []
        tbv = tbuf[:].rearrange("p (hf dc f) -> p hf dc f", hf=2, dc=8)
        for wsrc, wdst in ((wo_sl, wo_bf), (bw_sl, bw_bf)):
            for j in range(16):
                for q_ in range(4):
                    s_ = wsrc[j, :, q_ * 8:(q_ + 1) * 8, :]
                    d_ = wdst[j][:, q_ * 2048:(q_ + 1) * 2048]
                    t_ = tbv[:, q_ % 2]
                    tk_ = "tbh%d" % (q_ % 2)
                    pc_jobs.append((s_, d_, t_, tk_))
        p.recarve(["tb0", "tb1", "tb2", "tb3"], ["tbh0", "tbh1"])
        pc_i = [0]

        def pump_precast(njobs):
            for _ in range(njobs):
                if pc_i[0] >= len(pc_jobs):
                    return
                s_, d_, t_, tk_ = pc_jobs[pc_i[0]]
                pc_i[0] += 1
                em("gpsimd", lambda e, a=t_, b=s_: e.dma_start(a, b),
                   writes=[tk_], sem="g_pc")
                em("gpsimd", lambda e, a=d_, b=t_: e.dma_start(a, b),
                   reads=[tk_], writes=["wo_bf"], sem="g_ps")

        load_qh(0)
        pend_b = None
        for g in range(NKV):
            gp = g % 2
            if g + 1 < NKV and g >= 1:
                load_kv(g + 1)
            for q4 in range(4):
                h = 4 * g + q4
                hp = h % 2
                if h + 1 < NQ:
                    load_qh(h + 1)
                pump_precast(4)
                kkey, qkey = "kg%d" % gp, "qh%d" % hp
                for t in range(64):
                    kc, half = t // 2, t % 2
                    em("tensor", lambda e, o=scv[t % 2],
                       a=kg[gp][:, kc * 128:(kc + 1) * 128],
                       b=qh[hp][:, half * 512:(half + 1) * 512]:
                       e.matmul(o, a, b, start=True, stop=True),
                       reads=[kkey, qkey], writes=["sc%d" % (t % 2)])
                    em("scalar", lambda e, o=pb[t % 3], i_=scv[t % 2]:
                       e.activation(o, i_, AF.Exp, scale=QSCALE),
                       reads=["sc%d" % (t % 2)], writes=["pb%d" % (t % 3)])
                    if t == 6 and pend_b is not None:
                        pend_b()
                        pend_b = None
                    if t >= 2:
                        emit_pv_l(gp, hp, t - 2)
                emit_pv_l(gp, hp, 62)
                emit_pv_l(gp, hp, 63)
                fa, fb = make_finalizer(h)
                fa()
                pend_b = fb
        pend_b()
        pump_precast(len(pc_jobs))
        if dbg_scratch:
            em("sync", lambda e: e.dma_start(
                ctx_dbg.rearrange("(h p) t -> p h t", p=128), ctxv),
               reads=["w%d_%d" % (i_, j_) for i_ in range(32) for j_ in range(2)],
               writes=["ctx_dbg"], sem="s_c")
        load_w(0, wo_bf)

        if upto < 3:
            fin()
            p.lower(block, semobjs)
            sfree(*list(scoped.keys()))
            return nc

        p.recarve(["kg0", "kg1", "vg0", "vg1", "qh0", "qh1",
                   "pb0", "pb1", "pb2", "ilbc", "l_sb"],
                  ["yb0", "yb1", "yb2", "yb3"])
        p.recarve(["tbh0", "tbh1"], ["tb0", "tb1", "tb2", "tb3"])
        p.recarve(["sc0", "sc1"], ["psA0", "psA1"])
        p.recarve(["invl", "lps"], ["psB0", "psB1"])
        sfree("kg0", "kg1", "vg0", "vg1", "qh0", "qh1",
              "pb0", "pb1", "pb2", "ilbc", "l_sb", "psC0", "psC1")
        ybig = salloc("ybig", [128, 16384], F32)
        acat = salloc("acat", [128, 4096], BF16)
        bcat_s = salloc("bcat_s", [128, 4096], BF16)
        bbb = salloc("bbb", [1, 4096], BF16)
        em("gpsimd", lambda e: e.dma_start(
            acat[:].rearrange("p (dc r) -> p dc r", dc=32), acat_sl),
           writes=["acat"], sem="g_c")
        em("gpsimd", lambda e: e.dma_start(bcat_s[:], bcat),
           writes=["bcat_s"], sem="g_c")
        em("gpsimd", lambda e: e.dma_start(bbb[:], bb), writes=["bbb"],
           sem="g_c")
        psDt = palloc("psDt", [64, 1024], BF16)
        psE = palloc("psE", [128, 512])
        ybv = ybig[:].rearrange("p (tb f) -> p tb f", tb=4)
        yblk = [ybv[:, i, :] for i in range(4)]
        wres = wview
        rwv = rwb[:].rearrange("p (dc r) -> p dc r", dc=32)
        acv = acat[:].rearrange("p (dc r) -> p dc r", dc=32)

        def wkey(i, tc):
            return "w%d_%d" % (i, tc)

        def emit_bn(tb, i):
            yb = yblk[tb]
            stv = stats[:, 48 * tb:48 * (tb + 1)]
            em("vector", lambda e, o=stv[:, 6 * i:6 * i + 6],
               a=yb[:, 512 * i:512 * (i + 1)]:
               e.bn_stats(o, a), reads=["yb%d" % tb],
               writes=["st%d_%d" % (tb, i)])

        def ln_pass(tb, stats_done=False):
            ybk = "yb%d" % tb
            yb = yblk[tb]
            stv = stats[:, 48 * tb:48 * (tb + 1)]
            mvv = mv[:, 2 * tb:2 * (tb + 1)]
            s1v, s2v, s3v = (s1[:, tb:tb + 1], s2[:, tb:tb + 1],
                             s3[:, tb:tb + 1])
            if not stats_done:
                for i in range(8):
                    emit_bn(tb, i)
            em("vector", lambda e, o=mvv, a=stv: e.bn_aggr(o, a),
               reads=["st%d_%d" % (tb, i2) for i2 in range(8)],
               writes=["mv%d" % tb])
            em("scalar", lambda e, o=s1v, a=mvv[:, 1:2]:
               e.activation(o, a, AF.Ln, bias=epsb[:]),
               reads=["mv%d" % tb, "epsb"], writes=["s1%d" % tb])
            em("scalar", lambda e, o=s2v, a=s1v:
               e.activation(o, a, AF.Exp, scale=-0.5),
               reads=["s1%d" % tb], writes=["s2%d" % tb])
            em("vector", lambda e, o=s3v, a=mvv[:, 0:1], b=s2v:
               e.tensor_tensor(o, a, b, ALU.mult),
               reads=["mv%d" % tb, "s2%d" % tb], writes=["s3%d" % tb])
            em("vector", lambda e, o=yb, a=s2v, b=s3v: e.tensor_scalar(
                o, o, a, b, ALU.mult, op1=ALU.subtract),
               reads=[ybk, "s2%d" % tb, "s3%d" % tb], writes=[ybk])

        def mm_pass(tc, wbf, lora):
            for obc in range(16):
                if obc + 1 < 16:
                    load_w(obc + 1, wbf)
                par = obc % 2
                for tb in range(4):
                    pk, pa0 = rot(ei[0]); ei[0] += 1
                    pa = pa0[:, 0:256]
                    for dc in range(DC):
                        em("tensor", lambda e, o=pa,
                           a=wres[:, dc, tc * 512 + tb * 128:
                                  tc * 512 + (tb + 1) * 128],
                           b=wsl[par][:, dc, :], st0=(dc == 0),
                           sp=(dc == 31 and not lora):
                           e.matmul(o, a, b, start=st0, stop=sp),
                           reads=[wkey(dc, tc), "ws%d" % par], writes=[pk])
                    if lora:
                        em("tensor", lambda e, o=pa,
                           b=bbb[:, obc * 256:(obc + 1) * 256]:
                           e.matmul(o, oner[:], b, start=False, stop=False),
                           reads=["oner", "bbb"], writes=[pk])
                        em("tensor", lambda e, o=pa,
                           a=hwb[:, tb * 128:(tb + 1) * 128],
                           b=bcat_s[:, obc * 256:(obc + 1) * 256]:
                           e.matmul(o, a, b, start=False, stop=True),
                           reads=["hwb", "bcat_s"], writes=[pk])
                    em("vector", lambda e,
                       o=yblk[tb][:, obc * 256:(obc + 1) * 256],
                       i_=pa: e.tensor_tensor(o, o, i_, ALU.add),
                       reads=[pk, "yb%d" % tb], writes=["yb%d" % tb])
                    if obc % 2 == 1:
                        emit_bn(tb, obc // 2)

        for tc in range(2):
            for tb in range(4):
                trow = tc * 512 + tb * 128
                em("sync", lambda e, d=yblk[tb], s=xo[trow:trow + 128, :]:
                   e.dma_start(d, s), writes=["yb%d" % tb], sem="s_res")
            mm_pass(tc, wo_bf, lora=False)
            load_w(0, bw_bf)

            psCt = palloc("psCt", [128, 2048], BF16)
            rps = []
            for tb in range(4):
                ln_pass(tb, stats_done=True)
                em("vector", lambda e, i_=yblk[tb]: e.tensor_copy(tbuf[:], i_),
                   reads=["yb%d" % tb], writes=["tb0", "tb1", "tb2", "tb3"])
                for dg in range(8):
                    for j in range(4):
                        dc = dg * 4 + j
                        em("tensor", lambda e,
                           o=psCt[:, j * 128:(j + 1) * 128],
                           i_=tbuf[:, dc * 128:(dc + 1) * 128]:
                           e.transpose(o, i_, idb[:]),
                           reads=["tb0", "tb1", "tb2", "tb3", "idb"],
                           writes=["psC"])
                    em("vector", lambda e,
                       o=wres[:, dg * 4:dg * 4 + 4,
                              tc * 512 + tb * 128:tc * 512 + (tb + 1) * 128],
                       i_=psCt[:, 0:512].rearrange("p (j t) -> p j t", j=4):
                       e.tensor_copy(o, i_), reads=["psC"],
                       writes=[wkey(dg * 4 + j2, tc) for j2 in range(4)])
                pk, pa = rot(ei[0]); ei[0] += 1
                rps.append((pk, pa))
                em("tensor", lambda e, o=pa[:, 0:16]:
                   e.matmul(o, oner[:], rbb[:], start=True, stop=False),
                   reads=["oner", "rbb"], writes=[pk])
                for dc in range(DC):
                    em("tensor", lambda e, o=pa[:, 0:16],
                       a=wres[:, dc, tc * 512 + tb * 128:
                              tc * 512 + (tb + 1) * 128],
                       b=rwv[:, dc, :], sp=(dc == 31):
                       e.matmul(o, a, b, start=False, stop=sp),
                       reads=[wkey(dc, tc), "rwb"], writes=[pk])
            sfree("psCt")

            for tb in range(4):
                pk, pa = rps[tb]
                lgv = lg[:, 0:15]
                prv = pr[:, 0:15]
                s1v, s2v, s3v = (s1[:, tb:tb + 1], s2[:, tb:tb + 1],
                                 s3[:, tb:tb + 1])
                em("vector", lambda e, a=pa[:, 0:16], o=lg[:]:
                   e.tensor_copy(o, a), reads=[pk], writes=["lg"])
                em("vector", lambda e, o=s1v: e.tensor_reduce(
                    o, lgv, AX.X, ALU.max, negate=True),
                   reads=["lg"], writes=["s1%d" % tb])
                em("scalar", lambda e, o=prv, b=s1v, ac=s2v: e.activation(
                    o, lgv, AF.Exp, bias=b, accum_out=ac),
                   reads=["lg", "s1%d" % tb], writes=["pr", "s2%d" % tb])
                em("vector", lambda e, o=s3v, a=s2v: e.reciprocal(o, a),
                   reads=["s2%d" % tb], writes=["s3%d" % tb])
                em("vector", lambda e, o=prv, a=s3v: e.tensor_scalar(
                    o, o, a, None, ALU.mult),
                   reads=["pr", "s3%d" % tb], writes=["pr"])
                for n in range(15):
                    em("vector", lambda e, nn_=n: e.tensor_scalar(
                        j15[:, 0:15], pr[:, 0:15], pr[:, nn_:nn_ + 1], None,
                        ALU.is_gt, op1=ALU.add,
                        accum_out=cnts[:, nn_:nn_ + 1]),
                       reads=["pr"], writes=["j15", "c%d" % n])
                em("vector", lambda e: e.tensor_scalar(
                    mskb[:, 0:15], cnts[:, 0:15], 8.0, None, ALU.is_lt),
                   reads=["c%d" % n2 for n2 in range(15)], writes=["mskb"])
                em("vector", lambda e: e.tensor_tensor(
                    pmb[:, 0:15], pr[:, 0:15], mskb[:, 0:15], ALU.mult),
                   reads=["pr", "mskb"], writes=["pmb"])
                em("tensor", lambda e: e.transpose(
                    psDt[0:15, 0:128], pmb[:, 0:15], idb[:]),
                   reads=["pmb", "idb"], writes=["psD"])
                em("vector", lambda e, o=pmT[:, tb * 128:(tb + 1) * 128]:
                   e.tensor_copy(o, psDt[0:15, 0:128]),
                   reads=["psD"], writes=["pmT"])
            em("tensor", lambda e: e.matmul(psE[0:8, 0:512], mmapb[:], pmT[:],
                                            start=True, stop=True),
               reads=["mmapb", "pmT"], writes=["psE"])
            em("vector", lambda e: e.tensor_copy(wtb[:], psE[0:8, 0:512]),
               reads=["psE"], writes=["wtb"])
            em("tensor", lambda e: e.matmul(psE[:], e16b[:], wtb[:],
                                            start=True, stop=True),
               reads=["e16b", "wtb"], writes=["psE"])
            em("vector", lambda e: e.tensor_copy(wbcb[:], psE[:]),
               reads=["psE"], writes=["wbcb"])
            psCh = palloc("psCh", [128, 512])
            for dc in range(DC):
                em("tensor", lambda e, a=acv[:, dc, :],
                   b=wres[:, dc, tc * 512:(tc + 1) * 512],
                   st0=(dc == 0), sp=(dc == 31):
                   e.matmul(psCh[:], a, b, start=st0, stop=sp),
                   reads=["acat", wkey(dc, tc)], writes=["psC"])
            em("vector", lambda e: e.tensor_tensor(hwb[:], psCh[:],
                                                   wbcb[:], ALU.mult),
               reads=["psC", "wbcb"], writes=["hwb"])
            sfree("psCh")
            mm_pass(tc, bw_bf, lora=True)
            if tc == 0:
                load_w(0, wo_bf)
            for tb in range(4):
                ln_pass(tb, stats_done=True)
                trow = tc * 512 + tb * 128
                em("sync", lambda e, d=y[trow:trow + 128, :], s=yblk[tb]:
                   e.dma_start(d, s), reads=["yb%d" % tb], writes=["y_hbm"],
                   sem="s_y")

        fin()
        p.lower(block, semobjs)
        sfree(*list(scoped.keys()))
    return nc


def _host_inputs(inp):
    x = np.ascontiguousarray(np.asarray(inp["L0_attn_hidden_states"],
                                        dtype=np.float32))
    c = np.ascontiguousarray

    def slot_img(w, nf):
        O = w.shape[0]
        return c(w.reshape(O // nf, nf, 32, 128).transpose(0, 3, 2, 1))

    wq_sl = slot_img(np.asarray(inp["L0_attn_Wq"], np.float32), 128)
    wk_sl = slot_img(np.asarray(inp["L0_attn_Wk"], np.float32), 128)
    wv_sl = slot_img(np.asarray(inp["L0_attn_Wv"], np.float32), 256)
    wo_sl = slot_img(np.asarray(inp["L0_attn_Wo"], np.float32), 256)
    bw_sl = slot_img(np.asarray(inp["L0_ffn_base_W"], np.float32), 256)
    bv = np.asarray(inp["L0_attn_bv"], np.float32).reshape(1, 1024)
    bb = np.asarray(inp["L0_ffn_base_b"], np.float32).reshape(1, D)
    rw = np.asarray(inp["L0_ffn_router_W"], np.float32)
    rwp = np.zeros((16, D), np.float32)
    rwp[0:15] = rw
    rw_sl = slot_img(rwp, 16)[0]
    rb = np.zeros((1, 16), np.float32)
    rb[0, 0:15] = np.asarray(inp["L0_ffn_router_b"], np.float32)
    A = np.asarray(inp["L0_ffn_A"], np.float32)
    acat_sl = slot_img(A.reshape(128, D), 128)[0]
    Bm = np.asarray(inp["L0_ffn_B"], np.float32)
    bcat = c(Bm.transpose(0, 2, 1).reshape(128, D))
    emap = np.asarray(inp["L0_ffn_expert_map"], np.int32).reshape(15, 1)
    iota8 = np.broadcast_to(np.arange(8, dtype=np.float32), (15, 8)).copy()
    e16 = np.zeros((8, 128), np.float32)
    for e_ in range(8):
        e16[e_, e_ * 16:(e_ + 1) * 16] = LORA_SCALE
    idbf = np.eye(128).astype(ml_dtypes.bfloat16)

    shared = dict(wq_sl=wq_sl, wk_sl=wk_sl, wv_sl=wv_sl, wo_sl=wo_sl,
                  bw_sl=bw_sl, bv=bv, bb=bb, rw_sl=rw_sl, rb=rb,
                  acat_sl=acat_sl, bcat=bcat, emap=emap,
                  iota8=iota8, e16=e16, idbf=idbf)
    xT = [c(x[b_].T) for b_ in range(B)]
    in_maps = []
    for core in range(NCORE):
        b_ = core // 4
        q0 = (core % 4) * TPC
        m = dict(shared)
        m["xoT"] = c(xT[b_][:, q0:q0 + TPC])
        m["xo"] = c(x[b_, q0:q0 + TPC, :])
        in_maps.append(m)
    return in_maps


def kernel(**inputs):
    in_maps = _host_inputs(inputs)
    nc = build_nc()
    res = run_bass_kernel_spmd(nc, in_maps, core_ids=list(range(NCORE)))
    out = np.empty((B, S, D), np.float32)
    for core in range(NCORE):
        b_ = core // 4
        q0 = (core % 4) * TPC
        out[b_, q0:q0 + TPC, :] = np.asarray(res.results[core]["y"],
                                             dtype=np.float32)
    return out


# version 18
# speedup vs baseline: 1.1985x; 1.0844x over previous
import numpy as np
import ml_dtypes
import concourse.bass as bass
import concourse.mybir as mybir
from concourse.bass_utils import run_bass_kernel_spmd

F32 = mybir.dt.float32
BF16 = mybir.dt.bfloat16
I32 = mybir.dt.int32
AF = mybir.ActivationFunctionType
ALU = mybir.AluOpType
AX = mybir.AxisListType

B, S, D = 2, 4096, 4096
NKV, NQ, HD = 8, 32, 128
NCORE = 8
TPC = 1024
DC = D // 128
QSCALE = float(HD) ** -0.5
EPS = 1e-5
LORA_SCALE = 2.0

_STREAMS = ("tensor", "scalar", "vector", "gpsimd", "sync")


class _Op:
    __slots__ = ("idx", "stream", "pos", "fn", "sem", "val", "waits", "inc",
                 "snap", "tag")


class Prog:

    def __init__(self):
        self.ops = []
        self.pos = {s: 0 for s in _STREAMS}
        self.res = {}
        self.clock = {s: {} for s in _STREAMS}
        self.dmacnt = {}

    def emit(self, stream, fn, reads=(), writes=(), sem=None, tag=""):
        op = _Op()
        op.idx = len(self.ops)
        op.stream = stream
        op.fn = fn
        op.sem = sem
        op.tag = tag
        op.inc = False
        op.pos = self.pos[stream]
        self.pos[stream] += 1
        deps = {}
        for k in reads:
            r = self.res.get(k)
            if r and r[0] is not None:
                deps[r[0].idx] = r[0]
        for k in writes:
            r = self.res.get(k)
            if r:
                if r[0] is not None:
                    deps[r[0].idx] = r[0]
                for rd in r[1]:
                    deps[rd.idx] = rd
        clk = self.clock[stream]
        waits = {}
        for d in deps.values():
            if d.stream == stream and d.sem is None and stream == "tensor":
                continue
            if sem is not None and d.sem == sem:
                continue
            if d.sem is None:
                ck, onum = "s:" + d.stream, d.pos
            else:
                ck, onum = "d:" + d.sem, d.val
            if clk.get(ck, -1) >= onum:
                continue
            if d.sem is not None:
                onum = self.dmacnt[d.sem]
            d.inc = True
            prev = waits.get(ck)
            if prev is None or prev[1] < onum:
                waits[ck] = (d, onum)
            for k2, v2 in d.snap.items():
                if clk.get(k2, -1) < v2:
                    clk[k2] = v2
            clk[ck] = max(clk.get(ck, -1), onum)
        op.waits = list(waits.items())
        if sem is not None:
            self.dmacnt[sem] = self.dmacnt.get(sem, 0) + 16
            op.val = self.dmacnt[sem]
        snap = dict(clk)
        ck0 = ("s:" + stream) if sem is None else ("d:" + sem)
        o0 = op.pos if sem is None else op.val
        snap[ck0] = max(snap.get(ck0, -1), o0)
        op.snap = snap
        for k in reads:
            self.res.setdefault(k, [None, []])[1].append(op)
        for k in writes:
            e = self.res.setdefault(k, [None, []])
            e[0] = op
            e[1] = []
        self.ops.append(op)
        return op

    def recarve(self, old_keys, new_keys):
        pend = []
        for k in old_keys:
            r = self.res.pop(k, None)
            if r:
                if r[0] is not None:
                    pend.append(r[0])
                pend.extend(r[1])
        for k in new_keys:
            self.res[k] = [None, list(pend)]

    def lower(self, block, semobjs):
        cnt = {s: 0 for s in _STREAMS}
        val = {}
        for op in self.ops:
            if op.sem is None and op.inc:
                cnt[op.stream] += 1
                val[op.idx] = cnt[op.stream]
        streams = {s: [] for s in _STREAMS}
        for op in self.ops:
            streams[op.stream].append(op)

        def mk(ops):
            def body(e):
                for op in ops:
                    for ck, (d, onum) in op.waits:
                        if d.sem is None:
                            e.wait_ge(semobjs["s:" + d.stream], val[d.idx])
                        else:
                            e.wait_ge(semobjs["d:" + d.sem], onum)
                    last = op.fn(e)
                    if op.sem is not None:
                        last.then_inc(semobjs["d:" + op.sem], 16)
                    elif op.inc:
                        last.then_inc(semobjs["s:" + op.stream], 1)
            return body

        for s in _STREAMS:
            if streams[s]:
                getattr(block, s)(mk(streams[s]))


def build_nc(dbg_scratch=False, upto=4):
    nc = bass.Bass("TRN2", target_bir_lowering=False)
    p = Prog()

    def din(name, shape, dt=F32):
        return nc.dram_tensor(name, shape, dt, kind="ExternalInput").ap()

    xoT = din("xoT", [D, TPC])
    xo = din("xo", [TPC, D])
    wq_sl = din("wq_sl", [32, 128, 32, 128])
    wk_sl = din("wk_sl", [8, 128, 32, 128])
    wv_sl = din("wv_sl", [4, 128, 32, 256])
    wo_sl = din("wo_sl", [16, 128, 32, 256])
    bw_sl = din("bw_sl", [16, 128, 32, 256])
    bv = din("bv", [1, 1024])
    bb = din("bb", [1, D])
    rw_sl = din("rw_sl", [128, 32, 16])
    rb = din("rb", [1, 16])
    acat_sl = din("acat_sl", [128, 32, 128])
    bcat = din("bcat", [128, D])
    emap = din("emap", [15, 1], I32)
    iota8 = din("iota8", [15, 8])
    e16 = din("e16", [8, 128])
    idbf = din("idbf", [128, 128], BF16)

    y = nc.dram_tensor("y", [TPC, D], F32, kind="ExternalOutput").ap()

    skind = {"kind": "ExternalOutput"} if dbg_scratch else {}
    if dbg_scratch:
        ctx_dbg = nc.dram_tensor("ctx_dbg", [D, TPC], BF16,
                                 kind="ExternalOutput").ap()
    k_own = nc.dram_tensor("k_own", [1024, TPC], BF16).ap()
    v_own = nc.dram_tensor("v_own", [4, TPC, 256], BF16).ap()
    k_scr = nc.dram_tensor("k_scr", [4 * 1024, TPC], BF16, **skind).ap()
    v_scr = nc.dram_tensor("v_scr", [4, 4, TPC, 256], BF16, **skind).ap()
    q_scr = nc.dram_tensor("q_scr", [D, TPC], BF16, **skind).ap()
    wo_bf = nc.dram_tensor("wo_bf", [16, 128, 8192], BF16).ap()
    bw_bf = nc.dram_tensor("bw_bf", [16, 128, 8192], BF16).ap()

    from contextlib import ExitStack
    with ExitStack() as st:
        def sb(name, shape, dt):
            return st.enter_context(nc.sbuf_tensor(name, shape, dt))

        def psalloc(name, shape, dt=F32):
            return st.enter_context(nc.psum_tensor(name, shape, dt))

        wbig = sb("wbig", [128, 32768], BF16)
        tbuf = sb("tbuf", [128, 4096], BF16)
        cx = sb("cx", [128, 16384], BF16)
        scoped = {}
        scoped_seq = [0]

        def salloc(name, shape, dt):
            g = nc.sbuf_tensor("%s_s%d" % (name, scoped_seq[0]), shape, dt)
            scoped[name] = (scoped_seq[0], g)
            scoped_seq[0] += 1
            return g.__enter__()

        def sfree(*names):
            for n in sorted(names, key=lambda n2: -scoped[n2][0]):
                scoped.pop(n)[1].__exit__(None, None, None)

        def palloc(name, shape, dt=F32):
            g = nc.psum_tensor("%s_s%d" % (name, scoped_seq[0]), shape, dt)
            scoped[name] = (scoped_seq[0], g)
            scoped_seq[0] += 1
            return g.__enter__()
        rwb = sb("rwb", [128, 512], BF16)
        idb = sb("idb", [128, 128], BF16)
        hwb = sb("hwb", [128, 512], BF16)
        wbcb = sb("wbcb", [128, 512], BF16)
        bvb = sb("bvb", [1, 1024], BF16)
        rbb = sb("rbb", [1, 16], BF16)
        e16b = sb("e16b", [8, 128], BF16)
        mmapb = sb("mmapb", [15, 8], BF16)
        mmapf = sb("mmapf", [15, 8], F32)
        iot = sb("iot", [15, 8], F32)
        emi = sb("emi", [15, 1], I32)
        emf = sb("emf", [15, 1], F32)
        onec = sb("onec", [128, 1], BF16)
        oner = sb("oner", [1, 128], BF16)
        onerf = sb("onerf", [1, 128], F32)
        epsb = sb("epsb", [128, 1], F32)
        lg = sb("lg", [128, 16], F32)
        pr = sb("pr", [128, 16], F32)
        j15 = sb("j15", [128, 16], F32)
        cnts = sb("cnts", [128, 16], F32)
        mskb = sb("mskb", [128, 16], F32)
        pmb = sb("pmb", [128, 16], BF16)
        pmT = sb("pmT", [15, 512], BF16)
        wtb = sb("wtb", [8, 512], BF16)
        stats = sb("stats", [128, 4 * 48], F32)
        mv = sb("mv", [128, 4 * 2], F32)
        s1 = sb("s1", [128, 4], F32)
        s2 = sb("s2", [128, 4], F32)
        s3 = sb("s3", [128, 4], F32)

        psA = psalloc("psA", [128, 1024])
        psB = psalloc("psB", [128, 1024])

        sem_names = (["s:" + s for s in _STREAMS] +
                     ["d:" + n for n in
                      ("g_w", "g_x0", "g_x1", "g_wq0", "g_wq1", "g_c",
                       "s_c", "s_k", "s_v", "s_q", "s_ctx", "s_y",
                       "s_kg0", "s_kg1", "s_kg2", "s_vg0", "s_vg1", "s_vg2", "s_qh0", "s_qh1",
                       "s_cx", "s_res", "g_w0", "g_w1", "g_pc", "g_ps", "g_wv0", "g_wv1")])
        semobjs = {n: st.enter_context(nc.semaphore(n.replace(":", "_")))
                   for n in sem_names}
        block = st.enter_context(nc.Block())

        em = p.emit

        em("sync", lambda e: e.dma_start(idb[:], idbf), writes=["idb"], sem="s_c")
        em("sync", lambda e: e.dma_start(iot[:], iota8), writes=["iot"], sem="s_c")
        em("sync", lambda e: e.dma_start(emi[:], emap), writes=["emi"], sem="s_c")
        em("gpsimd", lambda e: e.dma_start(bvb[:], bv), writes=["bvb"], sem="g_c")
        em("gpsimd", lambda e: e.dma_start(rbb[:], rb), writes=["rbb"], sem="g_c")
        em("gpsimd", lambda e: e.dma_start(e16b[:], e16), writes=["e16b"], sem="g_c")
        em("gpsimd", lambda e: e.dma_start(
            rwb[:].rearrange("p (dc r) -> p dc r", dc=32), rw_sl),
           writes=["rwb"], sem="g_c")
        em("vector", lambda e: e.memset(onec[:], 1.0), writes=["onec"])
        em("vector", lambda e: e.memset(oner[:], 1.0), writes=["oner"])
        em("vector", lambda e: e.memset(onerf[:], 1.0), writes=["onerf"])
        em("vector", lambda e: e.memset(epsb[:], EPS), writes=["epsb"])
        em("vector", lambda e: e.tensor_copy(emf[:], emi[:]),
           reads=["emi"], writes=["emf"])
        em("vector", lambda e: e.tensor_scalar(
            mmapf[:], iot[:], emf[:], None, ALU.is_equal),
           reads=["iot", "emf"], writes=["mmapf"])
        em("vector", lambda e: e.tensor_copy(mmapb[:], mmapf[:]),
           reads=["mmapf"], writes=["mmapb"])

        wview = wbig[:].rearrange("p (dc f) -> p dc f", f=1024)

        psum_rot = [("psA0", psA[:, 0:512]), ("psA1", psA[:, 512:1024]),
                    ("psB0", psB[:, 0:512]), ("psB1", psB[:, 512:1024])]

        def rot(i):
            return psum_rot[i % 4]

        tq = [("tb0", tbuf[:, 0:1024]), ("tb1", tbuf[:, 1024:2048]),
              ("tb2", tbuf[:, 2048:3072]), ("tb3", tbuf[:, 3072:4096])]

        ei = [0]

        def mm_group(ps_key, ps_ap, lhs_list, rhs_list, reads):
            n = len(lhs_list)
            for i, (lh, rh) in enumerate(zip(lhs_list, rhs_list)):
                em("tensor",
                   lambda e, a=lh, b=rh, st0=(i == 0), sp=(i == n - 1), o=ps_ap:
                   e.matmul(o, a, b, start=st0, stop=sp),
                   reads=list(reads), writes=[ps_key])

        p.recarve(["wbig"], ["wbx0", "wbx1"])
        kg_t = (salloc("kg0", [128, 4096], BF16), salloc("kg1", [128, 4096], BF16))
        vg_t = (salloc("vg0", [128, 4096], BF16), salloc("vg1", [128, 4096], BF16))
        qh_t = (salloc("qh0", [128, 1024], BF16), salloc("qh1", [128, 1024], BF16))
        wq0 = salloc("wq0", [128, 4096], BF16)
        wq1 = salloc("wq1", [128, 4096], BF16)
        wvs0 = salloc("wvs0", [128, 8192], BF16)
        wvs1 = salloc("wvs1", [128, 8192], BF16)
        wqv = (wq0[:].rearrange("p (dc f) -> p dc f", dc=32),
               wq1[:].rearrange("p (dc f) -> p dc f", dc=32))
        xqv = wbig[:].rearrange("p (dc t) -> p dc t", dc=32)
        wvv2 = (wvs0[:].rearrange("p (dc f) -> p dc f", dc=32),
                wvs1[:].rearrange("p (dc f) -> p dc f", dc=32))
        xsrc = xoT.rearrange("(dc p) t -> p dc t", p=128)
        em("gpsimd", lambda e: e.dma_start(xqv[:, 0:16, :], xsrc[:, 0:16, :]),
           writes=["wbx0"], sem="g_w", tag="xqa")
        em("gpsimd", lambda e: e.dma_start(xqv[:, 16:32, :], xsrc[:, 16:32, :]),
           writes=["wbx1"], sem="g_w", tag="xqb")

        def load_wslot(wsl_in, fc, q="gpsimd"):
            par = fc % 2
            em(q, lambda e, s=wsl_in[fc], d=wqv[par]: e.dma_start(d, s),
               writes=["wq%d" % par], sem="g_wq%d" % par)

        load_wslot(wk_sl, 0)
        for kfc in range(8):
            if kfc + 1 < 8:
                load_wslot(wk_sl, kfc + 1)
            par = kfc % 2
            for osc in range(2):
                pk, pa = rot(ei[0]); ei[0] += 1
                lhs = [wqv[par][:, dc, :] for dc in range(DC)]
                rhs = [xqv[:, dc, osc * 512:(osc + 1) * 512] for dc in range(DC)]
                n_ = len(lhs)
                for i_, (lh_, rh_) in enumerate(zip(lhs, rhs)):
                    em("tensor",
                       lambda e, a=lh_, b=rh_, st0=(i_ == 0),
                       sp=(i_ == n_ - 1), o=pa:
                       e.matmul(o, a, b, start=st0, stop=sp),
                       reads=["wq%d" % par, "wbx%d" % (i_ // 16)], writes=[pk])
                tk, ta = tq[ei[0] % 4]
                em("vector", lambda e, o=ta[:, 0:512], i_=pa: e.tensor_copy(o, i_),
                   reads=[pk], writes=[tk])
                dst = k_own[kfc * 128:(kfc + 1) * 128, osc * 512:(osc + 1) * 512]
                em("sync", lambda e, d=dst, s=ta[:, 0:512]: e.dma_start(d, s),
                   reads=[tk], writes=["k_own"], sem="s_k")

        for vfc in range(4):
            par = vfc % 2
            em("gpsimd", lambda e, s=wv_sl[vfc], d=wvv2[par]: e.dma_start(d, s),
               writes=["wbv%d" % par], sem="g_wv%d" % par, tag="wv%d" % vfc)
            wvv = wvv2[par]
            for tb8 in range(8):
                pk, pa0 = rot(ei[0]); ei[0] += 1
                pa = pa0[:, 0:256]
                for dc in range(DC):
                    em("tensor", lambda e, o=pa,
                       a=xqv[:, dc, tb8 * 128:(tb8 + 1) * 128],
                       b=wvv[:, dc, :], st0=(dc == 0):
                       e.matmul(o, a, b, start=st0, stop=False),
                       reads=["wbv%d" % par, "wbx%d" % (dc // 16)], writes=[pk])
                em("tensor", lambda e, o=pa,
                   b=bvb[:, vfc * 256:(vfc + 1) * 256]:
                   e.matmul(o, oner[:], b, start=False, stop=True),
                   reads=["oner", "bvb"], writes=[pk])
                tk, ta = tq[ei[0] % 4]
                em("vector", lambda e, o=ta[:, 0:256], i_=pa: e.tensor_copy(o, i_),
                   reads=[pk], writes=[tk])
                dst = v_own[vfc, tb8 * 128:(tb8 + 1) * 128, :]
                em("sync", lambda e, d=dst, s=ta[:, 0:256]:
                   e.dma_start(d, s), reads=[tk], writes=["v_own"], sem="s_v")

        sfree("wvs0", "wvs1")
        kg = tuple(t[:] for t in kg_t)
        vg = tuple(t[:].rearrange("p (kc h) -> p kc h", kc=32) for t in vg_t)
        vgflat = tuple(t[:] for t in vg_t)
        qh = (qh_t[0][:], qh_t[1][:])

        def load_kv(g, q="sync"):
            par = g % 2
            sk_ = k_scr.rearrange("(r kf) t -> r kf t", r=4)[
                :, g * 128:(g + 1) * 128, :].rearrange("r p t -> p r t")
            em(q, lambda e, d=kg[par].rearrange("p (r t) -> p r t", r=4),
               s=sk_: e.dma_start(d, s), reads=["k_scr"],
               writes=["kg%d" % par], sem="s_kg%d" % par)

            vsrc = v_scr.rearrange("r v (kcl p) h -> r v p kcl h", p=128)
            for rr in range(4):
                em(q, lambda e,
                   d=vgflat[par][:, rr * 1024:(rr + 1) * 1024].rearrange(
                       "p (kcl h) -> p kcl h", kcl=8),
                   s=vsrc[rr, g // 2, :, :, (g % 2) * 128:(g % 2 + 1) * 128]:
                   e.dma_start(d, s),
                   reads=["v_scr"], writes=["vg%d" % par], sem="s_vg%d" % par)

        def load_qh(h):
            par = h % 2
            em("scalar", lambda e, d=qh[par], s=q_scr[h * 128:(h + 1) * 128, :]:
               e.dma_start(d, s), reads=["q_scr"],
               writes=["qh%d" % par], sem="s_qh%d" % par)


        groups = [[0, 1, 2, 3], [4, 5, 6, 7]]

        def emit_gathers():
            em("gpsimd", lambda e: e.collective_compute(
                "AllGather", ALU.bypass, replica_groups=groups,
                ins=[k_own.opt()], outs=[k_scr.opt()]),
               reads=["k_own"], writes=["k_scr"], tag="ag_k")
            em("gpsimd", lambda e: e.collective_compute(
                "AllGather", ALU.bypass, replica_groups=groups,
                ins=[v_own.opt()], outs=[v_scr.opt()]),
               reads=["v_own"], writes=["v_scr"], tag="ag_v")

        load_wslot(wq_sl, 0)
        for qfc in range(32):
            if qfc == 24:
                emit_gathers()
                load_kv(0, q="scalar")
                load_kv(1, q="scalar")
            if qfc + 1 < 32:
                load_wslot(wq_sl, qfc + 1)
            par = qfc % 2
            for osc in range(2):
                pk, pa = rot(ei[0]); ei[0] += 1
                lhs = [wqv[par][:, dc, :] for dc in range(DC)]
                rhs = [xqv[:, dc, osc * 512:(osc + 1) * 512] for dc in range(DC)]
                n_ = len(lhs)
                for i_, (lh_, rh_) in enumerate(zip(lhs, rhs)):
                    em("tensor",
                       lambda e, a=lh_, b=rh_, st0=(i_ == 0),
                       sp=(i_ == n_ - 1), o=pa:
                       e.matmul(o, a, b, start=st0, stop=sp),
                       reads=["wq%d" % par, "wbx%d" % (i_ // 16)], writes=[pk])
                tk, ta = tq[ei[0] % 4]
                em("vector", lambda e, o=ta[:, 0:512], i_=pa: e.tensor_copy(o, i_),
                   reads=[pk], writes=[tk])
                dst = q_scr[qfc * 128:(qfc + 1) * 128, osc * 512:(osc + 1) * 512]
                em("sync", lambda e, d=dst, s=ta[:, 0:512]: e.dma_start(d, s),
                   reads=[tk], writes=["q_scr"], sem="s_q")

        def fin():
            em("sync", lambda e: e.nop(),
               reads=["y_hbm", "k_scr", "v_scr", "q_scr"])

        if upto < 2:
            fin()
            p.lower(block, semobjs)
            sfree(*list(scoped.keys()))
            return nc

        p.recarve(["wq0", "wq1", "wbv0", "wbv1"],
                  ["pb0", "pb1", "pb2", "ilbc", "l_sb"])
        wsl = (cx[:, 0:8192].rearrange("p (dc f) -> p dc f", dc=32),
               cx[:, 8192:16384].rearrange("p (dc f) -> p dc f", dc=32))

        def load_w(obc, wT):
            par = obc % 2
            em("gpsimd", lambda e, s=wT[obc], d=cx[:, par * 8192:(par + 1) * 8192]:
               e.dma_start(d, s),
               reads=["wo_bf"], writes=["ws%d" % par], sem="g_w%d" % par)
        p.recarve(["wbx0", "wbx1"], ["w%d_%d" % (i_, j_) for i_ in range(32)
                             for j_ in range(2)])
        p.recarve(["psA0", "psA1"], ["sc0", "sc1"])
        p.recarve(["psB0", "psB1"], ["invl", "lps"])
        sfree("wq0", "wq1")
        pb_t = (salloc("pb0", [128, 512], BF16), salloc("pb1", [128, 512], BF16),
                salloc("pb2", [128, 512], BF16))
        ilbc = salloc("ilbc", [128, 1024], F32)[:]
        l_sb = salloc("l_sb", [1, 1024], F32)[:]
        il_sb = l_sb
        psC0 = palloc("psC0", [128, 1024])
        psC1 = palloc("psC1", [128, 1024])
        pb = (pb_t[0][:], pb_t[1][:], pb_t[2][:])
        scv = (psA[:, 0:512], psA[:, 512:1024])
        invlps = psB[:, 0:512]
        lps = psB[:, 512:1024]
        ctxps = (psC0, psC1)
        ctxv = wview

        def emit_pv_l(gp, hp, t):
            kc, half = t // 2, t % 2
            pk = "pb%d" % (t % 3)
            pbh = pb[t % 3]
            cp = ctxps[hp]
            em("tensor", lambda e, o=cp[:, half * 512:(half + 1) * 512],
               a=vg[gp][:, kc, :], b=pbh,
               st0=(t < 2), sp=(t >= 62):
               e.matmul(o, a, b, start=st0, stop=sp),
               reads=["vg%d" % gp, pk], writes=["psC%d" % hp])
            lrow = lps[0:1, :] if half == 0 else lps[32:33, :]
            em("tensor", lambda e, o=lrow, b=pbh,
               st0=(t < 2), sp=(t >= 62):
               e.matmul(o, onec[:], b, start=st0, stop=sp,
                        skip_group_check=True),
               reads=[pk, "onec"], writes=["lps"])

        def make_finalizer(h):
            hp2 = h % 2

            def fin_head_a():
                em("vector", lambda e: e.tensor_copy(l_sb[:, 0:512],
                                                     lps[0:1, :]),
                   reads=["lps"], writes=["l_sb"])
                em("vector", lambda e: e.tensor_copy(l_sb[:, 512:1024],
                                                     lps[32:33, :]),
                   reads=["lps"], writes=["l_sb"])
                em("scalar", lambda e: e.activation(l_sb, l_sb, AF.Ln),
                   reads=["l_sb"], writes=["l_sb"])
                em("scalar", lambda e: e.activation(l_sb, l_sb, AF.Exp,
                                                    scale=-1.0),
                   reads=["l_sb"], writes=["l_sb"])

            def fin_head_b():
                for half in range(2):
                    em("tensor", lambda e,
                       b=il_sb[:, half * 512:(half + 1) * 512]:
                       e.matmul(invlps, onerf[:], b, start=True, stop=True),
                       reads=["l_sb", "onerf"], writes=["invl"])
                    em("vector", lambda e,
                       o=ilbc[:, half * 512:(half + 1) * 512]:
                       e.tensor_copy(o, invlps), reads=["invl"],
                       writes=["ilbc"])
                em("vector", lambda e, o=ctxv[:, h, :], a=ctxps[hp2]:
                   e.tensor_tensor(o, a[:, :], ilbc, ALU.mult),
                   reads=["psC%d" % hp2, "ilbc"],
                   writes=["w%d_0" % h, "w%d_1" % h])
            return fin_head_a, fin_head_b

        pc_jobs = []
        tbv = tbuf[:].rearrange("p (hf dc f) -> p hf dc f", hf=2, dc=8)
        for wsrc, wdst in ((wo_sl, wo_bf), (bw_sl, bw_bf)):
            for j in range(16):
                for q_ in range(4):
                    s_ = wsrc[j, :, q_ * 8:(q_ + 1) * 8, :]
                    d_ = wdst[j][:, q_ * 2048:(q_ + 1) * 2048]
                    t_ = tbv[:, q_ % 2]
                    tk_ = "tbh%d" % (q_ % 2)
                    pc_jobs.append((s_, d_, t_, tk_))
        p.recarve(["tb0", "tb1", "tb2", "tb3"], ["tbh0", "tbh1"])
        pc_i = [0]

        def pump_precast(njobs):
            for _ in range(njobs):
                if pc_i[0] >= len(pc_jobs):
                    return
                s_, d_, t_, tk_ = pc_jobs[pc_i[0]]
                pc_i[0] += 1
                em("gpsimd", lambda e, a=t_, b=s_: e.dma_start(a, b),
                   writes=[tk_], sem="g_pc")
                em("gpsimd", lambda e, a=d_, b=t_: e.dma_start(a, b),
                   reads=[tk_], writes=["wo_bf"], sem="g_ps")

        load_qh(0)
        pend_b = None
        for g in range(NKV):
            gp = g % 2
            if g + 1 < NKV and g >= 1:
                load_kv(g + 1)
            for q4 in range(4):
                h = 4 * g + q4
                hp = h % 2
                if h + 1 < NQ:
                    load_qh(h + 1)
                pump_precast(4)
                kkey, qkey = "kg%d" % gp, "qh%d" % hp
                for t in range(64):
                    kc, half = t // 2, t % 2
                    em("tensor", lambda e, o=scv[t % 2],
                       a=kg[gp][:, kc * 128:(kc + 1) * 128],
                       b=qh[hp][:, half * 512:(half + 1) * 512]:
                       e.matmul(o, a, b, start=True, stop=True),
                       reads=[kkey, qkey], writes=["sc%d" % (t % 2)])
                    em("scalar", lambda e, o=pb[t % 3], i_=scv[t % 2]:
                       e.activation(o, i_, AF.Exp, scale=QSCALE),
                       reads=["sc%d" % (t % 2)], writes=["pb%d" % (t % 3)])
                    if t == 6 and pend_b is not None:
                        pend_b()
                        pend_b = None
                    if t >= 2:
                        emit_pv_l(gp, hp, t - 2)
                emit_pv_l(gp, hp, 62)
                emit_pv_l(gp, hp, 63)
                fa, fb = make_finalizer(h)
                fa()
                pend_b = fb
        pend_b()
        pump_precast(len(pc_jobs))
        if dbg_scratch:
            em("sync", lambda e: e.dma_start(
                ctx_dbg.rearrange("(h p) t -> p h t", p=128), ctxv),
               reads=["w%d_%d" % (i_, j_) for i_ in range(32) for j_ in range(2)],
               writes=["ctx_dbg"], sem="s_c")
        load_w(0, wo_bf)

        if upto < 3:
            fin()
            p.lower(block, semobjs)
            sfree(*list(scoped.keys()))
            return nc

        p.recarve(["kg0", "kg1", "vg0", "vg1", "qh0", "qh1",
                   "pb0", "pb1", "pb2", "ilbc", "l_sb"],
                  ["yb0", "yb1", "yb2", "yb3"])
        p.recarve(["tbh0", "tbh1"], ["tb0", "tb1", "tb2", "tb3"])
        p.recarve(["sc0", "sc1"], ["psA0", "psA1"])
        p.recarve(["invl", "lps"], ["psB0", "psB1"])
        sfree("kg0", "kg1", "vg0", "vg1", "qh0", "qh1",
              "pb0", "pb1", "pb2", "ilbc", "l_sb", "psC0", "psC1")
        ybig = salloc("ybig", [128, 16384], F32)
        acat = salloc("acat", [128, 4096], BF16)
        bcat_s = salloc("bcat_s", [128, 4096], BF16)
        bbb = salloc("bbb", [1, 4096], BF16)
        em("gpsimd", lambda e: e.dma_start(
            acat[:].rearrange("p (dc r) -> p dc r", dc=32), acat_sl),
           writes=["acat"], sem="g_c")
        em("gpsimd", lambda e: e.dma_start(bcat_s[:], bcat),
           writes=["bcat_s"], sem="g_c")
        em("gpsimd", lambda e: e.dma_start(bbb[:], bb), writes=["bbb"],
           sem="g_c")
        psDt = palloc("psDt", [64, 1024], BF16)
        psE = palloc("psE", [128, 512])
        ybv = ybig[:].rearrange("p (tb f) -> p tb f", tb=4)
        yblk = [ybv[:, i, :] for i in range(4)]
        wres = wview
        rwv = rwb[:].rearrange("p (dc r) -> p dc r", dc=32)
        acv = acat[:].rearrange("p (dc r) -> p dc r", dc=32)

        def wkey(i, tc):
            return "w%d_%d" % (i, tc)

        def emit_bn(tb, i):
            yb = yblk[tb]
            stv = stats[:, 48 * tb:48 * (tb + 1)]
            em("vector", lambda e, o=stv[:, 6 * i:6 * i + 6],
               a=yb[:, 512 * i:512 * (i + 1)]:
               e.bn_stats(o, a), reads=["yb%d" % tb],
               writes=["st%d_%d" % (tb, i)])

        def ln_pass(tb, stats_done=False):
            ybk = "yb%d" % tb
            yb = yblk[tb]
            stv = stats[:, 48 * tb:48 * (tb + 1)]
            mvv = mv[:, 2 * tb:2 * (tb + 1)]
            s1v, s2v, s3v = (s1[:, tb:tb + 1], s2[:, tb:tb + 1],
                             s3[:, tb:tb + 1])
            if not stats_done:
                for i in range(8):
                    emit_bn(tb, i)
            em("vector", lambda e, o=mvv, a=stv: e.bn_aggr(o, a),
               reads=["st%d_%d" % (tb, i2) for i2 in range(8)],
               writes=["mv%d" % tb])
            em("scalar", lambda e, o=s1v, a=mvv[:, 1:2]:
               e.activation(o, a, AF.Ln, bias=epsb[:]),
               reads=["mv%d" % tb, "epsb"], writes=["s1%d" % tb])
            em("scalar", lambda e, o=s2v, a=s1v:
               e.activation(o, a, AF.Exp, scale=-0.5),
               reads=["s1%d" % tb], writes=["s2%d" % tb])
            em("vector", lambda e, o=s3v, a=mvv[:, 0:1], b=s2v:
               e.tensor_tensor(o, a, b, ALU.mult),
               reads=["mv%d" % tb, "s2%d" % tb], writes=["s3%d" % tb])
            em("vector", lambda e, o=yb, a=s2v, b=s3v: e.tensor_scalar(
                o, o, a, b, ALU.mult, op1=ALU.subtract),
               reads=[ybk, "s2%d" % tb, "s3%d" % tb], writes=[ybk])

        def mm_pass(tc, wbf, lora):
            for obc in range(16):
                if obc + 1 < 16:
                    load_w(obc + 1, wbf)
                par = obc % 2
                for tb in range(4):
                    pk, pa0 = rot(ei[0]); ei[0] += 1
                    pa = pa0[:, 0:256]
                    for dc in range(DC):
                        em("tensor", lambda e, o=pa,
                           a=wres[:, dc, tc * 512 + tb * 128:
                                  tc * 512 + (tb + 1) * 128],
                           b=wsl[par][:, dc, :], st0=(dc == 0),
                           sp=(dc == 31 and not lora):
                           e.matmul(o, a, b, start=st0, stop=sp),
                           reads=[wkey(dc, tc), "ws%d" % par], writes=[pk])
                    if lora:
                        em("tensor", lambda e, o=pa,
                           b=bbb[:, obc * 256:(obc + 1) * 256]:
                           e.matmul(o, oner[:], b, start=False, stop=False),
                           reads=["oner", "bbb"], writes=[pk])
                        em("tensor", lambda e, o=pa,
                           a=hwb[:, tb * 128:(tb + 1) * 128],
                           b=bcat_s[:, obc * 256:(obc + 1) * 256]:
                           e.matmul(o, a, b, start=False, stop=True),
                           reads=["hwb", "bcat_s"], writes=[pk])
                    em("vector", lambda e,
                       o=yblk[tb][:, obc * 256:(obc + 1) * 256],
                       i_=pa: e.tensor_tensor(o, o, i_, ALU.add),
                       reads=[pk, "yb%d" % tb], writes=["yb%d" % tb])
                    if obc % 2 == 1:
                        emit_bn(tb, obc // 2)

        for tc in range(2):
            for tb in range(4):
                trow = tc * 512 + tb * 128
                em("sync", lambda e, d=yblk[tb], s=xo[trow:trow + 128, :]:
                   e.dma_start(d, s), writes=["yb%d" % tb], sem="s_res")
            mm_pass(tc, wo_bf, lora=False)
            load_w(0, bw_bf)

            psCt = palloc("psCt", [128, 2048], BF16)
            rps = []
            for tb in range(4):
                ln_pass(tb, stats_done=True)
                em("vector", lambda e, i_=yblk[tb]: e.tensor_copy(tbuf[:], i_),
                   reads=["yb%d" % tb], writes=["tb0", "tb1", "tb2", "tb3"])
                for dg in range(8):
                    for j in range(4):
                        dc = dg * 4 + j
                        em("tensor", lambda e,
                           o=psCt[:, j * 128:(j + 1) * 128],
                           i_=tbuf[:, dc * 128:(dc + 1) * 128]:
                           e.transpose(o, i_, idb[:]),
                           reads=["tb0", "tb1", "tb2", "tb3", "idb"],
                           writes=["psC"])
                    em("vector", lambda e,
                       o=wres[:, dg * 4:dg * 4 + 4,
                              tc * 512 + tb * 128:tc * 512 + (tb + 1) * 128],
                       i_=psCt[:, 0:512].rearrange("p (j t) -> p j t", j=4):
                       e.tensor_copy(o, i_), reads=["psC"],
                       writes=[wkey(dg * 4 + j2, tc) for j2 in range(4)])
                pk, pa = rot(ei[0]); ei[0] += 1
                rps.append((pk, pa))
                em("tensor", lambda e, o=pa[:, 0:16]:
                   e.matmul(o, oner[:], rbb[:], start=True, stop=False),
                   reads=["oner", "rbb"], writes=[pk])
                for dc in range(DC):
                    em("tensor", lambda e, o=pa[:, 0:16],
                       a=wres[:, dc, tc * 512 + tb * 128:
                              tc * 512 + (tb + 1) * 128],
                       b=rwv[:, dc, :], sp=(dc == 31):
                       e.matmul(o, a, b, start=False, stop=sp),
                       reads=[wkey(dc, tc), "rwb"], writes=[pk])
            sfree("psCt")

            for tb in range(4):
                pk, pa = rps[tb]
                lgv = lg[:, 0:15]
                prv = pr[:, 0:15]
                s1v, s2v, s3v = (s1[:, tb:tb + 1], s2[:, tb:tb + 1],
                                 s3[:, tb:tb + 1])
                em("vector", lambda e, a=pa[:, 0:16], o=lg[:]:
                   e.tensor_copy(o, a), reads=[pk], writes=["lg"])
                em("vector", lambda e, o=s1v: e.tensor_reduce(
                    o, lgv, AX.X, ALU.max, negate=True),
                   reads=["lg"], writes=["s1%d" % tb])
                em("scalar", lambda e, o=prv, b=s1v, ac=s2v: e.activation(
                    o, lgv, AF.Exp, bias=b, accum_out=ac),
                   reads=["lg", "s1%d" % tb], writes=["pr", "s2%d" % tb])
                em("vector", lambda e, o=s3v, a=s2v: e.reciprocal(o, a),
                   reads=["s2%d" % tb], writes=["s3%d" % tb])
                em("vector", lambda e, o=prv, a=s3v: e.tensor_scalar(
                    o, o, a, None, ALU.mult),
                   reads=["pr", "s3%d" % tb], writes=["pr"])
                for n in range(15):
                    em("vector", lambda e, nn_=n: e.tensor_scalar(
                        j15[:, 0:15], pr[:, 0:15], pr[:, nn_:nn_ + 1], None,
                        ALU.is_gt, op1=ALU.add,
                        accum_out=cnts[:, nn_:nn_ + 1]),
                       reads=["pr"], writes=["j15", "c%d" % n])
                em("vector", lambda e: e.tensor_scalar(
                    mskb[:, 0:15], cnts[:, 0:15], 8.0, None, ALU.is_lt),
                   reads=["c%d" % n2 for n2 in range(15)], writes=["mskb"])
                em("vector", lambda e: e.tensor_tensor(
                    pmb[:, 0:15], pr[:, 0:15], mskb[:, 0:15], ALU.mult),
                   reads=["pr", "mskb"], writes=["pmb"])
                em("tensor", lambda e: e.transpose(
                    psDt[0:15, 0:128], pmb[:, 0:15], idb[:]),
                   reads=["pmb", "idb"], writes=["psD"])
                em("vector", lambda e, o=pmT[:, tb * 128:(tb + 1) * 128]:
                   e.tensor_copy(o, psDt[0:15, 0:128]),
                   reads=["psD"], writes=["pmT"])
            em("tensor", lambda e: e.matmul(psE[0:8, 0:512], mmapb[:], pmT[:],
                                            start=True, stop=True),
               reads=["mmapb", "pmT"], writes=["psE"])
            em("vector", lambda e: e.tensor_copy(wtb[:], psE[0:8, 0:512]),
               reads=["psE"], writes=["wtb"])
            em("tensor", lambda e: e.matmul(psE[:], e16b[:], wtb[:],
                                            start=True, stop=True),
               reads=["e16b", "wtb"], writes=["psE"])
            em("vector", lambda e: e.tensor_copy(wbcb[:], psE[:]),
               reads=["psE"], writes=["wbcb"])
            psCh = palloc("psCh", [128, 512])
            for dc in range(DC):
                em("tensor", lambda e, a=acv[:, dc, :],
                   b=wres[:, dc, tc * 512:(tc + 1) * 512],
                   st0=(dc == 0), sp=(dc == 31):
                   e.matmul(psCh[:], a, b, start=st0, stop=sp),
                   reads=["acat", wkey(dc, tc)], writes=["psC"])
            em("vector", lambda e: e.tensor_tensor(hwb[:], psCh[:],
                                                   wbcb[:], ALU.mult),
               reads=["psC", "wbcb"], writes=["hwb"])
            sfree("psCh")
            mm_pass(tc, bw_bf, lora=True)
            if tc == 0:
                load_w(0, wo_bf)
            for tb in range(4):
                ln_pass(tb, stats_done=True)
                trow = tc * 512 + tb * 128
                em("sync", lambda e, d=y[trow:trow + 128, :], s=yblk[tb]:
                   e.dma_start(d, s), reads=["yb%d" % tb], writes=["y_hbm"],
                   sem="s_y")

        fin()
        p.lower(block, semobjs)
        sfree(*list(scoped.keys()))
    return nc


def _host_inputs(inp):
    x = np.ascontiguousarray(np.asarray(inp["L0_attn_hidden_states"],
                                        dtype=np.float32))
    c = np.ascontiguousarray

    def slot_img(w, nf):
        O = w.shape[0]
        return c(w.reshape(O // nf, nf, 32, 128).transpose(0, 3, 2, 1))

    wq_sl = slot_img(np.asarray(inp["L0_attn_Wq"], np.float32), 128)
    wk_sl = slot_img(np.asarray(inp["L0_attn_Wk"], np.float32), 128)
    wv_sl = slot_img(np.asarray(inp["L0_attn_Wv"], np.float32), 256)
    wo_sl = slot_img(np.asarray(inp["L0_attn_Wo"], np.float32), 256)
    bw_sl = slot_img(np.asarray(inp["L0_ffn_base_W"], np.float32), 256)
    bv = np.asarray(inp["L0_attn_bv"], np.float32).reshape(1, 1024)
    bb = np.asarray(inp["L0_ffn_base_b"], np.float32).reshape(1, D)
    rw = np.asarray(inp["L0_ffn_router_W"], np.float32)
    rwp = np.zeros((16, D), np.float32)
    rwp[0:15] = rw
    rw_sl = slot_img(rwp, 16)[0]
    rb = np.zeros((1, 16), np.float32)
    rb[0, 0:15] = np.asarray(inp["L0_ffn_router_b"], np.float32)
    A = np.asarray(inp["L0_ffn_A"], np.float32)
    acat_sl = slot_img(A.reshape(128, D), 128)[0]
    Bm = np.asarray(inp["L0_ffn_B"], np.float32)
    bcat = c(Bm.transpose(0, 2, 1).reshape(128, D))
    emap = np.asarray(inp["L0_ffn_expert_map"], np.int32).reshape(15, 1)
    iota8 = np.broadcast_to(np.arange(8, dtype=np.float32), (15, 8)).copy()
    e16 = np.zeros((8, 128), np.float32)
    for e_ in range(8):
        e16[e_, e_ * 16:(e_ + 1) * 16] = LORA_SCALE
    idbf = np.eye(128).astype(ml_dtypes.bfloat16)

    shared = dict(wq_sl=wq_sl, wk_sl=wk_sl, wv_sl=wv_sl, wo_sl=wo_sl,
                  bw_sl=bw_sl, bv=bv, bb=bb, rw_sl=rw_sl, rb=rb,
                  acat_sl=acat_sl, bcat=bcat, emap=emap,
                  iota8=iota8, e16=e16, idbf=idbf)
    xT = [c(x[b_].T) for b_ in range(B)]
    in_maps = []
    for core in range(NCORE):
        b_ = core // 4
        q0 = (core % 4) * TPC
        m = dict(shared)
        m["xoT"] = c(xT[b_][:, q0:q0 + TPC])
        m["xo"] = c(x[b_, q0:q0 + TPC, :])
        in_maps.append(m)
    return in_maps


def kernel(**inputs):
    in_maps = _host_inputs(inputs)
    nc = build_nc()
    res = run_bass_kernel_spmd(nc, in_maps, core_ids=list(range(NCORE)))
    out = np.empty((B, S, D), np.float32)
    for core in range(NCORE):
        b_ = core // 4
        q0 = (core % 4) * TPC
        out[b_, q0:q0 + TPC, :] = np.asarray(res.results[core]["y"],
                                             dtype=np.float32)
    return out


# version 20
# speedup vs baseline: 1.2138x; 1.0127x over previous
import numpy as np
import ml_dtypes
import concourse.bass as bass
import concourse.mybir as mybir
from concourse.bass_utils import run_bass_kernel_spmd

F32 = mybir.dt.float32
BF16 = mybir.dt.bfloat16
I32 = mybir.dt.int32
AF = mybir.ActivationFunctionType
ALU = mybir.AluOpType
AX = mybir.AxisListType

B, S, D = 2, 4096, 4096
NKV, NQ, HD = 8, 32, 128
NCORE = 8
TPC = 1024
DC = D // 128
QSCALE = float(HD) ** -0.5
EPS = 1e-5
LORA_SCALE = 2.0

_STREAMS = ("tensor", "scalar", "vector", "gpsimd", "sync")


class _Op:
    __slots__ = ("idx", "stream", "pos", "fn", "sem", "val", "waits", "inc",
                 "snap", "tag")


class Prog:

    def __init__(self):
        self.ops = []
        self.pos = {s: 0 for s in _STREAMS}
        self.res = {}
        self.clock = {s: {} for s in _STREAMS}
        self.dmacnt = {}

    def emit(self, stream, fn, reads=(), writes=(), sem=None, tag=""):
        op = _Op()
        op.idx = len(self.ops)
        op.stream = stream
        op.fn = fn
        op.sem = sem
        op.tag = tag
        op.inc = False
        op.pos = self.pos[stream]
        self.pos[stream] += 1
        deps = {}
        for k in reads:
            r = self.res.get(k)
            if r and r[0] is not None:
                deps[r[0].idx] = r[0]
        for k in writes:
            r = self.res.get(k)
            if r:
                if r[0] is not None:
                    deps[r[0].idx] = r[0]
                for rd in r[1]:
                    deps[rd.idx] = rd
        clk = self.clock[stream]
        waits = {}
        for d in deps.values():
            if d.stream == stream and d.sem is None and stream == "tensor":
                continue
            if sem is not None and d.sem == sem:
                continue
            if d.sem is None:
                ck, onum = "s:" + d.stream, d.pos
            else:
                ck, onum = "d:" + d.sem, d.val
            if clk.get(ck, -1) >= onum:
                continue
            if d.sem is not None:
                onum = self.dmacnt[d.sem]
            d.inc = True
            prev = waits.get(ck)
            if prev is None or prev[1] < onum:
                waits[ck] = (d, onum)
            for k2, v2 in d.snap.items():
                if clk.get(k2, -1) < v2:
                    clk[k2] = v2
            clk[ck] = max(clk.get(ck, -1), onum)
        op.waits = list(waits.items())
        if sem is not None:
            self.dmacnt[sem] = self.dmacnt.get(sem, 0) + 16
            op.val = self.dmacnt[sem]
        snap = dict(clk)
        ck0 = ("s:" + stream) if sem is None else ("d:" + sem)
        o0 = op.pos if sem is None else op.val
        snap[ck0] = max(snap.get(ck0, -1), o0)
        op.snap = snap
        for k in reads:
            self.res.setdefault(k, [None, []])[1].append(op)
        for k in writes:
            e = self.res.setdefault(k, [None, []])
            e[0] = op
            e[1] = []
        self.ops.append(op)
        return op

    def recarve(self, old_keys, new_keys):
        pend = []
        for k in old_keys:
            r = self.res.pop(k, None)
            if r:
                if r[0] is not None:
                    pend.append(r[0])
                pend.extend(r[1])
        for k in new_keys:
            self.res[k] = [None, list(pend)]

    def lower(self, block, semobjs):
        cnt = {s: 0 for s in _STREAMS}
        val = {}
        for op in self.ops:
            if op.sem is None and op.inc:
                cnt[op.stream] += 1
                val[op.idx] = cnt[op.stream]
        streams = {s: [] for s in _STREAMS}
        for op in self.ops:
            streams[op.stream].append(op)

        def mk(ops):
            def body(e):
                for op in ops:
                    for ck, (d, onum) in op.waits:
                        if d.sem is None:
                            e.wait_ge(semobjs["s:" + d.stream], val[d.idx])
                        else:
                            e.wait_ge(semobjs["d:" + d.sem], onum)
                    last = op.fn(e)
                    if op.sem is not None:
                        last.then_inc(semobjs["d:" + op.sem], 16)
                    elif op.inc:
                        last.then_inc(semobjs["s:" + op.stream], 1)
            return body

        for s in _STREAMS:
            if streams[s]:
                getattr(block, s)(mk(streams[s]))


def build_nc(dbg_scratch=False, upto=4):
    nc = bass.Bass("TRN2", target_bir_lowering=False)
    p = Prog()

    def din(name, shape, dt=F32):
        return nc.dram_tensor(name, shape, dt, kind="ExternalInput").ap()

    xoT = din("xoT", [D, TPC])
    xo = din("xo", [TPC, D])
    wq_sl = din("wq_sl", [32, 128, 32, 128])
    wk_sl = din("wk_sl", [8, 128, 32, 128])
    wv_sl = din("wv_sl", [4, 128, 32, 256])
    wo_sl = din("wo_sl", [16, 128, 32, 256])
    bw_sl = din("bw_sl", [16, 128, 32, 256])
    bv = din("bv", [1, 1024])
    bb = din("bb", [1, D])
    rw_sl = din("rw_sl", [128, 32, 16])
    rb = din("rb", [1, 16])
    acat_sl = din("acat_sl", [128, 32, 128])
    bcat = din("bcat", [128, D])
    emap = din("emap", [15, 1], I32)
    iota8 = din("iota8", [15, 8])
    e16 = din("e16", [8, 128])
    idbf = din("idbf", [128, 128], BF16)

    y = nc.dram_tensor("y", [TPC, D], F32, kind="ExternalOutput").ap()

    skind = {"kind": "ExternalOutput"} if dbg_scratch else {}
    if dbg_scratch:
        ctx_dbg = nc.dram_tensor("ctx_dbg", [D, TPC], BF16,
                                 kind="ExternalOutput").ap()
    k_own = nc.dram_tensor("k_own", [1024, TPC], BF16).ap()
    v_own = nc.dram_tensor("v_own", [4, TPC, 256], BF16).ap()
    k_scr = nc.dram_tensor("k_scr", [4 * 1024, TPC], BF16, **skind).ap()
    v_scr = nc.dram_tensor("v_scr", [4, 4, TPC, 256], BF16, **skind).ap()
    q_scr = nc.dram_tensor("q_scr", [D, TPC], BF16, **skind).ap()
    wo_bf = nc.dram_tensor("wo_bf", [16, 128, 8192], BF16).ap()
    bw_bf = nc.dram_tensor("bw_bf", [16, 128, 8192], BF16).ap()

    from contextlib import ExitStack
    with ExitStack() as st:
        def sb(name, shape, dt):
            return st.enter_context(nc.sbuf_tensor(name, shape, dt))

        def psalloc(name, shape, dt=F32):
            return st.enter_context(nc.psum_tensor(name, shape, dt))

        wbig = sb("wbig", [128, 32768], BF16)
        tbuf = sb("tbuf", [128, 4096], BF16)
        cx = sb("cx", [128, 16384], BF16)
        scoped = {}
        scoped_seq = [0]

        def salloc(name, shape, dt):
            g = nc.sbuf_tensor("%s_s%d" % (name, scoped_seq[0]), shape, dt)
            scoped[name] = (scoped_seq[0], g)
            scoped_seq[0] += 1
            return g.__enter__()

        def sfree(*names):
            for n in sorted(names, key=lambda n2: -scoped[n2][0]):
                scoped.pop(n)[1].__exit__(None, None, None)

        def palloc(name, shape, dt=F32):
            g = nc.psum_tensor("%s_s%d" % (name, scoped_seq[0]), shape, dt)
            scoped[name] = (scoped_seq[0], g)
            scoped_seq[0] += 1
            return g.__enter__()
        rwb = sb("rwb", [128, 512], BF16)
        idb = sb("idb", [128, 128], BF16)
        hwb = sb("hwb", [128, 512], BF16)
        wbcb = sb("wbcb", [128, 512], BF16)
        bvb = sb("bvb", [1, 1024], BF16)
        rbb = sb("rbb", [1, 16], BF16)
        e16b = sb("e16b", [8, 128], BF16)
        mmapb = sb("mmapb", [15, 8], BF16)
        mmapf = sb("mmapf", [15, 8], F32)
        iot = sb("iot", [15, 8], F32)
        emi = sb("emi", [15, 1], I32)
        emf = sb("emf", [15, 1], F32)
        onec = sb("onec", [128, 1], BF16)
        oner = sb("oner", [1, 128], BF16)
        onerf = sb("onerf", [1, 128], F32)
        epsb = sb("epsb", [128, 1], F32)
        lg = sb("lg", [128, 16], F32)
        pr = sb("pr", [128, 16], F32)
        j15 = sb("j15", [128, 16], F32)
        cnts = sb("cnts", [128, 16], F32)
        mskb = sb("mskb", [128, 16], F32)
        pmb = sb("pmb", [128, 16], BF16)
        pmT = sb("pmT", [15, 512], BF16)
        wtb = sb("wtb", [8, 512], BF16)
        stats = sb("stats", [128, 4 * 48], F32)
        mv = sb("mv", [128, 4 * 2], F32)
        s1 = sb("s1", [128, 4], F32)
        s2 = sb("s2", [128, 4], F32)
        s3 = sb("s3", [128, 4], F32)

        psA = psalloc("psA", [128, 1024])
        psB = psalloc("psB", [128, 1024])

        sem_names = (["s:" + s for s in _STREAMS] +
                     ["d:" + n for n in
                      ("g_w", "g_x0", "g_x1", "g_wq0", "g_wq1", "g_c",
                       "s_c", "s_k", "s_v", "s_q", "s_ctx", "s_y",
                       "s_kg0", "s_kg1", "s_kg2", "s_vg0", "s_vg1", "s_vg2", "s_qh0", "s_qh1",
                       "s_cx", "s_res", "g_w0", "g_w1", "g_pc", "g_ps", "g_wv0", "g_wv1")])
        semobjs = {n: st.enter_context(nc.semaphore(n.replace(":", "_")))
                   for n in sem_names}
        block = st.enter_context(nc.Block())

        em = p.emit

        em("sync", lambda e: e.dma_start(idb[:], idbf), writes=["idb"], sem="s_c")
        em("sync", lambda e: e.dma_start(iot[:], iota8), writes=["iot"], sem="s_c")
        em("sync", lambda e: e.dma_start(emi[:], emap), writes=["emi"], sem="s_c")
        em("gpsimd", lambda e: e.dma_start(bvb[:], bv), writes=["bvb"], sem="g_c")
        em("gpsimd", lambda e: e.dma_start(rbb[:], rb), writes=["rbb"], sem="g_c")
        em("gpsimd", lambda e: e.dma_start(e16b[:], e16), writes=["e16b"], sem="g_c")
        em("gpsimd", lambda e: e.dma_start(
            rwb[:].rearrange("p (dc r) -> p dc r", dc=32), rw_sl),
           writes=["rwb"], sem="g_c")
        em("vector", lambda e: e.memset(onec[:], 1.0), writes=["onec"])
        em("vector", lambda e: e.memset(oner[:], 1.0), writes=["oner"])
        em("vector", lambda e: e.memset(onerf[:], 1.0), writes=["onerf"])
        em("vector", lambda e: e.memset(epsb[:], EPS), writes=["epsb"])
        em("vector", lambda e: e.tensor_copy(emf[:], emi[:]),
           reads=["emi"], writes=["emf"])
        em("vector", lambda e: e.tensor_scalar(
            mmapf[:], iot[:], emf[:], None, ALU.is_equal),
           reads=["iot", "emf"], writes=["mmapf"])
        em("vector", lambda e: e.tensor_copy(mmapb[:], mmapf[:]),
           reads=["mmapf"], writes=["mmapb"])

        wview = wbig[:].rearrange("p (dc f) -> p dc f", f=1024)

        psum_rot = [("psA0", psA[:, 0:512]), ("psA1", psA[:, 512:1024]),
                    ("psB0", psB[:, 0:512]), ("psB1", psB[:, 512:1024])]

        def rot(i):
            return psum_rot[i % 4]

        tq = [("tb0", tbuf[:, 0:1024]), ("tb1", tbuf[:, 1024:2048]),
              ("tb2", tbuf[:, 2048:3072]), ("tb3", tbuf[:, 3072:4096])]

        ei = [0]

        def mm_group(ps_key, ps_ap, lhs_list, rhs_list, reads):
            n = len(lhs_list)
            for i, (lh, rh) in enumerate(zip(lhs_list, rhs_list)):
                em("tensor",
                   lambda e, a=lh, b=rh, st0=(i == 0), sp=(i == n - 1), o=ps_ap:
                   e.matmul(o, a, b, start=st0, stop=sp),
                   reads=list(reads), writes=[ps_key])

        p.recarve(["wbig"], ["wbx0", "wbx1"])
        kg_t = (salloc("kg0", [128, 4096], BF16), salloc("kg1", [128, 4096], BF16))
        vg_t = (salloc("vg0", [128, 4096], BF16), salloc("vg1", [128, 4096], BF16))
        qh_t = (salloc("qh0", [128, 1024], BF16), salloc("qh1", [128, 1024], BF16))
        wq0 = salloc("wq0", [128, 4096], BF16)
        wq1 = salloc("wq1", [128, 4096], BF16)
        wvs0 = salloc("wvs0", [128, 8192], BF16)
        wvs1 = salloc("wvs1", [128, 8192], BF16)
        wqv = (wq0[:].rearrange("p (dc f) -> p dc f", dc=32),
               wq1[:].rearrange("p (dc f) -> p dc f", dc=32))
        xqv = wbig[:].rearrange("p (dc t) -> p dc t", dc=32)
        wvv2 = (wvs0[:].rearrange("p (dc f) -> p dc f", dc=32),
                wvs1[:].rearrange("p (dc f) -> p dc f", dc=32))
        xsrc = xoT.rearrange("(dc p) t -> p dc t", p=128)
        em("gpsimd", lambda e: e.dma_start(xqv[:, 0:16, :], xsrc[:, 0:16, :]),
           writes=["wbx0"], sem="g_w", tag="xqa")
        em("gpsimd", lambda e: e.dma_start(xqv[:, 16:32, :], xsrc[:, 16:32, :]),
           writes=["wbx1"], sem="g_w", tag="xqb")

        def load_wslot(wsl_in, fc, q="gpsimd"):
            par = fc % 2
            em(q, lambda e, s=wsl_in[fc], d=wqv[par]: e.dma_start(d, s),
               writes=["wq%d" % par], sem="g_wq%d" % par)

        load_wslot(wk_sl, 0)
        for kfc in range(8):
            if kfc + 1 < 8:
                load_wslot(wk_sl, kfc + 1)
            par = kfc % 2
            for osc in range(2):
                pk, pa = rot(ei[0]); ei[0] += 1
                lhs = [wqv[par][:, dc, :] for dc in range(DC)]
                rhs = [xqv[:, dc, osc * 512:(osc + 1) * 512] for dc in range(DC)]
                n_ = len(lhs)
                for i_, (lh_, rh_) in enumerate(zip(lhs, rhs)):
                    em("tensor",
                       lambda e, a=lh_, b=rh_, st0=(i_ == 0),
                       sp=(i_ == n_ - 1), o=pa:
                       e.matmul(o, a, b, start=st0, stop=sp),
                       reads=["wq%d" % par, "wbx%d" % (i_ // 16)], writes=[pk])
                tk, ta = tq[ei[0] % 4]
                em("vector", lambda e, o=ta[:, 0:512], i_=pa: e.tensor_copy(o, i_),
                   reads=[pk], writes=[tk])
                dst = k_own[kfc * 128:(kfc + 1) * 128, osc * 512:(osc + 1) * 512]
                em("sync", lambda e, d=dst, s=ta[:, 0:512]: e.dma_start(d, s),
                   reads=[tk], writes=["k_own"], sem="s_k")

        for vfc in range(4):
            par = vfc % 2
            em("gpsimd", lambda e, s=wv_sl[vfc], d=wvv2[par]: e.dma_start(d, s),
               writes=["wbv%d" % par], sem="g_wv%d" % par, tag="wv%d" % vfc)
            wvv = wvv2[par]
            for tb8 in range(8):
                pk, pa0 = rot(ei[0]); ei[0] += 1
                pa = pa0[:, 0:256]
                for dc in range(DC):
                    em("tensor", lambda e, o=pa,
                       a=xqv[:, dc, tb8 * 128:(tb8 + 1) * 128],
                       b=wvv[:, dc, :], st0=(dc == 0):
                       e.matmul(o, a, b, start=st0, stop=False),
                       reads=["wbv%d" % par, "wbx%d" % (dc // 16)], writes=[pk])
                em("tensor", lambda e, o=pa,
                   b=bvb[:, vfc * 256:(vfc + 1) * 256]:
                   e.matmul(o, oner[:], b, start=False, stop=True),
                   reads=["oner", "bvb"], writes=[pk])
                tk, ta = tq[ei[0] % 4]
                em("vector", lambda e, o=ta[:, 0:256], i_=pa: e.tensor_copy(o, i_),
                   reads=[pk], writes=[tk])
                dst = v_own[vfc, tb8 * 128:(tb8 + 1) * 128, :]
                em("sync", lambda e, d=dst, s=ta[:, 0:256]:
                   e.dma_start(d, s), reads=[tk], writes=["v_own"], sem="s_v")

        groups = [[0, 1, 2, 3], [4, 5, 6, 7]]
        em("gpsimd", lambda e: e.collective_compute(
            "AllGather", ALU.bypass, replica_groups=groups,
            ins=[k_own.opt()], outs=[k_scr.opt()]),
           reads=["k_own"], writes=["k_scr"], tag="ag_k")
        em("gpsimd", lambda e: e.collective_compute(
            "AllGather", ALU.bypass, replica_groups=groups,
            ins=[v_own.opt()], outs=[v_scr.opt()]),
           reads=["v_own"], writes=["v_scr"], tag="ag_v")

        sfree("wvs0", "wvs1")
        kg = tuple(t[:] for t in kg_t)
        vg = tuple(t[:].rearrange("p (kc h) -> p kc h", kc=32) for t in vg_t)
        vgflat = tuple(t[:] for t in vg_t)
        qh = (qh_t[0][:], qh_t[1][:])

        def load_kv(g, q="sync"):
            par = g % 2
            sk_ = k_scr.rearrange("(r kf) t -> r kf t", r=4)[
                :, g * 128:(g + 1) * 128, :].rearrange("r p t -> p r t")
            em(q, lambda e, d=kg[par].rearrange("p (r t) -> p r t", r=4),
               s=sk_: e.dma_start(d, s), reads=["k_scr"],
               writes=["kg%d" % par], sem="s_kg%d" % par)

            vsrc = v_scr.rearrange("r v (kcl p) h -> r v p kcl h", p=128)
            for rr in range(4):
                em(q, lambda e,
                   d=vgflat[par][:, rr * 1024:(rr + 1) * 1024].rearrange(
                       "p (kcl h) -> p kcl h", kcl=8),
                   s=vsrc[rr, g // 2, :, :, (g % 2) * 128:(g % 2 + 1) * 128]:
                   e.dma_start(d, s),
                   reads=["v_scr"], writes=["vg%d" % par], sem="s_vg%d" % par)

        def load_qh(h):
            par = h % 2
            em("scalar", lambda e, d=qh[par], s=q_scr[h * 128:(h + 1) * 128, :]:
               e.dma_start(d, s), reads=["q_scr"],
               writes=["qh%d" % par], sem="s_qh%d" % par)

        load_kv(0, q="scalar")
        load_kv(1, q="scalar")

        load_wslot(wq_sl, 0)
        for qfc in range(32):
            if qfc + 1 < 32:
                load_wslot(wq_sl, qfc + 1)
            par = qfc % 2
            for osc in range(2):
                pk, pa = rot(ei[0]); ei[0] += 1
                lhs = [wqv[par][:, dc, :] for dc in range(DC)]
                rhs = [xqv[:, dc, osc * 512:(osc + 1) * 512] for dc in range(DC)]
                n_ = len(lhs)
                for i_, (lh_, rh_) in enumerate(zip(lhs, rhs)):
                    em("tensor",
                       lambda e, a=lh_, b=rh_, st0=(i_ == 0),
                       sp=(i_ == n_ - 1), o=pa:
                       e.matmul(o, a, b, start=st0, stop=sp),
                       reads=["wq%d" % par, "wbx%d" % (i_ // 16)], writes=[pk])
                tk, ta = tq[ei[0] % 4]
                em("vector", lambda e, o=ta[:, 0:512], i_=pa: e.tensor_copy(o, i_),
                   reads=[pk], writes=[tk])
                dst = q_scr[qfc * 128:(qfc + 1) * 128, osc * 512:(osc + 1) * 512]
                em("sync", lambda e, d=dst, s=ta[:, 0:512]: e.dma_start(d, s),
                   reads=[tk], writes=["q_scr"], sem="s_q")

        def fin():
            em("sync", lambda e: e.nop(),
               reads=["y_hbm", "k_scr", "v_scr", "q_scr"])

        if upto < 2:
            fin()
            p.lower(block, semobjs)
            sfree(*list(scoped.keys()))
            return nc

        p.recarve(["wq0", "wq1", "wbv0", "wbv1"],
                  ["pb0", "pb1", "pb2", "ilbc", "l_sb"])
        wsl = (cx[:, 0:8192].rearrange("p (dc f) -> p dc f", dc=32),
               cx[:, 8192:16384].rearrange("p (dc f) -> p dc f", dc=32))

        def load_w(obc, wT):
            par = obc % 2
            em("gpsimd", lambda e, s=wT[obc], d=cx[:, par * 8192:(par + 1) * 8192]:
               e.dma_start(d, s),
               reads=["wo_bf"], writes=["ws%d" % par], sem="g_w%d" % par)
        p.recarve(["wbx0", "wbx1"], ["w%d_%d" % (i_, j_) for i_ in range(32)
                             for j_ in range(2)])
        p.recarve(["psA0", "psA1"], ["sc0", "sc1"])
        p.recarve(["psB0", "psB1"], ["invl", "lps"])
        sfree("wq0", "wq1")
        pb_t = (salloc("pb0", [128, 512], BF16), salloc("pb1", [128, 512], BF16),
                salloc("pb2", [128, 512], BF16))
        ilbc = salloc("ilbc", [128, 1024], F32)[:]
        l_sb = salloc("l_sb", [1, 1024], F32)[:]
        il_sb = l_sb
        psC0 = palloc("psC0", [128, 1024])
        psC1 = palloc("psC1", [128, 1024])
        pb = (pb_t[0][:], pb_t[1][:], pb_t[2][:])
        scv = (psA[:, 0:512], psA[:, 512:1024])
        invlps = psB[:, 0:512]
        lps = psB[:, 512:1024]
        ctxps = (psC0, psC1)
        ctxv = wview

        def emit_pv_l(gp, hp, t):
            kc, half = t // 2, t % 2
            pk = "pb%d" % (t % 3)
            pbh = pb[t % 3]
            cp = ctxps[hp]
            em("tensor", lambda e, o=cp[:, half * 512:(half + 1) * 512],
               a=vg[gp][:, kc, :], b=pbh,
               st0=(t < 2), sp=(t >= 62):
               e.matmul(o, a, b, start=st0, stop=sp),
               reads=["vg%d" % gp, pk], writes=["psC%d" % hp])
            lrow = lps[0:1, :] if half == 0 else lps[32:33, :]
            em("tensor", lambda e, o=lrow, b=pbh,
               st0=(t < 2), sp=(t >= 62):
               e.matmul(o, onec[:], b, start=st0, stop=sp,
                        skip_group_check=True),
               reads=[pk, "onec"], writes=["lps"])

        def make_finalizer(h):
            hp2 = h % 2

            def fin_head_a():
                em("vector", lambda e: e.tensor_copy(l_sb[:, 0:512],
                                                     lps[0:1, :]),
                   reads=["lps"], writes=["l_sb"])
                em("vector", lambda e: e.tensor_copy(l_sb[:, 512:1024],
                                                     lps[32:33, :]),
                   reads=["lps"], writes=["l_sb"])
                em("scalar", lambda e: e.activation(l_sb, l_sb, AF.Ln),
                   reads=["l_sb"], writes=["l_sb"])
                em("scalar", lambda e: e.activation(l_sb, l_sb, AF.Exp,
                                                    scale=-1.0),
                   reads=["l_sb"], writes=["l_sb"])

            def fin_head_b():
                for half in range(2):
                    em("tensor", lambda e,
                       b=il_sb[:, half * 512:(half + 1) * 512]:
                       e.matmul(invlps, onerf[:], b, start=True, stop=True),
                       reads=["l_sb", "onerf"], writes=["invl"])
                    em("vector", lambda e,
                       o=ilbc[:, half * 512:(half + 1) * 512]:
                       e.tensor_copy(o, invlps), reads=["invl"],
                       writes=["ilbc"])
                em("vector", lambda e, o=ctxv[:, h, :], a=ctxps[hp2]:
                   e.tensor_tensor(o, a[:, :], ilbc, ALU.mult),
                   reads=["psC%d" % hp2, "ilbc"],
                   writes=["w%d_0" % h, "w%d_1" % h])
            return fin_head_a, fin_head_b

        pc_jobs = []
        tbv = tbuf[:].rearrange("p (hf dc f) -> p hf dc f", hf=2, dc=8)
        for wsrc, wdst in ((wo_sl, wo_bf), (bw_sl, bw_bf)):
            for j in range(16):
                for q_ in range(4):
                    s_ = wsrc[j, :, q_ * 8:(q_ + 1) * 8, :]
                    d_ = wdst[j][:, q_ * 2048:(q_ + 1) * 2048]
                    t_ = tbv[:, q_ % 2]
                    tk_ = "tbh%d" % (q_ % 2)
                    pc_jobs.append((s_, d_, t_, tk_))
        p.recarve(["tb0", "tb1", "tb2", "tb3"], ["tbh0", "tbh1"])
        pc_i = [0]

        def pump_precast(njobs):
            for _ in range(njobs):
                if pc_i[0] >= len(pc_jobs):
                    return
                s_, d_, t_, tk_ = pc_jobs[pc_i[0]]
                pc_i[0] += 1
                em("gpsimd", lambda e, a=t_, b=s_: e.dma_start(a, b),
                   writes=[tk_], sem="g_pc")
                em("gpsimd", lambda e, a=d_, b=t_: e.dma_start(a, b),
                   reads=[tk_], writes=["wo_bf"], sem="g_ps")

        load_qh(0)
        pend_b = None
        for g in range(NKV):
            gp = g % 2
            if g + 1 < NKV and g >= 1:
                load_kv(g + 1)
            for q4 in range(4):
                h = 4 * g + q4
                hp = h % 2
                if h + 1 < NQ:
                    load_qh(h + 1)
                pump_precast(4)
                kkey, qkey = "kg%d" % gp, "qh%d" % hp
                for t in range(64):
                    kc, half = t // 2, t % 2
                    em("tensor", lambda e, o=scv[t % 2],
                       a=kg[gp][:, kc * 128:(kc + 1) * 128],
                       b=qh[hp][:, half * 512:(half + 1) * 512]:
                       e.matmul(o, a, b, start=True, stop=True),
                       reads=[kkey, qkey], writes=["sc%d" % (t % 2)])
                    em("scalar", lambda e, o=pb[t % 3], i_=scv[t % 2]:
                       e.activation(o, i_, AF.Exp, scale=QSCALE),
                       reads=["sc%d" % (t % 2)], writes=["pb%d" % (t % 3)])
                    if t == 6 and pend_b is not None:
                        pend_b()
                        pend_b = None
                    if t >= 2:
                        emit_pv_l(gp, hp, t - 2)
                emit_pv_l(gp, hp, 62)
                emit_pv_l(gp, hp, 63)
                fa, fb = make_finalizer(h)
                fa()
                pend_b = fb
        pend_b()
        pump_precast(len(pc_jobs))
        if dbg_scratch:
            em("sync", lambda e: e.dma_start(
                ctx_dbg.rearrange("(h p) t -> p h t", p=128), ctxv),
               reads=["w%d_%d" % (i_, j_) for i_ in range(32) for j_ in range(2)],
               writes=["ctx_dbg"], sem="s_c")
        load_w(0, wo_bf)

        if upto < 3:
            fin()
            p.lower(block, semobjs)
            sfree(*list(scoped.keys()))
            return nc

        p.recarve(["kg0", "kg1", "vg0", "vg1", "qh0", "qh1",
                   "pb0", "pb1", "pb2", "ilbc", "l_sb"],
                  ["yb0", "yb1", "yb2", "yb3"])
        p.recarve(["tbh0", "tbh1"], ["tb0", "tb1", "tb2", "tb3"])
        p.recarve(["sc0", "sc1"], ["psA0", "psA1"])
        p.recarve(["invl", "lps"], ["psB0", "psB1"])
        sfree("kg0", "kg1", "vg0", "vg1", "qh0", "qh1",
              "pb0", "pb1", "pb2", "ilbc", "l_sb", "psC0", "psC1")
        ybig = salloc("ybig", [128, 16384], F32)
        acat = salloc("acat", [128, 4096], BF16)
        bcat_s = salloc("bcat_s", [128, 4096], BF16)
        bbb = salloc("bbb", [1, 4096], BF16)
        em("gpsimd", lambda e: e.dma_start(
            acat[:].rearrange("p (dc r) -> p dc r", dc=32), acat_sl),
           writes=["acat"], sem="g_c")
        em("gpsimd", lambda e: e.dma_start(bcat_s[:], bcat),
           writes=["bcat_s"], sem="g_c")
        em("gpsimd", lambda e: e.dma_start(bbb[:], bb), writes=["bbb"],
           sem="g_c")
        psDt = palloc("psDt", [64, 1024], BF16)
        psE = palloc("psE", [128, 512])
        ybv = ybig[:].rearrange("p (tb f) -> p tb f", tb=4)
        yblk = [ybv[:, i, :] for i in range(4)]
        wres = wview
        rwv = rwb[:].rearrange("p (dc r) -> p dc r", dc=32)
        acv = acat[:].rearrange("p (dc r) -> p dc r", dc=32)

        def wkey(i, tc):
            return "w%d_%d" % (i, tc)

        def emit_bn(tb, i):
            yb = yblk[tb]
            stv = stats[:, 48 * tb:48 * (tb + 1)]
            em("vector", lambda e, o=stv[:, 6 * i:6 * i + 6],
               a=yb[:, 512 * i:512 * (i + 1)]:
               e.bn_stats(o, a), reads=["yb%d" % tb],
               writes=["st%d_%d" % (tb, i)])

        def ln_pass(tb, stats_done=False):
            ybk = "yb%d" % tb
            yb = yblk[tb]
            stv = stats[:, 48 * tb:48 * (tb + 1)]
            mvv = mv[:, 2 * tb:2 * (tb + 1)]
            s1v, s2v, s3v = (s1[:, tb:tb + 1], s2[:, tb:tb + 1],
                             s3[:, tb:tb + 1])
            if not stats_done:
                for i in range(8):
                    emit_bn(tb, i)
            em("vector", lambda e, o=mvv, a=stv: e.bn_aggr(o, a),
               reads=["st%d_%d" % (tb, i2) for i2 in range(8)],
               writes=["mv%d" % tb])
            em("scalar", lambda e, o=s1v, a=mvv[:, 1:2]:
               e.activation(o, a, AF.Ln, bias=epsb[:]),
               reads=["mv%d" % tb, "epsb"], writes=["s1%d" % tb])
            em("scalar", lambda e, o=s2v, a=s1v:
               e.activation(o, a, AF.Exp, scale=-0.5),
               reads=["s1%d" % tb], writes=["s2%d" % tb])
            em("vector", lambda e, o=s3v, a=mvv[:, 0:1], b=s2v:
               e.tensor_tensor(o, a, b, ALU.mult),
               reads=["mv%d" % tb, "s2%d" % tb], writes=["s3%d" % tb])
            em("vector", lambda e, o=yb, a=s2v, b=s3v: e.tensor_scalar(
                o, o, a, b, ALU.mult, op1=ALU.subtract),
               reads=[ybk, "s2%d" % tb, "s3%d" % tb], writes=[ybk])

        def mm_pass(tc, wbf, lora):
            psF = palloc("psF", [128, 1024])
            rot6 = psum_rot + [("psF0", psF[:, 0:512]),
                               ("psF1", psF[:, 512:1024])]
            ri = [0]
            for obc in range(16):
                if obc + 1 < 16:
                    load_w(obc + 1, wbf)
                par = obc % 2
                for tb in range(4):
                    pk, pa0 = rot6[ri[0] % 6]; ri[0] += 1
                    pa = pa0[:, 0:256]
                    for dc in range(DC):
                        em("tensor", lambda e, o=pa,
                           a=wres[:, dc, tc * 512 + tb * 128:
                                  tc * 512 + (tb + 1) * 128],
                           b=wsl[par][:, dc, :], st0=(dc == 0),
                           sp=(dc == 31 and not lora):
                           e.matmul(o, a, b, start=st0, stop=sp),
                           reads=[wkey(dc, tc), "ws%d" % par], writes=[pk])
                    if lora:
                        em("tensor", lambda e, o=pa,
                           b=bbb[:, obc * 256:(obc + 1) * 256]:
                           e.matmul(o, oner[:], b, start=False, stop=False),
                           reads=["oner", "bbb"], writes=[pk])
                        em("tensor", lambda e, o=pa,
                           a=hwb[:, tb * 128:(tb + 1) * 128],
                           b=bcat_s[:, obc * 256:(obc + 1) * 256]:
                           e.matmul(o, a, b, start=False, stop=True),
                           reads=["hwb", "bcat_s"], writes=[pk])
                    em("vector", lambda e,
                       o=yblk[tb][:, obc * 256:(obc + 1) * 256],
                       i_=pa: e.tensor_tensor(o, o, i_, ALU.add),
                       reads=[pk, "yb%d" % tb], writes=["yb%d" % tb])
                    if obc % 2 == 1:
                        emit_bn(tb, obc // 2)
            sfree("psF")

        for tc in range(2):
            for tb in range(4):
                trow = tc * 512 + tb * 128
                em("sync", lambda e, d=yblk[tb], s=xo[trow:trow + 128, :]:
                   e.dma_start(d, s), writes=["yb%d" % tb], sem="s_res")
            mm_pass(tc, wo_bf, lora=False)
            load_w(0, bw_bf)

            psCt = palloc("psCt", [128, 2048], BF16)
            rps = []
            for tb in range(4):
                ln_pass(tb, stats_done=True)
                em("vector", lambda e, i_=yblk[tb]: e.tensor_copy(tbuf[:], i_),
                   reads=["yb%d" % tb], writes=["tb0", "tb1", "tb2", "tb3"])
                for dg in range(8):
                    for j in range(4):
                        dc = dg * 4 + j
                        em("tensor", lambda e,
                           o=psCt[:, j * 128:(j + 1) * 128],
                           i_=tbuf[:, dc * 128:(dc + 1) * 128]:
                           e.transpose(o, i_, idb[:]),
                           reads=["tb0", "tb1", "tb2", "tb3", "idb"],
                           writes=["psC"])
                    em("vector", lambda e,
                       o=wres[:, dg * 4:dg * 4 + 4,
                              tc * 512 + tb * 128:tc * 512 + (tb + 1) * 128],
                       i_=psCt[:, 0:512].rearrange("p (j t) -> p j t", j=4):
                       e.tensor_copy(o, i_), reads=["psC"],
                       writes=[wkey(dg * 4 + j2, tc) for j2 in range(4)])
                pk, pa = rot(ei[0]); ei[0] += 1
                rps.append((pk, pa))
                em("tensor", lambda e, o=pa[:, 0:16]:
                   e.matmul(o, oner[:], rbb[:], start=True, stop=False),
                   reads=["oner", "rbb"], writes=[pk])
                for dc in range(DC):
                    em("tensor", lambda e, o=pa[:, 0:16],
                       a=wres[:, dc, tc * 512 + tb * 128:
                              tc * 512 + (tb + 1) * 128],
                       b=rwv[:, dc, :], sp=(dc == 31):
                       e.matmul(o, a, b, start=False, stop=sp),
                       reads=[wkey(dc, tc), "rwb"], writes=[pk])
            sfree("psCt")

            for tb in range(4):
                pk, pa = rps[tb]
                lgv = lg[:, 0:15]
                prv = pr[:, 0:15]
                s1v, s2v, s3v = (s1[:, tb:tb + 1], s2[:, tb:tb + 1],
                                 s3[:, tb:tb + 1])
                em("vector", lambda e, a=pa[:, 0:16], o=lg[:]:
                   e.tensor_copy(o, a), reads=[pk], writes=["lg"])
                em("vector", lambda e, o=s1v: e.tensor_reduce(
                    o, lgv, AX.X, ALU.max, negate=True),
                   reads=["lg"], writes=["s1%d" % tb])
                em("scalar", lambda e, o=prv, b=s1v, ac=s2v: e.activation(
                    o, lgv, AF.Exp, bias=b, accum_out=ac),
                   reads=["lg", "s1%d" % tb], writes=["pr", "s2%d" % tb])
                em("vector", lambda e, o=s3v, a=s2v: e.reciprocal(o, a),
                   reads=["s2%d" % tb], writes=["s3%d" % tb])
                em("vector", lambda e, o=prv, a=s3v: e.tensor_scalar(
                    o, o, a, None, ALU.mult),
                   reads=["pr", "s3%d" % tb], writes=["pr"])
                for n in range(15):
                    em("vector", lambda e, nn_=n: e.tensor_scalar(
                        j15[:, 0:15], pr[:, 0:15], pr[:, nn_:nn_ + 1], None,
                        ALU.is_gt, op1=ALU.add,
                        accum_out=cnts[:, nn_:nn_ + 1]),
                       reads=["pr"], writes=["j15", "c%d" % n])
                em("vector", lambda e: e.tensor_scalar(
                    mskb[:, 0:15], cnts[:, 0:15], 8.0, None, ALU.is_lt),
                   reads=["c%d" % n2 for n2 in range(15)], writes=["mskb"])
                em("vector", lambda e: e.tensor_tensor(
                    pmb[:, 0:15], pr[:, 0:15], mskb[:, 0:15], ALU.mult),
                   reads=["pr", "mskb"], writes=["pmb"])
                em("tensor", lambda e: e.transpose(
                    psDt[0:15, 0:128], pmb[:, 0:15], idb[:]),
                   reads=["pmb", "idb"], writes=["psD"])
                em("vector", lambda e, o=pmT[:, tb * 128:(tb + 1) * 128]:
                   e.tensor_copy(o, psDt[0:15, 0:128]),
                   reads=["psD"], writes=["pmT"])
            em("tensor", lambda e: e.matmul(psE[0:8, 0:512], mmapb[:], pmT[:],
                                            start=True, stop=True),
               reads=["mmapb", "pmT"], writes=["psE"])
            em("vector", lambda e: e.tensor_copy(wtb[:], psE[0:8, 0:512]),
               reads=["psE"], writes=["wtb"])
            em("tensor", lambda e: e.matmul(psE[:], e16b[:], wtb[:],
                                            start=True, stop=True),
               reads=["e16b", "wtb"], writes=["psE"])
            em("vector", lambda e: e.tensor_copy(wbcb[:], psE[:]),
               reads=["psE"], writes=["wbcb"])
            psCh = palloc("psCh", [128, 512])
            for dc in range(DC):
                em("tensor", lambda e, a=acv[:, dc, :],
                   b=wres[:, dc, tc * 512:(tc + 1) * 512],
                   st0=(dc == 0), sp=(dc == 31):
                   e.matmul(psCh[:], a, b, start=st0, stop=sp),
                   reads=["acat", wkey(dc, tc)], writes=["psC"])
            em("vector", lambda e: e.tensor_tensor(hwb[:], psCh[:],
                                                   wbcb[:], ALU.mult),
               reads=["psC", "wbcb"], writes=["hwb"])
            sfree("psCh")
            mm_pass(tc, bw_bf, lora=True)
            if tc == 0:
                load_w(0, wo_bf)
            for tb in range(4):
                ln_pass(tb, stats_done=True)
                trow = tc * 512 + tb * 128
                em("sync", lambda e, d=y[trow:trow + 128, :], s=yblk[tb]:
                   e.dma_start(d, s), reads=["yb%d" % tb], writes=["y_hbm"],
                   sem="s_y")

        fin()
        p.lower(block, semobjs)
        sfree(*list(scoped.keys()))
    return nc


def _host_inputs(inp):
    x = np.ascontiguousarray(np.asarray(inp["L0_attn_hidden_states"],
                                        dtype=np.float32))
    c = np.ascontiguousarray

    def slot_img(w, nf):
        O = w.shape[0]
        return c(w.reshape(O // nf, nf, 32, 128).transpose(0, 3, 2, 1))

    wq_sl = slot_img(np.asarray(inp["L0_attn_Wq"], np.float32), 128)
    wk_sl = slot_img(np.asarray(inp["L0_attn_Wk"], np.float32), 128)
    wv_sl = slot_img(np.asarray(inp["L0_attn_Wv"], np.float32), 256)
    wo_sl = slot_img(np.asarray(inp["L0_attn_Wo"], np.float32), 256)
    bw_sl = slot_img(np.asarray(inp["L0_ffn_base_W"], np.float32), 256)
    bv = np.asarray(inp["L0_attn_bv"], np.float32).reshape(1, 1024)
    bb = np.asarray(inp["L0_ffn_base_b"], np.float32).reshape(1, D)
    rw = np.asarray(inp["L0_ffn_router_W"], np.float32)
    rwp = np.zeros((16, D), np.float32)
    rwp[0:15] = rw
    rw_sl = slot_img(rwp, 16)[0]
    rb = np.zeros((1, 16), np.float32)
    rb[0, 0:15] = np.asarray(inp["L0_ffn_router_b"], np.float32)
    A = np.asarray(inp["L0_ffn_A"], np.float32)
    acat_sl = slot_img(A.reshape(128, D), 128)[0]
    Bm = np.asarray(inp["L0_ffn_B"], np.float32)
    bcat = c(Bm.transpose(0, 2, 1).reshape(128, D))
    emap = np.asarray(inp["L0_ffn_expert_map"], np.int32).reshape(15, 1)
    iota8 = np.broadcast_to(np.arange(8, dtype=np.float32), (15, 8)).copy()
    e16 = np.zeros((8, 128), np.float32)
    for e_ in range(8):
        e16[e_, e_ * 16:(e_ + 1) * 16] = LORA_SCALE
    idbf = np.eye(128).astype(ml_dtypes.bfloat16)

    shared = dict(wq_sl=wq_sl, wk_sl=wk_sl, wv_sl=wv_sl, wo_sl=wo_sl,
                  bw_sl=bw_sl, bv=bv, bb=bb, rw_sl=rw_sl, rb=rb,
                  acat_sl=acat_sl, bcat=bcat, emap=emap,
                  iota8=iota8, e16=e16, idbf=idbf)
    xT = [c(x[b_].T) for b_ in range(B)]
    in_maps = []
    for core in range(NCORE):
        b_ = core // 4
        q0 = (core % 4) * TPC
        m = dict(shared)
        m["xoT"] = c(xT[b_][:, q0:q0 + TPC])
        m["xo"] = c(x[b_, q0:q0 + TPC, :])
        in_maps.append(m)
    return in_maps


def kernel(**inputs):
    in_maps = _host_inputs(inputs)
    nc = build_nc()
    res = run_bass_kernel_spmd(nc, in_maps, core_ids=list(range(NCORE)))
    out = np.empty((B, S, D), np.float32)
    for core in range(NCORE):
        b_ = core // 4
        q0 = (core % 4) * TPC
        out[b_, q0:q0 + TPC, :] = np.asarray(res.results[core]["y"],
                                             dtype=np.float32)
    return out
